# version 06
# baseline (speedup 1.0000x reference)
import contextlib
import math

import numpy as np
import ml_dtypes

import concourse.bass as bass
import concourse.mybir as mybir
from concourse.bass_utils import run_bass_kernel_spmd

F32 = mybir.dt.float32
BF16 = mybir.dt.bfloat16
I32 = mybir.dt.int32
AF = mybir.ActivationFunctionType
ALU = mybir.AluOpType

B, T, D = 4, 2048, 2048
NCORE = 8
TOK = 1024
NQS = 8
H = 16
SCALE = 1.0 / math.sqrt(128.0)
NW0 = 2048 * 8192

ENGINES = ("sync", "scalar", "vector", "tensor", "gpsimd")
PAIRS = [[0, 1], [2, 3], [4, 5], [6, 7]]
ALL8 = [[0, 1, 2, 3, 4, 5, 6, 7]]


class Sem:
    def __init__(self, handle, name):
        self.h, self.name, self.n = handle, name, 0


class Sched:

    def __init__(self):
        self.q = {e: [] for e in ENGINES}
        self.last_wait = {e: {} for e in ENGINES}
        self.auto = {}

    def _raw_emit(self, eng, fn):
        self.q[eng].append(fn)

    def emit(self, eng, fn, auto=True):
        asem = self.auto.get(eng)
        if asem is not None and auto:
            self.wait(eng, asem, asem.n)
            asem.n += 1
            self.q[eng].append(
                lambda e, fn=fn, s=asem.h: fn(e).then_inc(s, 1))
        else:
            self.q[eng].append(fn)

    def inc(self, eng, sem, k=1):
        asem = self.auto.get(eng)
        if asem is not None and sem is asem:
            return sem.n
        sem.n += k
        fn = self.q[eng].pop()
        self.q[eng].append(lambda e, fn=fn, s=sem.h, k=k: fn(e).then_inc(s, k))
        return sem.n

    def wait(self, eng, sem, val=None):
        v = sem.n if val is None else val
        if v <= 0:
            return
        lw = self.last_wait[eng].get(sem.name, 0)
        if v <= lw:
            return
        self.last_wait[eng][sem.name] = v
        self.q[eng].append(lambda e, v=v, s=sem.h: e.wait_ge(s, v))

    def run(self, eng_name, eng):
        for fn in self.q[eng_name]:
            fn(eng)


def build_program(debug=False, stop_after=99):
    nc = bass.Bass("TRN2", target_bir_lowering=False)

    def din(name, shape, dtype=F32):
        return nc.dram_tensor(name, shape, dtype, kind="ExternalInput").ap()

    x_in = din("x", [TOK, D])
    qidx_in = din("qidx", [128, NQS])
    ident_in = din("ident", [128, 128], BF16)
    identf_in = din("identf", [128, 128], F32)
    wT = {(l, m): din(f"L{l}_{m}T", [D, D])
          for l in (0, 1) for m in ("wq", "wk", "wv", "wo")}
    bias_in = {m: din(f"L0_b{m}", [128, 16]) for m in ("q", "k", "v", "o")}
    tern_sl = {m: din(f"t{m}_sl", [D, 1024] if m != "d" else [1024, D])
               for m in ("u", "g", "d")}
    pk_sl = {m: din(f"pk{m}_sl", [1024, 2048], I32) for m in ("g", "u", "d")}
    sx_sl = {m: din(f"sx{m}_sl", [1024, 2048]) for m in ("g", "u", "d")}
    out_d = nc.dram_tensor("out", [TOK, D], F32, kind="ExternalOutput").ap()

    dbg_d = {}
    if debug:
        for nm, shape, dt in (
            ("d_R1", [TOK, D], F32), ("d_R2", [TOK, D], F32),
            ("d_R3", [TOK, D], F32), ("d_scl", [1, 8], F32),
            ("d_ternu", [D, 1024], BF16), ("d_deqg", [2048, 2048], BF16),
            ("d_q0", [D, TOK], BF16), ("d_z0", [D, TOK], BF16),
        ):
            dbg_d[nm] = nc.dram_tensor(nm, shape, dt, kind="ExternalOutput").ap()

    def idram(name, shape, dtype, shared=False):
        return nc.dram_tensor(name, shape, dtype, kind="Internal",
                              addr_space="Local").ap()

    kv_c = {(l, m): idram(f"kv{l}{m}_c", [D, TOK], BF16)
            for l in (0, 1) for m in ("k", "v")}
    kv_ag = {(l, m): idram(f"kv{l}{m}_ag", [2, D, TOK], BF16)
             for l in (0, 1) for m in ("k", "v")}
    tern_c = {m: idram(f"tern{m}_c", [D, 1024] if m != "d" else [1024, D], BF16)
              for m in ("u", "g", "d")}
    tern_ag = {m: idram(f"tern{m}_ag",
                        [8, D, 1024] if m != "d" else [8, 1024, D], BF16,
                        shared=True) for m in ("u", "g", "d")}
    deq_c = {m: idram(f"deq{m}_c", [2048, 2048], BF16) for m in ("g", "u", "d")}
    deq_ag = {m: idram(f"deq{m}_ag", [8, 2048, 2048], BF16, shared=True)
              for m in ("g", "u", "d")}
    ar_i = idram("ar_i", [1, 64], F32)
    ar_o = idram("ar_o", [1, 64], F32, shared=True)

    stack = contextlib.ExitStack()
    with stack:
        def sbuf(name, shape, dtype):
            return stack.enter_context(nc.sbuf_tensor(name, shape, dtype))

        def psum(name, shape, dtype=F32):
            return stack.enter_context(nc.psum_tensor(name, shape, dtype))

        R = sbuf("R", [128, 8 * 2048], F32)
        XZ = sbuf("XZ", [128, 16 * 1024], BF16)
        QA = sbuf("QA", [128, 16 * 1024], BF16)
        BFA = sbuf("BFA", [128, 20480], BF16)
        DQOB = sbuf("DQOB", [128, 1024], BF16)
        SILB = sbuf("SILB", [128, 1024], BF16)
        SCR = sbuf("SCR", [128, 2048], F32)
        PRF = sbuf("PRF", [128, 2048], F32)
        DQV = sbuf("DQV", [128, 2048], I32)
        KIOT = DQV
        DQNI = sbuf("DQNI", [128, 512], I32)
        DQUI = sbuf("DQUI", [128, 512], I32)
        PTB = sbuf("PTB", [128, 512], BF16)
        IDE = sbuf("IDE", [128, 128], BF16)
        IDEF = sbuf("IDEF", [128, 128], F32)
        QIDX = sbuf("QIDX", [128, NQS], F32)
        BIA = sbuf("BIA", [128, 64], F32)
        BQS = sbuf("BQS", [128, 16], F32)
        ZBI = sbuf("ZBI", [128, 1], F32)
        CONST = sbuf("CONST", [128, 8], F32)
        MV = sbuf("MV", [128, 2], F32)
        BNS = sbuf("BNS", [128, 24], F32)
        RSTD = sbuf("RSTD", [128, 1], F32)
        NMX = sbuf("NMX", [128, 1], F32)
        MXV = sbuf("MXV", [128, 1], F32)
        LS0 = sbuf("LS0", [128, 1], F32)
        LS1 = sbuf("LS1", [128, 1], F32)
        RLS = sbuf("RLS", [128, 1], F32)
        ACC = sbuf("ACC", [128, 4], F32)
        SCL = sbuf("SCL", [128, 64], F32)
        THR = sbuf("THR", [128, 8], F32)
        ONE1 = sbuf("ONE1", [128, 1], F32)
        CM = sbuf("CM", [128, 8], F32)
        LSA = sbuf("LSA", [128, 1], F32)
        LSB = sbuf("LSB", [128, 1], F32)
        ONER = sbuf("ONER", [1, 128], F32)

        PS0 = psum("PS0", [128, 1024])
        PS1 = psum("PS1", [128, 1024])
        PS2 = psum("PS2", [128, 1024])
        PS3f = psum("PS3f", [128, 512])
        PS3b = psum("PS3b", [128, 512], BF16)

        semnames = ["svec", "ssca", "sten", "scc", "sgp",
                    "sph0", "swa", "swb", "swcst", "ststg", "sdqld", "sdqo",
                    "skv0", "skv1", "sout", "sgd"] + [f"swx{i}" for i in range(8)]
        sems = {nm: Sem(stack.enter_context(nc.semaphore(nm)), nm)
                for nm in semnames}
        SVEC, SSCA, STEN = sems["svec"], sems["ssca"], sems["sten"]
        SCC, SGP = sems["scc"], sems["sgp"]
        SPH0, SWA, SWB = sems["sph0"], sems["swa"], sems["swb"]
        SWCST, STSTG = sems["swcst"], sems["ststg"]
        SDQLD, SDQO = sems["sdqld"], sems["sdqo"]
        SKV = [sems["skv0"], sems["skv1"]]
        SOUT, SGD = sems["sout"], sems["sgd"]
        SWX = [sems[f"swx{i}"] for i in range(8)]

        block = stack.enter_context(nc.Block())
        S = Sched()
        S.auto = {"vector": SVEC, "scalar": SSCA, "gpsimd": SGP}

        def dma(q, sem, out, in_):
            S._raw_emit(q, lambda e, o=out, i=in_: e.dma_start(out=o, in_=i))
            sem.n += 16
            fn = S.q[q].pop()
            S.q[q].append(lambda e, fn=fn, s=sem.h: fn(e).then_inc(s, 16))
            return sem.n

        dma("sync", SPH0, QIDX[:, :], qidx_in[:, :])
        dma("sync", SPH0, IDE[:, :], ident_in[:, :])
        dma("sync", SPH0, IDEF[:, :], identf_in[:, :])
        for j, m in enumerate(("q", "k", "v", "o")):
            dma("sync", SPH0, BIA[:, 16 * j:16 * (j + 1)], bias_in[m][:, :])
        x_r = x_in.rearrange("(s p) d -> s p d", p=128)
        for s in range(NQS):
            dma("sync", SPH0, R[:, 2048 * s:2048 * (s + 1)], x_r[s, :, :])
        v_ld0 = SPH0.n

        S.emit("gpsimd", lambda e: e.iota(out=KIOT[:, :], pattern=[[1, 2048]],
                                          base=0, channel_multiplier=0))
        S.inc("gpsimd", SGP)
        for _ms in [lambda e: e.memset(CONST[:, 0:1], 1e-5),
                    lambda e: e.memset(CONST[:, 1:2], 7.5),
                    lambda e: e.memset(CONST[:, 2:3], -4.0),
                    lambda e: e.memset(CONST[:, 3:4], -12.0),
                    lambda e: e.memset(ONE1[:, :], 1.0),
                    lambda e: e.memset(ZBI[:, :], 0.0),
                    lambda e: e.memset(ACC[:, :], 0.0)]:
            S.emit("gpsimd", _ms)
            S.inc("gpsimd", SGP)
        v_gp0 = SGP.n

        S.wait("vector", SPH0, v_ld0)
        S.emit("vector", lambda e: e.tensor_scalar(
            out=BQS[:, :], in0=BIA[:, 0:16], scalar1=SCALE, scalar2=None,
            op0=ALU.mult))
        S.inc("vector", SVEC)

        def xt_rebuild():
            for s in range(NQS):
                for dc in range(16):
                    idx = s * 16 + dc
                    slot = idx % 2
                    if idx >= 2 or idx == 0:
                        S.wait("tensor", SVEC)
                    S.emit("tensor", lambda e, s=s, dc=dc, slot=slot: e.transpose(
                        PS3f[:, 128 * slot:128 * (slot + 1)],
                        R[:, 2048 * s + 128 * dc:2048 * s + 128 * (dc + 1)],
                        IDEF[:, :]))
                    tv = S.inc("tensor", STEN)
                    S.wait("vector", STEN, tv)
                    S.emit("vector", lambda e, s=s, dc=dc, slot=slot: e.tensor_copy(
                        out=XZ[:, 1024 * dc + 128 * s:1024 * dc + 128 * (s + 1)],
                        in_=PS3f[:, 128 * slot:128 * (slot + 1)]))
                    S.inc("vector", SVEC)
            return SVEC.n

        S.wait("tensor", SPH0, v_ld0)
        S.wait("tensor", SGP, v_gp0)
        S.emit("tensor", lambda e: e.transpose(
            PS3f[0:1, 0:128], ONE1[:, 0:1], IDEF[:, :]))
        tv0 = S.inc("tensor", STEN)
        S.wait("vector", STEN, tv0)
        S.emit("vector", lambda e: e.tensor_copy(out=ONER[:, :],
                                                 in_=PS3f[0:1, 0:128]))
        v_oner = S.inc("vector", SVEC)
        v_xt = xt_rebuild()

        TMAT = ("u", "g", "d") if stop_after >= -2 else ()
        for mi, m in enumerate(TMAT):
            src = tern_sl[m]
            nrow, ncol = src.tensor.shape
            src_r = src.rearrange("(c p) e -> c p e", p=128)
            for t in range(nrow // 128):
                buf = SCR if t % 2 == 0 else PRF
                sbf = SWA if t % 2 == 0 else SWB
                S.wait("sync", SVEC)
                v = dma("sync", sbf, buf[:, 0:ncol], src_r[t, :, :])
                S.wait("vector", sbf, v)
                S.emit("vector", lambda e, buf=buf, ncol=ncol: e.tensor_reduce(
                    out=MXV[:, :], in_=buf[:, 0:ncol], axis=mybir.AxisListType.X,
                    op=ALU.add, apply_absolute_value=True))
                S.inc("vector", SVEC)
                S.wait("vector", SVEC)
                S.wait("vector", SGP, v_gp0)
                S.emit("vector", lambda e, mi=mi: e.tensor_tensor(
                    out=ACC[:, mi:mi + 1], in0=ACC[:, mi:mi + 1], in1=MXV[:, :],
                    op=ALU.add))
                S.inc("vector", SVEC)
        v_abs = SVEC.n
        S.wait("tensor", SVEC, v_abs)
        tern_done = {}
        tern_ag_done = {"u": 0, "g": 0, "d": 0}
        deq_done = {"g": 0, "u": 0, "d": 0}
        v_thr = SVEC.n
        if stop_after >= -2:
            S.emit("tensor", lambda e: e.matmul(
                PS3f[0:1, 0:4], ONE1[:, :], ACC[:, 0:4], start=True, stop=True))
            vt = S.inc("tensor", STEN)
            S.wait("vector", STEN, vt)
            S.emit("vector", lambda e: e.memset(SCL[0:1, 0:64], 0.0))
            S.inc("vector", SVEC)
            S.wait("vector", SVEC)
            S.emit("vector", lambda e: e.tensor_copy(out=SCL[0:1, 0:4],
                                                     in_=PS3f[0:1, 0:4]))
            v_sums = S.inc("vector", SVEC)
            S.wait("sync", SVEC, v_sums)
            v_ari = dma("sync", SGD, ar_i[:, 0:64], SCL[0:1, 0:64])
            S.wait("gpsimd", SGD, v_ari)
            S.emit("gpsimd", lambda e: e.collective_compute(
                "AllReduce", ALU.add, ins=[ar_i[:, :]], outs=[ar_o[:, :]],
                replica_groups=ALL8), auto=False)
            v_ar = S.inc("gpsimd", SCC)
            S.wait("sync", SCC, v_ar)
            v_aro = dma("sync", SGD, SCL[0:1, 0:64], ar_o[:, :])
            S.wait("tensor", SGD, v_aro)
            S.wait("tensor", SVEC, v_oner)
            S.emit("tensor", lambda e: e.matmul(
                PS3f[:, 128:136], ONER[:, :], SCL[0:1, 0:8],
                start=True, stop=True))
            v_bcm = S.inc("tensor", STEN)
            S.wait("vector", STEN, v_bcm)
            S.emit("vector", lambda e: e.tensor_copy(out=SCL[:, 0:8],
                                                     in_=PS3f[:, 128:136]))
            S.inc("vector", SVEC)
            S.emit("vector", lambda e: e.tensor_scalar(
                out=THR[:, :], in0=SCL[:, 0:8], scalar1=0.5 / NW0,
                scalar2=0.5e-8, op0=ALU.mult, op1=ALU.add))
            S.emit("vector", lambda e: e.tensor_scalar(
                out=SCL[:, 0:8], in0=SCL[:, 0:8], scalar1=1.0 / NW0,
                scalar2=1e-8, op0=ALU.mult, op1=ALU.add))
            v_thr = S.inc("vector", SVEC)
            if debug:
                S.wait("sync", SVEC, v_thr)
                dma("sync", SOUT, dbg_d["d_scl"][:, :], SCL[0:1, 0:8])

            TSTG2B = [BFA[:, 4096:5120], BFA[:, 9216:10240]]
            SGNB2 = [BFA[:, 5120:6144], BFA[:, 6144:7168]]
            tern_done = {}
            tern_units = []

            def _tern_unit(mi, m, t, half, is_last):
                src_t = tern_sl[m]
                nrow, ncol = src_t.tensor.shape
                src_r = src_t.rearrange("(c p) e -> c p e", p=128)
                dst_r = tern_c[m].rearrange("(c p) e -> c p e", p=128)
                c0 = 1024 * half
                stv = [0]

                def load(pu):
                    stage = PRF[:, 1024 * pu:1024 * (pu + 1)]
                    S.wait("scalar", SVEC)
                    stv[0] = dma("scalar", SWB if pu == 0 else SWA,
                                 stage, src_r[t, :, c0:c0 + 1024])

                def compute(pu):
                    stage = PRF[:, 1024 * pu:1024 * (pu + 1)]
                    sgnb = SGNB2[pu]
                    S.wait("scalar", SWB if pu == 0 else SWA, stv[0])
                    S.wait("scalar", SVEC, v_thr)
                    S.emit("scalar", lambda e, sgnb=sgnb, stage=stage: e.activation(
                        out=sgnb[:, :], in_=stage, func=AF.Sign))
                    S.emit("scalar", lambda e, stage=stage: e.activation(
                        out=stage, in_=stage, func=AF.Abs))
                    vs = SSCA.n
                    S.wait("vector", SSCA, vs)
                    S.emit("vector", lambda e, mi=mi, stage=stage: e.tensor_scalar(
                        out=stage, in0=stage,
                        scalar1=THR[:, mi:mi + 1], scalar2=None, op0=ALU.is_ge))
                    tsem = STSTG if pu == 0 else SWCST
                    S.wait("vector", tsem)
                    S.emit("vector", lambda e, mi=mi, sgnb=sgnb, stage=stage,
                           pu=pu: e.scalar_tensor_tensor(
                        out=TSTG2B[pu][:, :], in0=sgnb[:, :],
                        scalar=SCL[:, mi:mi + 1], in1=stage,
                        op0=ALU.mult, op1=ALU.mult))
                    vv = SVEC.n
                    S.wait("scalar", SVEC, vv)
                    dma("scalar", tsem, dst_r[t, :, c0:c0 + 1024],
                        TSTG2B[pu][:, :])
                    if is_last:
                        tern_done[m] = (STSTG.n, SWCST.n)
                return (load, compute)

            for mi, m in enumerate(TMAT):
                nrow, ncol = tern_sl[m].tensor.shape
                nt, nh = nrow // 128, ncol // 1024
                cnt = 0
                for t in range(nt):
                    for half in range(nh):
                        cnt += 1
                        tern_units.append(
                            _tern_unit(mi, m, t, half, cnt == nt * nh))

            DQO2 = DQOB[:, :]
            DQN2 = SCR[:, 0:512]
            DQA2 = SCR[:, 512:1024]
            DQB2 = SCR[:, 1024:1536]
            DQG2 = SCR[:, 1536:2048]
            deq_done = {}
            deq_units = []

            def _deq_unit(m, jc, ch, is_last):
                pk = pk_sl[m].rearrange("(c p) n -> c p n", p=128)
                sx = sx_sl[m].rearrange("(c p) n -> c p n", p=128)
                dst_r = deq_c[m].rearrange("(c p two) n -> c two p n",
                                           p=128, two=2)
                c0 = 1024 * ch
                ldv = [0]

                def load(pu):
                    dqv = DQV[:, 1024 * pu:1024 * (pu + 1)]
                    dqs = PRF[:, 1024 * pu:1024 * (pu + 1)]
                    sem = SDQLD if pu == 0 else SWA
                    S.wait("scalar", SVEC)
                    dma("scalar", sem, dqv, pk[jc, :, c0:c0 + 1024])
                    ldv[0] = dma("scalar", sem, dqs, sx[jc, :, c0:c0 + 1024])

                def compute(pu):
                    dqv = DQV[:, 1024 * pu:1024 * (pu + 1)]
                    dqs = PRF[:, 1024 * pu:1024 * (pu + 1)]
                    S.wait("vector", SDQLD if pu == 0 else SWA, ldv[0])
                    for sub in range(2):
                        s0 = 512 * sub
                        for hi in range(2):
                            if hi == 0:
                                S.emit("vector", lambda e, s0=s0, dqv=dqv:
                                       e.tensor_scalar(
                                    out=DQNI[:, :], in0=dqv[:, s0:s0 + 512],
                                    scalar1=15, scalar2=None,
                                    op0=ALU.bitwise_and))
                            else:
                                S.emit("vector", lambda e, s0=s0, dqv=dqv:
                                       e.tensor_scalar(
                                    out=DQNI[:, :], in0=dqv[:, s0:s0 + 512],
                                    scalar1=4, scalar2=None,
                                    op0=ALU.logical_shift_right))
                            S.emit("vector", lambda e: e.tensor_scalar(
                                out=DQUI[:, :], in0=DQNI[:, :], scalar1=7,
                                scalar2=None, op0=ALU.bitwise_and))
                            vv = SVEC.n
                            S.wait("scalar", SVEC, vv)
                            S.emit("scalar", lambda e: e.activation(
                                out=DQG2[:, :], in_=DQNI[:, :], func=AF.Sign,
                                bias=CONST[:, 1:2], scale=-1.0))
                            S.emit("scalar", lambda e: e.activation(
                                out=DQN2[:, :], in_=DQUI[:, :],
                                func=AF.Identity))
                            S.emit("scalar", lambda e: e.activation(
                                out=DQA2[:, :], in_=DQUI[:, :], func=AF.Relu,
                                bias=CONST[:, 2:3], scale=1.0))
                            S.emit("scalar", lambda e: e.activation(
                                out=DQB2[:, :], in_=DQUI[:, :], func=AF.Relu,
                                bias=CONST[:, 3:4], scale=2.0))
                            vs = SSCA.n
                            S.wait("vector", SSCA, vs)
                            S.emit("vector", lambda e: e.tensor_tensor(
                                out=DQA2[:, :], in0=DQN2[:, :], in1=DQA2[:, :],
                                op=ALU.add))
                            S.emit("vector", lambda e: e.tensor_tensor(
                                out=DQA2[:, :], in0=DQA2[:, :], in1=DQB2[:, :],
                                op=ALU.add))
                            S.emit("vector", lambda e: e.scalar_tensor_tensor(
                                out=DQA2[:, :], in0=DQA2[:, :], scalar=0.5,
                                in1=DQG2[:, :], op0=ALU.mult, op1=ALU.mult))
                            oslot = (2 * sub + hi) % 2
                            dsem = SDQO if oslot == 0 else SGD
                            S.wait("vector", dsem)
                            S.emit("vector", lambda e, oslot=oslot, dqs=dqs,
                                   s0=s0: e.tensor_tensor(
                                out=DQO2[:, 512 * oslot:512 * (oslot + 1)],
                                in0=DQA2[:, :], in1=dqs[:, s0:s0 + 512],
                                op=ALU.mult))
                            vv2 = SVEC.n
                            S.wait("sync", SVEC, vv2)
                            dma("sync", dsem,
                                dst_r[jc, hi, :, c0 + s0:c0 + s0 + 512],
                                DQO2[:, 512 * oslot:512 * (oslot + 1)])
                    if is_last:
                        deq_done[m] = (SDQO.n, SGD.n)
                return (load, compute)

            for m in ("g", "u", "d"):
                for jc in range(8):
                    for ch in range(2):
                        deq_units.append(_deq_unit(
                            m, jc, ch, jc == 7 and ch == 1))

        class UnitDriver:
            def __init__(self, units):
                self.units = units
                self.li = 0
                self.ci = 0

            def step(self, k=1):
                for _ in range(k):
                    if self.li < len(self.units):
                        self.units[self.li][0](self.li % 2)
                        self.li += 1
                    if self.ci < self.li - 1 or (self.li == len(self.units)
                                                 and self.ci < self.li):
                        self.units[self.ci][1](self.ci % 2)
                        self.ci += 1

            def flush(self):
                while self.ci < len(self.units):
                    self.step()

        tern_drv = UnitDriver(tern_units if stop_after >= -2 else [])
        deq_drv = UnitDriver(deq_units if stop_after >= -2 else [])
        if stop_after < 1:
            tern_drv.flush()
        if stop_after < 4:
            deq_drv.flush()

        def qkv_phase(l, hook=None):
            WBSQ = [BFA[:, 1024 * i:1024 * (i + 1)] for i in range(2)]
            WSTQ = [SCR[:, 1024 * i:1024 * (i + 1)] for i in range(2)]
            KVST = [BFA[:, 7168 + 1024 * i:7168 + 1024 * (i + 1)] for i in range(2)]
            kv_done = {}
            S.wait("vector", SOUT)
            S.wait("scalar", STSTG)
            S.wait("scalar", SWCST)
            S.wait("scalar", SDQO)
            S.wait("scalar", SGD)
            S.wait("tensor", SVEC)
            S.wait("tensor", SSCA)
            wst_used = [SVEC.n] * 2
            wbs_used = [STEN.n] * 2
            for mat in ("wk", "wv", "wq"):
                src = wT[(l, mat)].rearrange("(dc p) o -> p dc o", p=128)
                for og in range(8):
                    for qt in range(4):
                        slot = qt % 2
                        S.wait("sync", SVEC, wst_used[slot])
                        v = dma("sync", SWX[slot], WSTQ[slot].rearrange(
                            "p (dc o) -> p dc o", o=256),
                            src[:, 4 * qt:4 * qt + 4, 256 * og:256 * (og + 1)])
                        S.wait("vector", SWX[slot], v)
                        S.wait("vector", STEN, wbs_used[slot])
                        S.emit("vector", lambda e, slot=slot: e.tensor_copy(
                            out=WBSQ[slot], in_=WSTQ[slot]))
                        wst_used[slot] = SVEC.n
                        v_cast = SVEC.n
                        S.wait("tensor", SVEC, v_cast)
                        if qt == 0:
                            S.wait("tensor", SSCA)
                        for dcc in range(4):
                            for oc in range(2):
                                PSX = PS0 if oc == 0 else PS1
                                for th in range(2):
                                    dc = 4 * qt + dcc
                                    S.emit("tensor", lambda e, slot=slot,
                                           dcc=dcc, oc=oc, th=th, dc=dc,
                                           PSX=PSX: e.matmul(
                                        PSX[:, 512 * th:512 * (th + 1)],
                                        WBSQ[slot][:, 256 * dcc + 128 * oc:
                                                   256 * dcc + 128 * (oc + 1)],
                                        XZ[:, 1024 * dc + 512 * th:
                                           1024 * dc + 512 * (th + 1)],
                                        start=(dc == 0), stop=(dc == 15)))
                        wbs_used[slot] = S.inc("tensor", STEN)
                    v_mm = STEN.n
                    S.wait("scalar", STEN, v_mm)
                    for oc in range(2):
                        PSX = PS0 if oc == 0 else PS1
                        occ = og * 2 + oc
                        if mat == "wq":
                            bias_ap = BQS[:, occ:occ + 1] if l == 0 else ZBI[:, :]
                            S.emit("scalar", lambda e, PSX=PSX, occ=occ,
                                   bias_ap=bias_ap: e.activation(
                                out=QA[:, 1024 * occ:1024 * (occ + 1)],
                                in_=PSX[:, :], func=AF.Identity,
                                bias=bias_ap, scale=SCALE))
                        else:
                            j = 1 if mat == "wk" else 2
                            bias_ap = BIA[:, 16 * j + occ:16 * j + occ + 1] \
                                if l == 0 else ZBI[:, :]
                            kslot = occ % 2
                            S.wait("scalar", SKV[kslot])
                            S.emit("scalar", lambda e, PSX=PSX, kslot=kslot,
                                   bias_ap=bias_ap: e.activation(
                                out=KVST[kslot], in_=PSX[:, :],
                                func=AF.Identity, bias=bias_ap, scale=1.0))
                            vs = SSCA.n
                            S.wait("sync", SSCA, vs)
                            dstm = kv_c[(l, "k" if mat == "wk" else "v")]
                            S._raw_emit("sync", lambda e, dstm=dstm, occ=occ,
                                        kslot=kslot: e.dma_start(
                                out=dstm[128 * occ:128 * (occ + 1), :],
                                in_=KVST[kslot]))
                            S.inc("sync", SKV[kslot], 16)
                    if hook is not None:
                        hook()
                if mat in ("wk", "wv"):
                    kv_done[mat] = (SKV[0].n, SKV[1].n)
            return kv_done

        def kv_allgather(l, kv_done):
            out = {}
            for m, key in (("k", "wk"), ("v", "wv")):
                S.wait("gpsimd", SKV[0], kv_done[key][0])
                S.wait("gpsimd", SKV[1], kv_done[key][1])
                S.emit("gpsimd", lambda e, l=l, m=m: e.collective_compute(
                    "AllGather", ALU.bypass, ins=[kv_c[(l, m)][:, :]],
                    outs=[kv_ag[(l, m)][:, :, :]], replica_groups=PAIRS), auto=False)
                out[m] = S.inc("gpsimd", SCC)
            return out

        def attn_phase(l, ag_done):
            KH = [BFA[:, 2048 * i:2048 * (i + 1)] for i in range(2)]
            VST = BFA[:, 4096:6144]
            VT = [BFA[:, 6144 + 2048 * i:6144 + 2048 * (i + 1)] for i in range(2)]
            PRB = [BFA[:, 10240 + 2048 * i:10240 + 2048 * (i + 1)]
                   for i in range(2)]
            SCB = [SCR, PRF]
            MSKF = [BFA[:, 14336 + 256 * s:14336 + 256 * s + 128]
                    for s in range(NQS)]
            MSKL = [BFA[:, 14464 + 256 * s:14464 + 256 * s + 128]
                    for s in range(NQS)]

            S.wait("vector", SPH0, v_ld0)
            S.wait("vector", SGP)
            S.wait("vector", SKV[0])
            S.wait("vector", SKV[1])
            for s in range(NQS):
                S.emit("vector", lambda e, s=s: e.tensor_scalar(
                    out=MSKF[s][:, :], in0=KIOT[:, 128 * s:128 * (s + 1)],
                    scalar1=QIDX[:, s:s + 1], scalar2=None, op0=ALU.is_le))
                S.emit("vector", lambda e, s=s: e.tensor_scalar(
                    out=MSKL[s][:, :],
                    in0=KIOT[:, 128 * (s + 8):128 * (s + 9)],
                    scalar1=QIDX[:, s:s + 1], scalar2=None, op0=ALU.is_le))
                S.emit("vector", lambda e, s=s: e.tensor_scalar(
                    out=CM[:, s:s + 1], in0=QIDX[:, s:s + 1],
                    scalar1=float(128 * (s + 8) - 1), scalar2=None,
                    op0=ALU.is_ge))
            v_msk = S.inc("vector", SVEC)

            kag, vag = kv_ag[(l, "k")], kv_ag[(l, "v")]
            kh_ready = [0, 0]
            kh_used = [STEN.n, STEN.n]
            vst_ready = [0]
            vst_used = [STEN.n]

            def load_k(h):
                slot = h % 2
                S.wait("scalar", STEN, kh_used[slot])
                S.wait("scalar", SCC, ag_done["k"])
                for r in range(2):
                    dma("scalar", SWX[5 + slot],
                        KH[slot][:, 1024 * r:1024 * (r + 1)],
                        kag[r, 128 * h:128 * (h + 1), :])
                kh_ready[slot] = SWX[5 + slot].n

            def load_v(h):
                S.wait("scalar", STEN, vst_used[0])
                S.wait("scalar", SCC, ag_done["v"])
                for r in range(2):
                    dma("scalar", SWX[7], VST[:, 1024 * r:1024 * (r + 1)],
                        vag[r, 128 * h:128 * (h + 1), :])
                vst_ready[0] = SWX[7].n

            S.wait("scalar", STSTG)
            S.wait("scalar", SWCST)
            S.wait("scalar", SDQO)
            S.wait("scalar", SGD)
            load_k(0)
            load_v(0)
            S.wait("tensor", SSCA)
            for h in range(H):
                slot = h % 2
                S.wait("tensor", SWX[7], vst_ready[0])
                S.wait("tensor", SVEC)
                for g in range(4):
                    for j in range(4 * g, 4 * g + 4):
                        S.emit("tensor", lambda e, j=j: e.transpose(
                            PS3b[:, 128 * (j % 4):128 * (j % 4 + 1)],
                            VST[:, 128 * j:128 * (j + 1)], IDE[:, :]))
                    tv = S.inc("tensor", STEN)
                    S.wait("vector", STEN, tv)
                    S.emit("vector", lambda e, g=g, slot=slot: e.tensor_copy(
                        out=VT[slot][:, 512 * g:512 * (g + 1)],
                        in_=PS3b[:, :]))
                    if g < 3:
                        S.wait("tensor", SVEC)
                vst_used[0] = STEN.n
                if h + 1 < H:
                    load_k(h + 1)
                    load_v(h + 1)
                S.wait("tensor", SWX[5 + slot], kh_ready[slot])

                evac_done = [0] * (NQS + 1)
                nrm_done = [0] * NQS
                exp_done = [0] * NQS
                prb_free = [STEN.n, STEN.n]
                scb_free = [SSCA.n, SSCA.n]

                def emit_scores(s):
                    E = 9 + s
                    cols = 128 * E
                    qtile = QA[:, 1024 * h + 128 * s:1024 * h + 128 * (s + 1)]
                    if s >= 1:
                        S.wait("tensor", SVEC, evac_done[s - 1])
                    for half, PSX in ((0, PS0), (1, PS1)):
                        base = 1024 * half
                        width = min(cols - base, 1024)
                        off = 0
                        while off < width:
                            w = min(512, width - off)
                            S.emit("tensor", lambda e, PSX=PSX, base=base,
                                   off=off, w=w, qtile=qtile, slot=slot:
                                   e.matmul(
                                PSX[:, off:off + w], qtile,
                                KH[slot][:, base + off:base + off + w],
                                start=True, stop=True))
                            off += 512
                    return S.inc("tensor", STEN)

                def emit_softmax(s, v_sc):
                    E = 9 + s
                    cols = 128 * E
                    sb = SCB[s % 2]
                    pb = PRB[s % 2]
                    S.wait("vector", STEN, v_sc)
                    S.wait("vector", SSCA, scb_free[s % 2])
                    S.emit("vector", lambda e, sb=sb: e.tensor_scalar(
                        out=sb[:, 0:1024], in0=PS0[:, :], scalar1=1.0,
                        scalar2=None, op0=ALU.mult, op1=ALU.max,
                        accum_out=MXV[:, :]))
                    S.emit("vector", lambda e, sb=sb, cols=cols: e.tensor_scalar(
                        out=sb[:, 1024:cols], in0=PS1[:, 0:cols - 1024],
                        scalar1=1.0, scalar2=MXV[:, :], op0=ALU.mult,
                        op1=ALU.max, accum_out=MXV[:, :]))
                    evac_done[s] = SVEC.n
                    v_ev = SVEC.n
                    S.wait("scalar", SVEC, v_ev)
                    S.wait("scalar", STEN, prb_free[s % 2])
                    S.emit("scalar", lambda e: e.activation(
                        out=NMX[:, :], in_=MXV[:, :], func=AF.Identity,
                        scale=-1.0))
                    if s > 0:
                        S.emit("scalar", lambda e, sb=sb, pb=pb, s=s:
                               e.activation(
                            out=pb[:, 0:128 * s], in_=sb[:, 0:128 * s],
                            func=AF.Exp, bias=NMX[:, :], scale=1.0,
                            accum_out=LS0[:, :]))
                    S.emit("scalar", lambda e, sb=sb, s=s, cols=cols:
                           e.activation(
                        out=sb[:, 128 * s:cols], in_=sb[:, 128 * s:cols],
                        func=AF.Exp, bias=NMX[:, :], scale=1.0))
                    exp_done[s] = SSCA.n
                    scb_free[s % 2] = SSCA.n
                    S.wait("vector", SSCA, exp_done[s])
                    S.wait("vector", STEN, prb_free[s % 2])
                    S.wait("vector", SVEC, v_msk)
                    if s == 0:
                        S.emit("vector", lambda e: e.memset(LS0[:, :], 0.0))
                    S.emit("vector", lambda e, sb=sb, pb=pb, s=s:
                           e.scalar_tensor_tensor(
                        out=pb[:, 128 * s:128 * (s + 1)],
                        in0=sb[:, 128 * s:128 * (s + 1)],
                        scalar=1.0, in1=MSKF[s][:, :], op0=ALU.mult,
                        op1=ALU.mult, accum_out=LS1[:, :]))
                    S.emit("vector", lambda e, sb=sb, pb=pb, s=s: e.tensor_scalar(
                        out=pb[:, 128 * (s + 1):128 * (s + 8)],
                        in0=sb[:, 128 * (s + 1):128 * (s + 8)],
                        scalar1=CM[:, s:s + 1], scalar2=None,
                        op0=ALU.mult, op1=ALU.add, accum_out=LSA[:, :]))
                    S.emit("vector", lambda e, sb=sb, pb=pb, s=s, cols=cols:
                           e.scalar_tensor_tensor(
                        out=pb[:, 128 * (s + 8):cols],
                        in0=sb[:, 128 * (s + 8):cols],
                        scalar=1.0, in1=MSKL[s][:, :], op0=ALU.mult,
                        op1=ALU.mult, accum_out=LSB[:, :]))
                    S.emit("vector", lambda e: e.tensor_tensor(
                        out=LS0[:, :], in0=LS0[:, :], in1=LS1[:, :],
                        op=ALU.add))
                    S.emit("vector", lambda e: e.tensor_tensor(
                        out=LSA[:, :], in0=LSA[:, :], in1=LSB[:, :],
                        op=ALU.add))
                    S.emit("vector", lambda e: e.tensor_tensor(
                        out=LS0[:, :], in0=LS0[:, :], in1=LSA[:, :],
                        op=ALU.add))
                    S.emit("vector", lambda e: e.reciprocal(
                        out=RLS[:, :], in_=LS0[:, :]))
                    S.emit("vector", lambda e, pb=pb, cols=cols: e.tensor_scalar(
                        out=pb[:, 0:cols], in0=pb[:, 0:cols],
                        scalar1=RLS[:, :], scalar2=None, op0=ALU.mult))
                    nrm_done[s] = SVEC.n

                def emit_pv(s):
                    E = 9 + s
                    pb = PRB[s % 2]
                    S.wait("tensor", SVEC, nrm_done[s])
                    ngrp = (E + 3) // 4
                    for g in range(ngrp):
                        j0, j1 = 4 * g, min(4 * g + 4, E)
                        if g > 0:
                            S.wait("tensor", SVEC)
                        for j in range(j0, j1):
                            S.emit("tensor", lambda e, j=j, pb=pb: e.transpose(
                                PS3b[:, 128 * (j % 4):128 * (j % 4 + 1)],
                                pb[:, 128 * j:128 * (j + 1)], IDE[:, :]))
                        tv = S.inc("tensor", STEN)
                        S.wait("vector", STEN, tv)
                        S.emit("vector", lambda e, j0=j0, j1=j1: e.tensor_copy(
                            out=PTB[:, 0:128 * (j1 - j0)],
                            in_=PS3b[:, 0:128 * (j1 - j0)]))
                        vv = SVEC.n
                        S.wait("tensor", SVEC, vv)
                        for j in range(j0, j1):
                            S.emit("tensor", lambda e, j=j, j0=j0, E=E,
                                   slot=slot: e.matmul(
                                PS3f[:, 256:384],
                                VT[slot][:, 128 * j:128 * (j + 1)],
                                PTB[:, 128 * (j - j0):128 * (j - j0 + 1)],
                                start=(j == 0), stop=(j == E - 1)))
                        S.inc("tensor", STEN)
                    prb_free[s % 2] = STEN.n
                    v_pv = STEN.n
                    S.wait("vector", STEN, v_pv)
                    S.emit("vector", lambda e, s=s, h=h: e.tensor_copy(
                        out=XZ[:, 1024 * h + 128 * s:1024 * h + 128 * (s + 1)],
                        in_=PS3f[:, 256:384]))

                for s in range(NQS + 1):
                    if s < NQS:
                        v_sc = emit_scores(s)
                        if s == NQS - 1:
                            kh_used[slot] = v_sc
                        emit_softmax(s, v_sc)
                    if s >= 1:
                        emit_pv(s - 1)

        def oproj_phase(l):
            src = wT[(l, "wo")]
            WOST = [SCR[:, 1024 * i:1024 * (i + 1)] for i in range(2)] + \
                   [PRF[:, 1024 * i:1024 * (i + 1)] for i in range(2)]
            wost_used = [SVEC.n] * 4
            for dh in range(2):
                S.wait("sync", STEN)
                S.wait("sync", SSCA)
                S.wait("sync", SOUT)
                for fc in range(16):
                    sl = fc % 4
                    S.wait("sync", SVEC, wost_used[sl])
                    v = dma("sync", SWX[sl], WOST[sl],
                            src[128 * fc:128 * (fc + 1),
                                1024 * dh:1024 * (dh + 1)])
                    S.wait("vector", SWX[sl], v)
                    S.wait("vector", STEN)
                    S.emit("vector", lambda e, sl=sl, fc=fc: e.tensor_copy(
                        out=QA[:, 1024 * fc:1024 * (fc + 1)], in_=WOST[sl]))
                    wost_used[sl] = SVEC.n
                v_w = SVEC.n
                S.wait("tensor", SVEC, v_w)
                S.wait("tensor", SSCA)
                for s in range(NQS):
                    PSX = PS0 if s % 2 == 0 else PS1
                    S.wait("tensor", SVEC)
                    for fc in range(16):
                        for th in range(2):
                            S.emit("tensor", lambda e, PSX=PSX, fc=fc, th=th,
                                   s=s: e.matmul(
                                PSX[:, 512 * th:512 * (th + 1)],
                                XZ[:, 1024 * fc + 128 * s:
                                   1024 * fc + 128 * (s + 1)],
                                QA[:, 1024 * fc + 512 * th:
                                   1024 * fc + 512 * (th + 1)],
                                start=(fc == 0), stop=(fc == 15)))
                    v_mm = S.inc("tensor", STEN)
                    S.wait("vector", STEN, v_mm)
                    S.emit("vector", lambda e, PSX=PSX, s=s, dh=dh:
                           e.scalar_tensor_tensor(
                        out=R[:, 2048 * s + 1024 * dh:2048 * s + 1024 * (dh + 1)],
                        in0=PSX[:, :], scalar=1.0,
                        in1=R[:, 2048 * s + 1024 * dh:2048 * s + 1024 * (dh + 1)],
                        op0=ALU.mult, op1=ALU.add))
                    S.inc("vector", SVEC)
            if l == 0:
                S.wait("tensor", SVEC)
                for i in range(16):
                    S.wait("tensor", SVEC)
                    S.emit("tensor", lambda e, i=i: e.transpose(
                        PS3f[0:1, 0:128], BIA[:, 48 + i:49 + i], IDEF[:, :]))
                    tv = S.inc("tensor", STEN)
                    S.wait("vector", STEN, tv)
                    S.emit("vector", lambda e, i=i: e.tensor_copy(
                        out=SCR[0:1, 128 * i:128 * (i + 1)],
                        in_=PS3f[0:1, 0:128]))
                    S.inc("vector", SVEC)
                v_row = SVEC.n
                S.wait("tensor", SVEC, v_row)
                for half, PSX in ((0, PS2), (1, PS1)):
                    for th in range(2):
                        S.emit("tensor", lambda e, PSX=PSX, half=half, th=th:
                               e.matmul(
                            PSX[:, 512 * th:512 * (th + 1)],
                            ONER[:, :],
                            SCR[0:1, 1024 * half + 512 * th:
                                1024 * half + 512 * (th + 1)],
                            start=True, stop=True))
                v_bo = S.inc("tensor", STEN)
                S.wait("vector", STEN, v_bo)
                for s in range(NQS):
                    for half, PSX in ((0, PS2), (1, PS1)):
                        S.emit("vector", lambda e, s=s, half=half, PSX=PSX:
                               e.tensor_tensor(
                            out=R[:, 2048 * s + 1024 * half:
                                  2048 * s + 1024 * (half + 1)],
                            in0=R[:, 2048 * s + 1024 * half:
                                  2048 * s + 1024 * (half + 1)],
                            in1=PSX[:, :], op=ALU.add))
                        S.inc("vector", SVEC)

        def ln_phase(rebuild_xt):
            S.wait("vector", SOUT)
            for s in range(NQS):
                for c in range(4):
                    S.emit("vector", lambda e, s=s, c=c: e.bn_stats(
                        out=BNS[:, 6 * c:6 * (c + 1)],
                        in_=R[:, 2048 * s + 512 * c:2048 * s + 512 * (c + 1)]))
                    S.inc("vector", SVEC)
                S.wait("vector", SVEC)
                S.emit("vector", lambda e: e.bn_aggr(out=MV[:, :], in_=BNS[:, :]))
                v_mv = S.inc("vector", SVEC)
                S.wait("scalar", SVEC, v_mv)
                S.emit("scalar", lambda e: e.activation(
                    out=RSTD[:, :], in_=MV[:, 1:2], func=AF.Ln,
                    bias=CONST[:, 0:1], scale=1.0))
                S.emit("scalar", lambda e: e.activation(
                    out=RSTD[:, :], in_=RSTD[:, :], func=AF.Exp, scale=-0.5))
                vs = S.inc("scalar", SSCA)
                S.wait("vector", SSCA, vs)
                S.emit("vector", lambda e, s=s: e.tensor_scalar(
                    out=R[:, 2048 * s:2048 * (s + 1)],
                    in0=R[:, 2048 * s:2048 * (s + 1)],
                    scalar1=MV[:, 0:1], scalar2=RSTD[:, :],
                    op0=ALU.subtract, op1=ALU.mult))
                S.inc("vector", SVEC)
            if rebuild_xt:
                S.wait("tensor", SVEC)
                return xt_rebuild()
            return SVEC.n

        def ffn_phase(l, deq_ag_done, hook=None):
            WG0 = [BFA[:, 2048 * i:2048 * (i + 1)] for i in range(2)]
            WG1 = [BFA[:, 4096 + 2048 * i:4096 + 2048 * (i + 1)] for i in range(2)]
            SST2B = SILB[:, :]
            if l == 0:
                wg_src, wu_src, wd_src = tern_ag["g"], tern_ag["u"], tern_ag["d"]
                etot = 8192
                ag_gu = max(tern_ag_done["g"], tern_ag_done["u"])
                ag_d = tern_ag_done["d"]
            else:
                wg_src, wu_src, wd_src = deq_ag["g"], deq_ag["u"], deq_ag["d"]
                etot = 16384
                ag_gu = ag_d = deq_ag_done
            n_ec = etot // 2048
            rw = 1024 if l == 0 else 2048
            gsrc3 = wg_src.rearrange("r (dc p) e -> r p dc e", p=128)
            usrc3 = wu_src.rearrange("r (dc p) e -> r p dc e", p=128)
            wdr = wd_src.rearrange("r (ec p) d -> r ec p d", p=128)
            wg_used = [STEN.n, STEN.n]
            down_done = [STEN.n]

            S.wait("tensor", SVEC)
            S.wait("tensor", SSCA)
            S.wait("vector", SOUT)
            S.wait("scalar", STSTG)
            S.wait("scalar", SDQO)
            S.wait("scalar", SWCST)
            S.wait("scalar", SGD)

            def wdnr_ap(es):
                if es < 4:
                    return BFA[:, 1024 * es:1024 * (es + 1)]
                return BFA[:, 8192 + 1024 * (es - 4):8192 + 1024 * (es - 3)]

            for ec in range(n_ec):
                S.wait("scalar", SCC, ag_gu)
                S.wait("scalar", SCC, ag_d)
                S.wait("scalar", STEN, down_done[0])
                for es in range(4, 16):
                    e0 = 2048 * ec + 128 * es
                    r, erow = e0 // rw, (e0 % rw) // 128
                    dma("scalar", SWX[4],
                        wdnr_ap(es), wdr[r, erow, :, 0:1024])
                for es in range(16):
                    e0 = 2048 * ec + 128 * es
                    r, c0 = e0 // rw, e0 % rw
                    slot = es % 2
                    S.wait("scalar", STEN, max(wg_used[slot], down_done[0]))
                    dma("scalar", SWX[slot],
                        WG0[slot][:, :].rearrange("p (dc e) -> p dc e", e=128),
                        gsrc3[r, :, :, c0:c0 + 128])
                    dma("scalar", SWX[2 + slot],
                        WG1[slot][:, :].rearrange("p (dc e) -> p dc e", e=128),
                        usrc3[r, :, :, c0:c0 + 128])
                    S.wait("tensor", SWX[slot])
                    S.wait("tensor", SWX[2 + slot])
                    S.wait("tensor", SSCA)
                    if es == 0:
                        S.wait("tensor", SVEC)
                    for th in range(2):
                        for dc in range(16):
                            S.emit("tensor", lambda e, slot=slot, dc=dc, th=th:
                                   e.matmul(
                                PS0[:, 512 * th:512 * (th + 1)],
                                WG0[slot][:, 128 * dc:128 * (dc + 1)],
                                XZ[:, 1024 * dc + 512 * th:
                                   1024 * dc + 512 * (th + 1)],
                                start=(dc == 0), stop=(dc == 15)))
                        for dc in range(16):
                            S.emit("tensor", lambda e, slot=slot, dc=dc, th=th:
                                   e.matmul(
                                PS1[:, 512 * th:512 * (th + 1)],
                                WG1[slot][:, 128 * dc:128 * (dc + 1)],
                                XZ[:, 1024 * dc + 512 * th:
                                   1024 * dc + 512 * (th + 1)],
                                start=(dc == 0), stop=(dc == 15)))
                    v_mm = S.inc("tensor", STEN)
                    wg_used[slot] = v_mm
                    S.wait("scalar", STEN, v_mm)
                    if l == 0:
                        S.emit("scalar", lambda e: e.activation(
                            out=SST2B[:, :], in_=PS1[:, :], func=AF.Silu))
                    else:
                        S.emit("scalar", lambda e: e.activation(
                            out=SST2B[:, :], in_=PS0[:, :],
                            func=AF.Gelu_apprx_tanh))
                    vs = S.inc("scalar", SSCA)
                    S.wait("vector", SSCA, vs)
                    PSO = PS0 if l == 0 else PS1
                    S.emit("vector", lambda e, es=es, PSO=PSO:
                           e.scalar_tensor_tensor(
                        out=QA[:, 1024 * es:1024 * (es + 1)], in0=PSO[:, :],
                        scalar=1.0, in1=SST2B[:, :], op0=ALU.mult,
                        op1=ALU.mult))
                    vv = S.inc("vector", SVEC)
                    S.wait("scalar", SVEC, vv)
                    S.emit("scalar", lambda e: e.nop())
                    if hook is not None:
                        hook()
                v_hid = SVEC.n

                for dh in range(2):
                    S.wait("scalar", STEN)
                    S.wait("scalar", SVEC)
                    es_list = range(4) if dh == 0 else range(16)
                    for es in es_list:
                        e0 = 2048 * ec + 128 * es
                        r, erow = e0 // rw, (e0 % rw) // 128
                        dma("scalar", SWX[4],
                            wdnr_ap(es),
                            wdr[r, erow, :, 1024 * dh:1024 * (dh + 1)])
                    v_wd = SWX[4].n
                    S.wait("tensor", SWX[4], v_wd)
                    S.wait("tensor", SVEC, v_hid)
                    for s in range(NQS):
                        PSX = PS2 if s % 2 == 0 else PS1
                        S.wait("tensor", SVEC)
                        for es in range(16):
                            for th in range(2):
                                S.emit("tensor", lambda e, PSX=PSX, es=es,
                                       th=th, s=s, ap=wdnr_ap(es): e.matmul(
                                    PSX[:, 512 * th:512 * (th + 1)],
                                    QA[:, 1024 * es + 128 * s:
                                       1024 * es + 128 * (s + 1)],
                                    ap[:, 512 * th:512 * (th + 1)],
                                    start=(es == 0), stop=(es == 15)))
                        v_mm = S.inc("tensor", STEN)
                        down_done[0] = v_mm
                        S.wait("vector", STEN, v_mm)
                        S.emit("vector", lambda e, PSX=PSX, s=s, dh=dh:
                               e.scalar_tensor_tensor(
                            out=R[:, 2048 * s + 1024 * dh:
                                  2048 * s + 1024 * (dh + 1)],
                            in0=PSX[:, :], scalar=1.0,
                            in1=R[:, 2048 * s + 1024 * dh:
                                  2048 * s + 1024 * (dh + 1)],
                            op0=ALU.mult, op1=ALU.add))
                        S.inc("vector", SVEC)

        S.wait("tensor", SVEC, v_xt)
        deq_ag_done = 0
        tern_ag_done = {"u": 0, "g": 0, "d": 0}
        v_fin = SVEC.n
        if stop_after >= 1:
            kvd0 = qkv_phase(0, hook=lambda: tern_drv.step(2))
            tern_drv.flush()
        if stop_after >= 2:
            agd0 = kv_allgather(0, kvd0)
            for m in TMAT:
                S.wait("gpsimd", STSTG, tern_done[m][0])
                S.wait("gpsimd", SWCST, tern_done[m][1])
                S.emit("gpsimd", lambda e, m=m: e.collective_compute(
                    "AllGather", ALU.bypass, ins=[tern_c[m][:, :]],
                    outs=[tern_ag[m][:, :, :]], replica_groups=ALL8),
                       auto=False)
                tern_ag_done[m] = S.inc("gpsimd", SCC)
            attn_phase(0, agd0)
        if debug and stop_after >= 2:
            S.wait("sync", SVEC)
            dzr = dbg_d["d_z0"].rearrange("(c p) t -> c p t", p=128)
            dqr = dbg_d["d_q0"].rearrange("(c p) t -> c p t", p=128)
            for fc in range(16):
                dma("sync", SOUT, dzr[fc, :, :], XZ[:, 1024 * fc:1024 * (fc + 1)])
                dma("sync", SOUT, dqr[fc, :, :], QA[:, 1024 * fc:1024 * (fc + 1)])
        if stop_after >= 3:
            oproj_phase(0)
            if debug:
                S.wait("sync", SVEC)
                rr = dbg_d["d_R1"].rearrange("(s p) d -> s p d", p=128)
                for s in range(NQS):
                    dma("sync", SOUT, rr[s, :, :], R[:, 2048 * s:2048 * (s + 1)])
            ln_phase(rebuild_xt=True)
        if stop_after >= 4:
            ffn_phase(0, deq_ag_done, hook=lambda: deq_drv.step(2))
            deq_drv.flush()
            S.wait("gpsimd", SDQLD)
            S.wait("gpsimd", SVEC)
            S.emit("gpsimd", lambda e: e.iota(
                out=KIOT[:, :], pattern=[[1, 2048]], base=0,
                channel_multiplier=0))
            for m in ("g", "u", "d"):
                S.wait("gpsimd", SDQO, deq_done[m][0])
                S.wait("gpsimd", SGD, deq_done[m][1])
                S.emit("gpsimd", lambda e, m=m: e.collective_compute(
                    "AllGather", ALU.bypass, ins=[deq_c[m][:, :]],
                    outs=[deq_ag[m][:, :, :]], replica_groups=ALL8), auto=False)
                S.inc("gpsimd", SCC)
            deq_ag_done = SCC.n
            if debug:
                S.wait("sync", SVEC)
                rr = dbg_d["d_R2"].rearrange("(s p) d -> s p d", p=128)
                for s in range(NQS):
                    dma("sync", SOUT, rr[s, :, :], R[:, 2048 * s:2048 * (s + 1)])
            ln_phase(rebuild_xt=True)
        if stop_after >= 5:
            kvd1 = qkv_phase(1)
            agd1 = kv_allgather(1, kvd1)
            attn_phase(1, agd1)
        if stop_after >= 6:
            oproj_phase(1)
            if debug:
                S.wait("sync", SVEC)
                rr = dbg_d["d_R3"].rearrange("(s p) d -> s p d", p=128)
                for s in range(NQS):
                    dma("sync", SOUT, rr[s, :, :], R[:, 2048 * s:2048 * (s + 1)])
            ln_phase(rebuild_xt=True)
        if stop_after >= 7:
            ffn_phase(1, deq_ag_done)
            v_fin = ln_phase(rebuild_xt=False)

        S.wait("sync", SVEC, v_fin)
        out_r = out_d.rearrange("(s p) d -> s p d", p=128)
        for s in range(NQS):
            dma("sync", SOUT, out_r[s, :, :], R[:, 2048 * s:2048 * (s + 1)])
        if debug and stop_after >= -2:
            S.wait("sync", SCC, tern_ag_done["u"])
            dma("sync", SOUT, dbg_d["d_ternu"][:, :], tern_ag["u"][0, :, :])
        if debug and stop_after >= 2:
            S.wait("sync", SCC, deq_ag_done)
            dma("sync", SOUT, dbg_d["d_deqg"][:, :], deq_ag["g"][0, :, :])
        S.wait("sync", SOUT, SOUT.n)

        @block.sync
        def _(eng):
            S.run("sync", eng)

        @block.scalar
        def _(eng):
            S.run("scalar", eng)

        @block.vector
        def _(eng):
            S.run("vector", eng)

        @block.tensor
        def _(eng):
            S.run("tensor", eng)

        @block.gpsimd
        def _(eng):
            S.run("gpsimd", eng)

    return nc


def _prep_inputs(inputs):
    f32 = np.float32
    x = np.ascontiguousarray(np.asarray(inputs["L0_attn_x"], dtype=f32)
                             .reshape(B * T, D))
    common = {}
    for l, pre in ((0, "L0_attn_"), (1, "L1_attn_")):
        if l == 0:
            wq = np.asarray(inputs[pre + "wq"], f32)
            wk = np.asarray(inputs[pre + "wk"], f32)
            wv = np.asarray(inputs[pre + "wv"], f32)
            wo = np.asarray(inputs[pre + "wo"], f32)
        else:
            wq = np.asarray(inputs[pre + "W_Q"], f32).reshape(D, D)
            wk = np.asarray(inputs[pre + "W_K"], f32).reshape(D, D)
            wv = np.asarray(inputs[pre + "W_V"], f32).reshape(D, D)
            wo = np.asarray(inputs[pre + "W_O"], f32)
        common[f"L{l}_wqT"] = np.ascontiguousarray(wq.T)
        common[f"L{l}_wkT"] = np.ascontiguousarray(wk.T)
        common[f"L{l}_wvT"] = np.ascontiguousarray(wv.T)
        common[f"L{l}_woT"] = np.ascontiguousarray(wo.T)
    for m in ("q", "k", "v", "o"):
        common[f"L0_b{m}"] = np.ascontiguousarray(
            np.asarray(inputs[f"L0_attn_b{m}"], f32).reshape(16, 128).T)
    common["ident"] = np.eye(128, dtype=ml_dtypes.bfloat16)
    common["identf"] = np.eye(128, dtype=np.float32)

    wupT = np.asarray(inputs["L0_ffn_W_up"], f32).T
    wgT = np.asarray(inputs["L0_ffn_W_gate"], f32).T
    wdT = np.asarray(inputs["L0_ffn_W_down"], f32).T

    pg = np.asarray(inputs["L1_ffn_gate_W_packed"], np.int32)
    pu = np.asarray(inputs["L1_ffn_up_W_packed"], np.int32)
    pd = np.asarray(inputs["L1_ffn_down_W_packed"], np.int32)
    sg = np.asarray(inputs["L1_ffn_gate_scales"], f32)
    su = np.asarray(inputs["L1_ffn_up_scales"], f32)
    sd = np.asarray(inputs["L1_ffn_down_scales"], f32)

    maps = []
    for c in range(NCORE):
        mp = dict(common)
        mp["x"] = np.ascontiguousarray(x[c * TOK:(c + 1) * TOK])
        qi = ((c % 2) * 1024 + np.arange(TOK, dtype=f32))
        mp["qidx"] = np.ascontiguousarray(qi.reshape(NQS, 128).T)
        mp["tu_sl"] = np.ascontiguousarray(wupT[:, c * 1024:(c + 1) * 1024])
        mp["tg_sl"] = np.ascontiguousarray(wgT[:, c * 1024:(c + 1) * 1024])
        mp["td_sl"] = np.ascontiguousarray(wdT[c * 1024:(c + 1) * 1024, :])
        mp["pkg_sl"] = np.ascontiguousarray(pg[c * 2048:(c + 1) * 2048, :].T)
        mp["pku_sl"] = np.ascontiguousarray(pu[c * 2048:(c + 1) * 2048, :].T)
        mp["pkd_sl"] = np.ascontiguousarray(pd.T[c * 1024:(c + 1) * 1024, :])
        mp["sxg_sl"] = np.ascontiguousarray(
            np.repeat(sg[c * 2048:(c + 1) * 2048, :].T, 16, axis=0))
        mp["sxu_sl"] = np.ascontiguousarray(
            np.repeat(su[c * 2048:(c + 1) * 2048, :].T, 16, axis=0))
        mp["sxd_sl"] = np.ascontiguousarray(
            np.repeat(sd.T, 16, axis=0)[c * 1024:(c + 1) * 1024, :])
        maps.append(mp)
    return maps


_cache = {}


def kernel(**inputs):
    maps = _prep_inputs(inputs)
    if "prog" not in _cache:
        _cache["prog"] = build_program(debug=False)
    res = run_bass_kernel_spmd(_cache["prog"], maps, core_ids=list(range(NCORE)))
    out = np.concatenate([r["out"] for r in res.results], axis=0)
    return out.reshape(B, T, D).astype(np.float32)


# version 07
# speedup vs baseline: 1.0422x; 1.0422x over previous
import contextlib
import math

import numpy as np
import ml_dtypes

import concourse.bass as bass
import concourse.mybir as mybir
from concourse.bass_utils import run_bass_kernel_spmd

F32 = mybir.dt.float32
BF16 = mybir.dt.bfloat16
I32 = mybir.dt.int32
AF = mybir.ActivationFunctionType
ALU = mybir.AluOpType

B, T, D = 4, 2048, 2048
NCORE = 8
TOK = 1024
NQS = 8
H = 16
SCALE = 1.0 / math.sqrt(128.0)
NW0 = 2048 * 8192

ENGINES = ("sync", "scalar", "vector", "tensor", "gpsimd")
PAIRS = [[0, 1], [2, 3], [4, 5], [6, 7]]
ALL8 = [[0, 1, 2, 3, 4, 5, 6, 7]]


class Sem:
    def __init__(self, handle, name):
        self.h, self.name, self.n = handle, name, 0


class Sched:

    def __init__(self):
        self.q = {e: [] for e in ENGINES}
        self.last_wait = {e: {} for e in ENGINES}
        self.auto = {}

    def _raw_emit(self, eng, fn):
        self.q[eng].append(fn)

    def emit(self, eng, fn, auto=True):
        asem = self.auto.get(eng)
        if asem is not None and auto:
            self.wait(eng, asem, asem.n)
            asem.n += 1
            self.q[eng].append(
                lambda e, fn=fn, s=asem.h: fn(e).then_inc(s, 1))
        else:
            self.q[eng].append(fn)

    def inc(self, eng, sem, k=1):
        asem = self.auto.get(eng)
        if asem is not None and sem is asem:
            return sem.n
        sem.n += k
        fn = self.q[eng].pop()
        self.q[eng].append(lambda e, fn=fn, s=sem.h, k=k: fn(e).then_inc(s, k))
        return sem.n

    def wait(self, eng, sem, val=None):
        v = sem.n if val is None else val
        if v <= 0:
            return
        lw = self.last_wait[eng].get(sem.name, 0)
        if v <= lw:
            return
        self.last_wait[eng][sem.name] = v
        self.q[eng].append(lambda e, v=v, s=sem.h: e.wait_ge(s, v))

    def run(self, eng_name, eng):
        for fn in self.q[eng_name]:
            fn(eng)


def build_program(debug=False, stop_after=99):
    nc = bass.Bass("TRN2", target_bir_lowering=False)

    def din(name, shape, dtype=F32):
        return nc.dram_tensor(name, shape, dtype, kind="ExternalInput").ap()

    x_in = din("x", [TOK, D])
    qidx_in = din("qidx", [128, NQS])
    ident_in = din("ident", [128, 128], BF16)
    identf_in = din("identf", [128, 128], F32)
    wT = {(l, m): din(f"L{l}_{m}T", [D, D])
          for l in (0, 1) for m in ("wq", "wk", "wv", "wo")}
    bias_in = {m: din(f"L0_b{m}", [128, 16]) for m in ("q", "k", "v", "o")}
    tern_sl = {m: din(f"t{m}_sl", [D, 1024] if m != "d" else [1024, D])
               for m in ("u", "g", "d")}
    pk_sl = {m: din(f"pk{m}_sl", [1024, 2048], I32) for m in ("g", "u", "d")}
    sx_sl = {m: din(f"sx{m}_sl", [1024, 2048]) for m in ("g", "u", "d")}
    out_d = nc.dram_tensor("out", [TOK, D], F32, kind="ExternalOutput").ap()

    dbg_d = {}
    if debug:
        for nm, shape, dt in (
            ("d_R1", [TOK, D], F32), ("d_R2", [TOK, D], F32),
            ("d_R3", [TOK, D], F32), ("d_scl", [1, 8], F32),
            ("d_ternu", [D, 1024], BF16), ("d_deqg", [2048, 2048], BF16),
            ("d_q0", [D, TOK], BF16), ("d_z0", [D, TOK], BF16),
        ):
            dbg_d[nm] = nc.dram_tensor(nm, shape, dt, kind="ExternalOutput").ap()

    def idram(name, shape, dtype, shared=False):
        return nc.dram_tensor(name, shape, dtype, kind="Internal",
                              addr_space="Local").ap()

    kv_c = {(l, m): idram(f"kv{l}{m}_c", [D, TOK], BF16)
            for l in (0, 1) for m in ("k", "v")}
    kv_ag = {(l, m): idram(f"kv{l}{m}_ag", [2, D, TOK], BF16)
             for l in (0, 1) for m in ("k", "v")}
    tern_c = {m: idram(f"tern{m}_c", [D, 1024] if m != "d" else [1024, D], BF16)
              for m in ("u", "g", "d")}
    tern_ag = {m: idram(f"tern{m}_ag",
                        [8, D, 1024] if m != "d" else [8, 1024, D], BF16,
                        shared=True) for m in ("u", "g", "d")}
    deq_c = {m: idram(f"deq{m}_c", [2048, 2048], BF16) for m in ("g", "u", "d")}
    deq_ag = {m: idram(f"deq{m}_ag", [8, 2048, 2048], BF16, shared=True)
              for m in ("g", "u", "d")}
    ar_i = idram("ar_i", [1, 64], F32)
    ar_o = idram("ar_o", [1, 64], F32, shared=True)

    stack = contextlib.ExitStack()
    with stack:
        def sbuf(name, shape, dtype):
            return stack.enter_context(nc.sbuf_tensor(name, shape, dtype))

        def psum(name, shape, dtype=F32):
            return stack.enter_context(nc.psum_tensor(name, shape, dtype))

        R = sbuf("R", [128, 8 * 2048], F32)
        XZ = sbuf("XZ", [128, 16 * 1024], BF16)
        QA = sbuf("QA", [128, 16 * 1024], BF16)
        BFA = sbuf("BFA", [128, 20480], BF16)
        DQOB = sbuf("DQOB", [128, 1024], BF16)
        SILB = sbuf("SILB", [128, 1024], BF16)
        SCR = sbuf("SCR", [128, 2048], F32)
        PRF = sbuf("PRF", [128, 2048], F32)
        DQV = sbuf("DQV", [128, 2048], I32)
        KIOT = DQV
        DQNI = sbuf("DQNI", [128, 512], I32)
        DQUI = sbuf("DQUI", [128, 512], I32)
        PTB = sbuf("PTB", [128, 512], BF16)
        IDE = sbuf("IDE", [128, 128], BF16)
        IDEF = sbuf("IDEF", [128, 128], F32)
        QIDX = sbuf("QIDX", [128, NQS], F32)
        BIA = sbuf("BIA", [128, 64], F32)
        BQS = sbuf("BQS", [128, 16], F32)
        ZBI = sbuf("ZBI", [128, 1], F32)
        CONST = sbuf("CONST", [128, 8], F32)
        MV = sbuf("MV", [128, 2], F32)
        BNS = sbuf("BNS", [128, 24], F32)
        RSTD = sbuf("RSTD", [128, 1], F32)
        NMX = sbuf("NMX", [128, 1], F32)
        MXV = sbuf("MXV", [128, 1], F32)
        LS0 = sbuf("LS0", [128, 1], F32)
        LS1 = sbuf("LS1", [128, 1], F32)
        RLS = sbuf("RLS", [128, 1], F32)
        ACC = sbuf("ACC", [128, 4], F32)
        SCL = sbuf("SCL", [128, 64], F32)
        THR = sbuf("THR", [128, 8], F32)
        ONE1 = sbuf("ONE1", [128, 1], F32)
        CM = sbuf("CM", [128, 8], F32)
        LSA = sbuf("LSA", [128, 1], F32)
        LSB = sbuf("LSB", [128, 1], F32)
        ONER = sbuf("ONER", [1, 128], F32)

        PS0 = psum("PS0", [128, 1024])
        PS1 = psum("PS1", [128, 1024])
        PS2 = psum("PS2", [128, 1024])
        PS3f = psum("PS3f", [128, 512])
        PS3b = psum("PS3b", [128, 512], BF16)

        semnames = ["svec", "ssca", "sten", "scc", "sgp",
                    "sph0", "swa", "swb", "swcst", "ststg", "sdqld", "sdqo",
                    "skv0", "skv1", "sout", "sgd"] + [f"swx{i}" for i in range(8)]
        sems = {nm: Sem(stack.enter_context(nc.semaphore(nm)), nm)
                for nm in semnames}
        SVEC, SSCA, STEN = sems["svec"], sems["ssca"], sems["sten"]
        SCC, SGP = sems["scc"], sems["sgp"]
        SPH0, SWA, SWB = sems["sph0"], sems["swa"], sems["swb"]
        SWCST, STSTG = sems["swcst"], sems["ststg"]
        SDQLD, SDQO = sems["sdqld"], sems["sdqo"]
        SKV = [sems["skv0"], sems["skv1"]]
        SOUT, SGD = sems["sout"], sems["sgd"]
        SWX = [sems[f"swx{i}"] for i in range(8)]

        block = stack.enter_context(nc.Block())
        S = Sched()
        S.auto = {"vector": SVEC, "scalar": SSCA, "gpsimd": SGP}

        def dma(q, sem, out, in_):
            S._raw_emit(q, lambda e, o=out, i=in_: e.dma_start(out=o, in_=i))
            sem.n += 16
            fn = S.q[q].pop()
            S.q[q].append(lambda e, fn=fn, s=sem.h: fn(e).then_inc(s, 16))
            return sem.n

        dma("sync", SPH0, QIDX[:, :], qidx_in[:, :])
        dma("sync", SPH0, IDE[:, :], ident_in[:, :])
        dma("sync", SPH0, IDEF[:, :], identf_in[:, :])
        for j, m in enumerate(("q", "k", "v", "o")):
            dma("sync", SPH0, BIA[:, 16 * j:16 * (j + 1)], bias_in[m][:, :])
        x_r = x_in.rearrange("(s p) d -> s p d", p=128)
        for s in range(NQS):
            dma("sync", SPH0, R[:, 2048 * s:2048 * (s + 1)], x_r[s, :, :])
        v_ld0 = SPH0.n

        S.emit("gpsimd", lambda e: e.iota(out=KIOT[:, :], pattern=[[1, 2048]],
                                          base=0, channel_multiplier=0))
        S.inc("gpsimd", SGP)
        for _ms in [lambda e: e.memset(CONST[:, 0:1], 1e-5),
                    lambda e: e.memset(CONST[:, 1:2], 7.5),
                    lambda e: e.memset(CONST[:, 2:3], -4.0),
                    lambda e: e.memset(CONST[:, 3:4], -12.0),
                    lambda e: e.memset(ONE1[:, :], 1.0),
                    lambda e: e.memset(ZBI[:, :], 0.0),
                    lambda e: e.memset(ACC[:, :], 0.0)]:
            S.emit("gpsimd", _ms)
            S.inc("gpsimd", SGP)
        v_gp0 = SGP.n

        S.wait("vector", SPH0, v_ld0)
        S.emit("vector", lambda e: e.tensor_scalar(
            out=BQS[:, :], in0=BIA[:, 0:16], scalar1=SCALE, scalar2=None,
            op0=ALU.mult))
        S.inc("vector", SVEC)

        def xt_rebuild():
            for s in range(NQS):
                for dc in range(16):
                    idx = s * 16 + dc
                    slot = idx % 2
                    if idx >= 2 or idx == 0:
                        S.wait("tensor", SVEC)
                    S.emit("tensor", lambda e, s=s, dc=dc, slot=slot: e.transpose(
                        PS3f[:, 128 * slot:128 * (slot + 1)],
                        R[:, 2048 * s + 128 * dc:2048 * s + 128 * (dc + 1)],
                        IDEF[:, :]))
                    tv = S.inc("tensor", STEN)
                    S.wait("vector", STEN, tv)
                    S.emit("vector", lambda e, s=s, dc=dc, slot=slot: e.tensor_copy(
                        out=XZ[:, 1024 * dc + 128 * s:1024 * dc + 128 * (s + 1)],
                        in_=PS3f[:, 128 * slot:128 * (slot + 1)]))
                    S.inc("vector", SVEC)
            return SVEC.n

        S.wait("tensor", SPH0, v_ld0)
        S.wait("tensor", SGP, v_gp0)
        S.emit("tensor", lambda e: e.transpose(
            PS3f[0:1, 0:128], ONE1[:, 0:1], IDEF[:, :]))
        tv0 = S.inc("tensor", STEN)
        S.wait("vector", STEN, tv0)
        S.emit("vector", lambda e: e.tensor_copy(out=ONER[:, :],
                                                 in_=PS3f[0:1, 0:128]))
        v_oner = S.inc("vector", SVEC)
        v_xt = xt_rebuild()

        TMAT = ("u", "g", "d") if stop_after >= -2 else ()
        for mi, m in enumerate(TMAT):
            src = tern_sl[m]
            nrow, ncol = src.tensor.shape
            src_r = src.rearrange("(c p) e -> c p e", p=128)
            for t in range(nrow // 128):
                buf = SCR if t % 2 == 0 else PRF
                sbf = SWA if t % 2 == 0 else SWB
                S.wait("sync", SVEC)
                v = dma("sync", sbf, buf[:, 0:ncol], src_r[t, :, :])
                S.wait("vector", sbf, v)
                S.emit("vector", lambda e, buf=buf, ncol=ncol: e.tensor_reduce(
                    out=MXV[:, :], in_=buf[:, 0:ncol], axis=mybir.AxisListType.X,
                    op=ALU.add, apply_absolute_value=True))
                S.inc("vector", SVEC)
                S.wait("vector", SVEC)
                S.wait("vector", SGP, v_gp0)
                S.emit("vector", lambda e, mi=mi: e.tensor_tensor(
                    out=ACC[:, mi:mi + 1], in0=ACC[:, mi:mi + 1], in1=MXV[:, :],
                    op=ALU.add))
                S.inc("vector", SVEC)
        v_abs = SVEC.n
        S.wait("tensor", SVEC, v_abs)
        tern_done = {}
        tern_ag_done = {"u": 0, "g": 0, "d": 0}
        deq_done = {"g": 0, "u": 0, "d": 0}
        v_thr = SVEC.n
        if stop_after >= -2:
            S.emit("tensor", lambda e: e.matmul(
                PS3f[0:1, 0:4], ONE1[:, :], ACC[:, 0:4], start=True, stop=True))
            vt = S.inc("tensor", STEN)
            S.wait("vector", STEN, vt)
            S.emit("vector", lambda e: e.memset(SCL[0:1, 0:64], 0.0))
            S.inc("vector", SVEC)
            S.wait("vector", SVEC)
            S.emit("vector", lambda e: e.tensor_copy(out=SCL[0:1, 0:4],
                                                     in_=PS3f[0:1, 0:4]))
            v_sums = S.inc("vector", SVEC)
            S.wait("sync", SVEC, v_sums)
            v_ari = dma("sync", SGD, ar_i[:, 0:64], SCL[0:1, 0:64])
            S.wait("gpsimd", SGD, v_ari)
            S.emit("gpsimd", lambda e: e.collective_compute(
                "AllReduce", ALU.add, ins=[ar_i[:, :]], outs=[ar_o[:, :]],
                replica_groups=ALL8), auto=False)
            v_ar = S.inc("gpsimd", SCC)
            S.wait("sync", SCC, v_ar)
            v_aro = dma("sync", SGD, SCL[0:1, 0:64], ar_o[:, :])
            S.wait("tensor", SGD, v_aro)
            S.wait("tensor", SVEC, v_oner)
            S.emit("tensor", lambda e: e.matmul(
                PS3f[:, 128:136], ONER[:, :], SCL[0:1, 0:8],
                start=True, stop=True))
            v_bcm = S.inc("tensor", STEN)
            S.wait("vector", STEN, v_bcm)
            S.emit("vector", lambda e: e.tensor_copy(out=SCL[:, 0:8],
                                                     in_=PS3f[:, 128:136]))
            S.inc("vector", SVEC)
            S.emit("vector", lambda e: e.tensor_scalar(
                out=THR[:, :], in0=SCL[:, 0:8], scalar1=0.5 / NW0,
                scalar2=0.5e-8, op0=ALU.mult, op1=ALU.add))
            S.emit("vector", lambda e: e.tensor_scalar(
                out=SCL[:, 0:8], in0=SCL[:, 0:8], scalar1=1.0 / NW0,
                scalar2=1e-8, op0=ALU.mult, op1=ALU.add))
            v_thr = S.inc("vector", SVEC)
            if debug:
                S.wait("sync", SVEC, v_thr)
                dma("sync", SOUT, dbg_d["d_scl"][:, :], SCL[0:1, 0:8])

            TSTG2B = [BFA[:, 4096:5120], BFA[:, 9216:10240]]
            SGNB2 = [BFA[:, 5120:6144], BFA[:, 6144:7168]]
            tern_done = {}
            tern_units = []

            def _tern_unit(mi, m, t, half, is_last):
                src_t = tern_sl[m]
                nrow, ncol = src_t.tensor.shape
                src_r = src_t.rearrange("(c p) e -> c p e", p=128)
                dst_r = tern_c[m].rearrange("(c p) e -> c p e", p=128)
                c0 = 1024 * half
                stv = [0]

                def load(pu):
                    stage = PRF[:, 1024 * pu:1024 * (pu + 1)]
                    S.wait("scalar", SVEC)
                    stv[0] = dma("scalar", SWB if pu == 0 else SWA,
                                 stage, src_r[t, :, c0:c0 + 1024])

                def compute(pu):
                    stage = PRF[:, 1024 * pu:1024 * (pu + 1)]
                    sgnb = SGNB2[pu]
                    S.wait("scalar", SWB if pu == 0 else SWA, stv[0])
                    S.wait("scalar", SVEC, v_thr)
                    S.emit("scalar", lambda e, sgnb=sgnb, stage=stage: e.activation(
                        out=sgnb[:, :], in_=stage, func=AF.Sign))
                    S.emit("scalar", lambda e, stage=stage: e.activation(
                        out=stage, in_=stage, func=AF.Abs))
                    vs = SSCA.n
                    S.wait("vector", SSCA, vs)
                    S.emit("vector", lambda e, mi=mi, stage=stage: e.tensor_scalar(
                        out=stage, in0=stage,
                        scalar1=THR[:, mi:mi + 1], scalar2=None, op0=ALU.is_ge))
                    tsem = STSTG if pu == 0 else SWCST
                    S.wait("vector", tsem)
                    S.emit("vector", lambda e, mi=mi, sgnb=sgnb, stage=stage,
                           pu=pu: e.scalar_tensor_tensor(
                        out=TSTG2B[pu][:, :], in0=sgnb[:, :],
                        scalar=SCL[:, mi:mi + 1], in1=stage,
                        op0=ALU.mult, op1=ALU.mult))
                    vv = SVEC.n
                    S.wait("scalar", SVEC, vv)
                    dma("scalar", tsem, dst_r[t, :, c0:c0 + 1024],
                        TSTG2B[pu][:, :])
                    if is_last:
                        tern_done[m] = (STSTG.n, SWCST.n)
                return (load, compute)

            for mi, m in enumerate(TMAT):
                nrow, ncol = tern_sl[m].tensor.shape
                nt, nh = nrow // 128, ncol // 1024
                cnt = 0
                for t in range(nt):
                    for half in range(nh):
                        cnt += 1
                        tern_units.append(
                            _tern_unit(mi, m, t, half, cnt == nt * nh))

            DQO2 = DQOB[:, :]
            DQN2 = SCR[:, 0:512]
            DQA2 = SCR[:, 512:1024]
            DQB2 = SCR[:, 1024:1536]
            DQG2 = SCR[:, 1536:2048]
            deq_done = {}
            deq_units = []

            def _deq_unit(m, jc, ch, is_last):
                pk = pk_sl[m].rearrange("(c p) n -> c p n", p=128)
                sx = sx_sl[m].rearrange("(c p) n -> c p n", p=128)
                dst_r = deq_c[m].rearrange("(c p two) n -> c two p n",
                                           p=128, two=2)
                c0 = 1024 * ch
                ldv = [0]

                def load(pu):
                    dqv = DQV[:, 1024 * pu:1024 * (pu + 1)]
                    dqs = PRF[:, 1024 * pu:1024 * (pu + 1)]
                    sem = SDQLD if pu == 0 else SWA
                    S.wait("sync", SVEC)
                    dma("sync", sem, dqv, pk[jc, :, c0:c0 + 1024])
                    ldv[0] = dma("sync", sem, dqs, sx[jc, :, c0:c0 + 1024])

                def compute(pu):
                    dqv = DQV[:, 1024 * pu:1024 * (pu + 1)]
                    dqs = PRF[:, 1024 * pu:1024 * (pu + 1)]
                    S.wait("vector", SDQLD if pu == 0 else SWA, ldv[0])
                    for sub in range(2):
                        s0 = 512 * sub
                        for hi in range(2):
                            if hi == 0:
                                S.emit("vector", lambda e, s0=s0, dqv=dqv:
                                       e.tensor_scalar(
                                    out=DQNI[:, :], in0=dqv[:, s0:s0 + 512],
                                    scalar1=15, scalar2=None,
                                    op0=ALU.bitwise_and))
                            else:
                                S.emit("vector", lambda e, s0=s0, dqv=dqv:
                                       e.tensor_scalar(
                                    out=DQNI[:, :], in0=dqv[:, s0:s0 + 512],
                                    scalar1=4, scalar2=None,
                                    op0=ALU.logical_shift_right))
                            S.emit("vector", lambda e: e.tensor_scalar(
                                out=DQUI[:, :], in0=DQNI[:, :], scalar1=7,
                                scalar2=None, op0=ALU.bitwise_and))
                            vv = SVEC.n
                            S.wait("scalar", SVEC, vv)
                            S.emit("scalar", lambda e: e.activation(
                                out=DQG2[:, :], in_=DQNI[:, :], func=AF.Sign,
                                bias=CONST[:, 1:2], scale=-1.0))
                            S.emit("scalar", lambda e: e.activation(
                                out=DQN2[:, :], in_=DQUI[:, :],
                                func=AF.Identity))
                            S.emit("scalar", lambda e: e.activation(
                                out=DQA2[:, :], in_=DQUI[:, :], func=AF.Relu,
                                bias=CONST[:, 2:3], scale=1.0))
                            S.emit("scalar", lambda e: e.activation(
                                out=DQB2[:, :], in_=DQUI[:, :], func=AF.Relu,
                                bias=CONST[:, 3:4], scale=2.0))
                            vs = SSCA.n
                            S.wait("vector", SSCA, vs)
                            S.emit("vector", lambda e: e.tensor_tensor(
                                out=DQA2[:, :], in0=DQN2[:, :], in1=DQA2[:, :],
                                op=ALU.add))
                            S.emit("vector", lambda e: e.tensor_tensor(
                                out=DQA2[:, :], in0=DQA2[:, :], in1=DQB2[:, :],
                                op=ALU.add))
                            S.emit("vector", lambda e: e.scalar_tensor_tensor(
                                out=DQA2[:, :], in0=DQA2[:, :], scalar=0.5,
                                in1=DQG2[:, :], op0=ALU.mult, op1=ALU.mult))
                            oslot = (2 * sub + hi) % 2
                            dsem = SDQO if oslot == 0 else SGD
                            S.wait("vector", dsem)
                            S.emit("vector", lambda e, oslot=oslot, dqs=dqs,
                                   s0=s0: e.tensor_tensor(
                                out=DQO2[:, 512 * oslot:512 * (oslot + 1)],
                                in0=DQA2[:, :], in1=dqs[:, s0:s0 + 512],
                                op=ALU.mult))
                            vv2 = SVEC.n
                            S.wait("sync", SVEC, vv2)
                            dma("sync", dsem,
                                dst_r[jc, hi, :, c0 + s0:c0 + s0 + 512],
                                DQO2[:, 512 * oslot:512 * (oslot + 1)])
                    if is_last:
                        deq_done[m] = (SDQO.n, SGD.n)
                return (load, compute)

            for m in ("g", "u", "d"):
                for jc in range(8):
                    for ch in range(2):
                        deq_units.append(_deq_unit(
                            m, jc, ch, jc == 7 and ch == 1))

        class UnitDriver:
            def __init__(self, units):
                self.units = units
                self.li = 0
                self.ci = 0

            def step(self, k=1):
                for _ in range(k):
                    if self.li < len(self.units):
                        self.units[self.li][0](self.li % 2)
                        self.li += 1
                    if self.ci < self.li - 1 or (self.li == len(self.units)
                                                 and self.ci < self.li):
                        self.units[self.ci][1](self.ci % 2)
                        self.ci += 1

            def flush(self):
                while self.ci < len(self.units):
                    self.step()

        tern_drv = UnitDriver(tern_units if stop_after >= -2 else [])
        deq_drv = UnitDriver(deq_units if stop_after >= -2 else [])
        if stop_after < 1:
            tern_drv.flush()
        if stop_after < 4:
            deq_drv.flush()

        def qkv_phase(l, hook=None):
            WBSQ = [BFA[:, 1024 * i:1024 * (i + 1)] for i in range(2)]
            WSTQ = [SCR[:, 1024 * i:1024 * (i + 1)] for i in range(2)]
            KVST = [BFA[:, 7168 + 1024 * i:7168 + 1024 * (i + 1)] for i in range(2)]
            kv_done = {}
            S.wait("vector", SOUT)
            S.wait("scalar", STSTG)
            S.wait("scalar", SWCST)
            S.wait("scalar", SDQO)
            S.wait("scalar", SGD)
            S.wait("tensor", SVEC)
            S.wait("tensor", SSCA)
            wst_used = [SVEC.n] * 2
            wbs_used = [STEN.n] * 2
            PSR = [PS0, PS1, PS2]
            ring_evac = [SSCA.n] * 3
            rk = [0]
            for mat in ("wk", "wv", "wq"):
                src = wT[(l, mat)].rearrange("(dc p) o -> p dc o", p=128)
                for og in range(8):
                    k0, k1 = rk[0] % 3, (rk[0] + 1) % 3
                    rk[0] += 2
                    PSA, PSB = PSR[k0], PSR[k1]
                    for qt in range(4):
                        slot = qt % 2
                        S.wait("sync", SVEC, wst_used[slot])
                        v = dma("sync", SWX[slot], WSTQ[slot].rearrange(
                            "p (dc o) -> p dc o", o=256),
                            src[:, 4 * qt:4 * qt + 4, 256 * og:256 * (og + 1)])
                        S.wait("vector", SWX[slot], v)
                        S.wait("vector", STEN, wbs_used[slot])
                        S.emit("vector", lambda e, slot=slot: e.tensor_copy(
                            out=WBSQ[slot], in_=WSTQ[slot]))
                        wst_used[slot] = SVEC.n
                        v_cast = SVEC.n
                        S.wait("tensor", SVEC, v_cast)
                        if qt == 0:
                            S.wait("tensor", SSCA, ring_evac[k0])
                            S.wait("tensor", SSCA, ring_evac[k1])
                        for dcc in range(4):
                            for oc in range(2):
                                PSX = PSA if oc == 0 else PSB
                                for th in range(2):
                                    dc = 4 * qt + dcc
                                    S.emit("tensor", lambda e, slot=slot,
                                           dcc=dcc, oc=oc, th=th, dc=dc,
                                           PSX=PSX: e.matmul(
                                        PSX[:, 512 * th:512 * (th + 1)],
                                        WBSQ[slot][:, 256 * dcc + 128 * oc:
                                                   256 * dcc + 128 * (oc + 1)],
                                        XZ[:, 1024 * dc + 512 * th:
                                           1024 * dc + 512 * (th + 1)],
                                        start=(dc == 0), stop=(dc == 15)))
                        wbs_used[slot] = S.inc("tensor", STEN)
                    v_mm = STEN.n
                    S.wait("scalar", STEN, v_mm)
                    for oc in range(2):
                        PSX = PSA if oc == 0 else PSB
                        occ = og * 2 + oc
                        if mat == "wq":
                            bias_ap = BQS[:, occ:occ + 1] if l == 0 else ZBI[:, :]
                            S.emit("scalar", lambda e, PSX=PSX, occ=occ,
                                   bias_ap=bias_ap: e.activation(
                                out=QA[:, 1024 * occ:1024 * (occ + 1)],
                                in_=PSX[:, :], func=AF.Identity,
                                bias=bias_ap, scale=SCALE))
                        else:
                            j = 1 if mat == "wk" else 2
                            bias_ap = BIA[:, 16 * j + occ:16 * j + occ + 1] \
                                if l == 0 else ZBI[:, :]
                            kslot = occ % 2
                            S.wait("scalar", SKV[kslot])
                            S.emit("scalar", lambda e, PSX=PSX, kslot=kslot,
                                   bias_ap=bias_ap: e.activation(
                                out=KVST[kslot], in_=PSX[:, :],
                                func=AF.Identity, bias=bias_ap, scale=1.0))
                            vs = SSCA.n
                            S.wait("sync", SSCA, vs)
                            dstm = kv_c[(l, "k" if mat == "wk" else "v")]
                            S._raw_emit("sync", lambda e, dstm=dstm, occ=occ,
                                        kslot=kslot: e.dma_start(
                                out=dstm[128 * occ:128 * (occ + 1), :],
                                in_=KVST[kslot]))
                            S.inc("sync", SKV[kslot], 16)
                    ring_evac[k0] = SSCA.n
                    ring_evac[k1] = SSCA.n
                    if hook is not None:
                        hook()
                if mat in ("wk", "wv"):
                    kv_done[mat] = (SKV[0].n, SKV[1].n)
            return kv_done

        def kv_allgather(l, kv_done):
            out = {}
            for m, key in (("k", "wk"), ("v", "wv")):
                S.wait("gpsimd", SKV[0], kv_done[key][0])
                S.wait("gpsimd", SKV[1], kv_done[key][1])
                S.emit("gpsimd", lambda e, l=l, m=m: e.collective_compute(
                    "AllGather", ALU.bypass, ins=[kv_c[(l, m)][:, :]],
                    outs=[kv_ag[(l, m)][:, :, :]], replica_groups=PAIRS), auto=False)
                out[m] = S.inc("gpsimd", SCC)
            return out

        def attn_phase(l, ag_done):
            KH = [BFA[:, 2048 * i:2048 * (i + 1)] for i in range(2)]
            VST = BFA[:, 4096:6144]
            VT = [BFA[:, 6144 + 2048 * i:6144 + 2048 * (i + 1)] for i in range(2)]
            PRB = [BFA[:, 10240 + 2048 * i:10240 + 2048 * (i + 1)]
                   for i in range(2)]
            SCB = [SCR, PRF]
            MSKF = [BFA[:, 14336 + 256 * s:14336 + 256 * s + 128]
                    for s in range(NQS)]
            MSKL = [BFA[:, 14464 + 256 * s:14464 + 256 * s + 128]
                    for s in range(NQS)]

            S.wait("vector", SPH0, v_ld0)
            S.wait("vector", SGP)
            S.wait("vector", SKV[0])
            S.wait("vector", SKV[1])
            for s in range(NQS):
                S.emit("vector", lambda e, s=s: e.tensor_scalar(
                    out=MSKF[s][:, :], in0=KIOT[:, 128 * s:128 * (s + 1)],
                    scalar1=QIDX[:, s:s + 1], scalar2=None, op0=ALU.is_le))
                S.emit("vector", lambda e, s=s: e.tensor_scalar(
                    out=MSKL[s][:, :],
                    in0=KIOT[:, 128 * (s + 8):128 * (s + 9)],
                    scalar1=QIDX[:, s:s + 1], scalar2=None, op0=ALU.is_le))
                S.emit("vector", lambda e, s=s: e.tensor_scalar(
                    out=CM[:, s:s + 1], in0=QIDX[:, s:s + 1],
                    scalar1=float(128 * (s + 8) - 1), scalar2=None,
                    op0=ALU.is_ge))
            v_msk = S.inc("vector", SVEC)

            kag, vag = kv_ag[(l, "k")], kv_ag[(l, "v")]
            kh_ready = [0, 0]
            kh_used = [STEN.n, STEN.n]
            vst_ready = [0]
            vst_used = [STEN.n]

            def load_k(h):
                slot = h % 2
                S.wait("scalar", STEN, kh_used[slot])
                S.wait("scalar", SCC, ag_done["k"])
                for r in range(2):
                    dma("scalar", SWX[5 + slot],
                        KH[slot][:, 1024 * r:1024 * (r + 1)],
                        kag[r, 128 * h:128 * (h + 1), :])
                kh_ready[slot] = SWX[5 + slot].n

            def load_v(h):
                S.wait("scalar", STEN, vst_used[0])
                S.wait("scalar", SCC, ag_done["v"])
                for r in range(2):
                    dma("scalar", SWX[7], VST[:, 1024 * r:1024 * (r + 1)],
                        vag[r, 128 * h:128 * (h + 1), :])
                vst_ready[0] = SWX[7].n

            S.wait("scalar", STSTG)
            S.wait("scalar", SWCST)
            S.wait("scalar", SDQO)
            S.wait("scalar", SGD)
            load_k(0)
            load_v(0)
            S.wait("tensor", SSCA)
            for h in range(H):
                slot = h % 2
                S.wait("tensor", SWX[7], vst_ready[0])
                S.wait("tensor", SVEC)
                for g in range(4):
                    for j in range(4 * g, 4 * g + 4):
                        S.emit("tensor", lambda e, j=j: e.transpose(
                            PS3b[:, 128 * (j % 4):128 * (j % 4 + 1)],
                            VST[:, 128 * j:128 * (j + 1)], IDE[:, :]))
                    tv = S.inc("tensor", STEN)
                    S.wait("vector", STEN, tv)
                    S.emit("vector", lambda e, g=g, slot=slot: e.tensor_copy(
                        out=VT[slot][:, 512 * g:512 * (g + 1)],
                        in_=PS3b[:, :]))
                    if g < 3:
                        S.wait("tensor", SVEC)
                vst_used[0] = STEN.n
                if h + 1 < H:
                    load_k(h + 1)
                    load_v(h + 1)
                S.wait("tensor", SWX[5 + slot], kh_ready[slot])

                evac_done = [0] * (NQS + 1)
                nrm_done = [0] * NQS
                exp_done = [0] * NQS
                prb_free = [STEN.n, STEN.n]
                scb_free = [SSCA.n, SSCA.n]

                def emit_scores(s):
                    E = 9 + s
                    cols = 128 * E
                    qtile = QA[:, 1024 * h + 128 * s:1024 * h + 128 * (s + 1)]
                    if s >= 1:
                        S.wait("tensor", SVEC, evac_done[s - 1])
                    for half, PSX in ((0, PS0), (1, PS1)):
                        base = 1024 * half
                        width = min(cols - base, 1024)
                        off = 0
                        while off < width:
                            w = min(512, width - off)
                            S.emit("tensor", lambda e, PSX=PSX, base=base,
                                   off=off, w=w, qtile=qtile, slot=slot:
                                   e.matmul(
                                PSX[:, off:off + w], qtile,
                                KH[slot][:, base + off:base + off + w],
                                start=True, stop=True))
                            off += 512
                    return S.inc("tensor", STEN)

                def emit_softmax(s, v_sc):
                    E = 9 + s
                    cols = 128 * E
                    sb = SCB[s % 2]
                    pb = PRB[s % 2]
                    S.wait("vector", STEN, v_sc)
                    S.wait("vector", SSCA, scb_free[s % 2])
                    S.emit("vector", lambda e, sb=sb: e.tensor_scalar(
                        out=sb[:, 0:1024], in0=PS0[:, :], scalar1=1.0,
                        scalar2=None, op0=ALU.mult, op1=ALU.max,
                        accum_out=MXV[:, :]))
                    S.emit("vector", lambda e, sb=sb, cols=cols: e.tensor_scalar(
                        out=sb[:, 1024:cols], in0=PS1[:, 0:cols - 1024],
                        scalar1=1.0, scalar2=MXV[:, :], op0=ALU.mult,
                        op1=ALU.max, accum_out=MXV[:, :]))
                    evac_done[s] = SVEC.n
                    v_ev = SVEC.n
                    S.wait("scalar", SVEC, v_ev)
                    S.wait("scalar", STEN, prb_free[s % 2])
                    S.emit("scalar", lambda e: e.activation(
                        out=NMX[:, :], in_=MXV[:, :], func=AF.Identity,
                        scale=-1.0))
                    if s > 0:
                        S.emit("scalar", lambda e, sb=sb, pb=pb, s=s:
                               e.activation(
                            out=pb[:, 0:128 * s], in_=sb[:, 0:128 * s],
                            func=AF.Exp, bias=NMX[:, :], scale=1.0,
                            accum_out=LS0[:, :]))
                    S.emit("scalar", lambda e, sb=sb, s=s, cols=cols:
                           e.activation(
                        out=sb[:, 128 * s:cols], in_=sb[:, 128 * s:cols],
                        func=AF.Exp, bias=NMX[:, :], scale=1.0))
                    exp_done[s] = SSCA.n
                    scb_free[s % 2] = SSCA.n
                    S.wait("vector", SSCA, exp_done[s])
                    S.wait("vector", STEN, prb_free[s % 2])
                    S.wait("vector", SVEC, v_msk)
                    if s == 0:
                        S.emit("vector", lambda e: e.memset(LS0[:, :], 0.0))
                    S.emit("vector", lambda e, sb=sb, pb=pb, s=s:
                           e.scalar_tensor_tensor(
                        out=pb[:, 128 * s:128 * (s + 1)],
                        in0=sb[:, 128 * s:128 * (s + 1)],
                        scalar=1.0, in1=MSKF[s][:, :], op0=ALU.mult,
                        op1=ALU.mult, accum_out=LS1[:, :]))
                    S.emit("vector", lambda e, sb=sb, pb=pb, s=s: e.tensor_scalar(
                        out=pb[:, 128 * (s + 1):128 * (s + 8)],
                        in0=sb[:, 128 * (s + 1):128 * (s + 8)],
                        scalar1=CM[:, s:s + 1], scalar2=None,
                        op0=ALU.mult, op1=ALU.add, accum_out=LSA[:, :]))
                    S.emit("vector", lambda e, sb=sb, pb=pb, s=s, cols=cols:
                           e.scalar_tensor_tensor(
                        out=pb[:, 128 * (s + 8):cols],
                        in0=sb[:, 128 * (s + 8):cols],
                        scalar=1.0, in1=MSKL[s][:, :], op0=ALU.mult,
                        op1=ALU.mult, accum_out=LSB[:, :]))
                    S.emit("vector", lambda e: e.tensor_tensor(
                        out=LS0[:, :], in0=LS0[:, :], in1=LS1[:, :],
                        op=ALU.add))
                    S.emit("vector", lambda e: e.tensor_tensor(
                        out=LSA[:, :], in0=LSA[:, :], in1=LSB[:, :],
                        op=ALU.add))
                    S.emit("vector", lambda e: e.tensor_tensor(
                        out=LS0[:, :], in0=LS0[:, :], in1=LSA[:, :],
                        op=ALU.add))
                    S.emit("vector", lambda e: e.reciprocal(
                        out=RLS[:, :], in_=LS0[:, :]))
                    S.emit("vector", lambda e, pb=pb, cols=cols: e.tensor_scalar(
                        out=pb[:, 0:cols], in0=pb[:, 0:cols],
                        scalar1=RLS[:, :], scalar2=None, op0=ALU.mult))
                    nrm_done[s] = SVEC.n

                def emit_pv(s):
                    E = 9 + s
                    pb = PRB[s % 2]
                    S.wait("tensor", SVEC, nrm_done[s])
                    ngrp = (E + 3) // 4
                    for g in range(ngrp):
                        j0, j1 = 4 * g, min(4 * g + 4, E)
                        if g > 0:
                            S.wait("tensor", SVEC)
                        for j in range(j0, j1):
                            S.emit("tensor", lambda e, j=j, pb=pb: e.transpose(
                                PS3b[:, 128 * (j % 4):128 * (j % 4 + 1)],
                                pb[:, 128 * j:128 * (j + 1)], IDE[:, :]))
                        tv = S.inc("tensor", STEN)
                        S.wait("vector", STEN, tv)
                        S.emit("vector", lambda e, j0=j0, j1=j1: e.tensor_copy(
                            out=PTB[:, 0:128 * (j1 - j0)],
                            in_=PS3b[:, 0:128 * (j1 - j0)]))
                        vv = SVEC.n
                        S.wait("tensor", SVEC, vv)
                        for j in range(j0, j1):
                            S.emit("tensor", lambda e, j=j, j0=j0, E=E,
                                   slot=slot: e.matmul(
                                PS3f[:, 256:384],
                                VT[slot][:, 128 * j:128 * (j + 1)],
                                PTB[:, 128 * (j - j0):128 * (j - j0 + 1)],
                                start=(j == 0), stop=(j == E - 1)))
                        S.inc("tensor", STEN)
                    prb_free[s % 2] = STEN.n
                    v_pv = STEN.n
                    S.wait("vector", STEN, v_pv)
                    S.emit("vector", lambda e, s=s, h=h: e.tensor_copy(
                        out=XZ[:, 1024 * h + 128 * s:1024 * h + 128 * (s + 1)],
                        in_=PS3f[:, 256:384]))

                for s in range(NQS + 1):
                    if s < NQS:
                        v_sc = emit_scores(s)
                        if s == NQS - 1:
                            kh_used[slot] = v_sc
                        emit_softmax(s, v_sc)
                    if s >= 1:
                        emit_pv(s - 1)

        def oproj_phase(l):
            src = wT[(l, "wo")]
            WOST = [SCR[:, 1024 * i:1024 * (i + 1)] for i in range(2)] + \
                   [PRF[:, 1024 * i:1024 * (i + 1)] for i in range(2)]
            wost_used = [SVEC.n] * 4
            opk = [0]
            opevac = [SVEC.n] * 3
            for dh in range(2):
                S.wait("sync", STEN)
                S.wait("sync", SSCA)
                S.wait("sync", SOUT)
                for fc in range(16):
                    sl = fc % 4
                    S.wait("sync", SVEC, wost_used[sl])
                    v = dma("sync", SWX[sl], WOST[sl],
                            src[128 * fc:128 * (fc + 1),
                                1024 * dh:1024 * (dh + 1)])
                    S.wait("vector", SWX[sl], v)
                    S.wait("vector", STEN)
                    S.emit("vector", lambda e, sl=sl, fc=fc: e.tensor_copy(
                        out=QA[:, 1024 * fc:1024 * (fc + 1)], in_=WOST[sl]))
                    wost_used[sl] = SVEC.n
                v_w = SVEC.n
                S.wait("tensor", SVEC, v_w)
                S.wait("tensor", SSCA)
                for s in range(NQS):
                    k = opk[0] % 3
                    opk[0] += 1
                    PSX = [PS0, PS1, PS2][k]
                    S.wait("tensor", SVEC, opevac[k])
                    for fc in range(16):
                        for th in range(2):
                            S.emit("tensor", lambda e, PSX=PSX, fc=fc, th=th,
                                   s=s: e.matmul(
                                PSX[:, 512 * th:512 * (th + 1)],
                                XZ[:, 1024 * fc + 128 * s:
                                   1024 * fc + 128 * (s + 1)],
                                QA[:, 1024 * fc + 512 * th:
                                   1024 * fc + 512 * (th + 1)],
                                start=(fc == 0), stop=(fc == 15)))
                    v_mm = S.inc("tensor", STEN)
                    S.wait("vector", STEN, v_mm)
                    S.emit("vector", lambda e, PSX=PSX, s=s, dh=dh:
                           e.scalar_tensor_tensor(
                        out=R[:, 2048 * s + 1024 * dh:2048 * s + 1024 * (dh + 1)],
                        in0=PSX[:, :], scalar=1.0,
                        in1=R[:, 2048 * s + 1024 * dh:2048 * s + 1024 * (dh + 1)],
                        op0=ALU.mult, op1=ALU.add))
                    opevac[k] = S.inc("vector", SVEC) or SVEC.n
            if l == 0:
                S.wait("tensor", SVEC)
                for i in range(16):
                    S.wait("tensor", SVEC)
                    S.emit("tensor", lambda e, i=i: e.transpose(
                        PS3f[0:1, 0:128], BIA[:, 48 + i:49 + i], IDEF[:, :]))
                    tv = S.inc("tensor", STEN)
                    S.wait("vector", STEN, tv)
                    S.emit("vector", lambda e, i=i: e.tensor_copy(
                        out=SCR[0:1, 128 * i:128 * (i + 1)],
                        in_=PS3f[0:1, 0:128]))
                    S.inc("vector", SVEC)
                v_row = SVEC.n
                S.wait("tensor", SVEC, v_row)
                for half, PSX in ((0, PS2), (1, PS1)):
                    for th in range(2):
                        S.emit("tensor", lambda e, PSX=PSX, half=half, th=th:
                               e.matmul(
                            PSX[:, 512 * th:512 * (th + 1)],
                            ONER[:, :],
                            SCR[0:1, 1024 * half + 512 * th:
                                1024 * half + 512 * (th + 1)],
                            start=True, stop=True))
                v_bo = S.inc("tensor", STEN)
                S.wait("vector", STEN, v_bo)
                for s in range(NQS):
                    for half, PSX in ((0, PS2), (1, PS1)):
                        S.emit("vector", lambda e, s=s, half=half, PSX=PSX:
                               e.tensor_tensor(
                            out=R[:, 2048 * s + 1024 * half:
                                  2048 * s + 1024 * (half + 1)],
                            in0=R[:, 2048 * s + 1024 * half:
                                  2048 * s + 1024 * (half + 1)],
                            in1=PSX[:, :], op=ALU.add))
                        S.inc("vector", SVEC)

        def ln_phase(rebuild_xt):
            S.wait("vector", SOUT)
            for s in range(NQS):
                for c in range(4):
                    S.emit("vector", lambda e, s=s, c=c: e.bn_stats(
                        out=BNS[:, 6 * c:6 * (c + 1)],
                        in_=R[:, 2048 * s + 512 * c:2048 * s + 512 * (c + 1)]))
                    S.inc("vector", SVEC)
                S.wait("vector", SVEC)
                S.emit("vector", lambda e: e.bn_aggr(out=MV[:, :], in_=BNS[:, :]))
                v_mv = S.inc("vector", SVEC)
                S.wait("scalar", SVEC, v_mv)
                S.emit("scalar", lambda e: e.activation(
                    out=RSTD[:, :], in_=MV[:, 1:2], func=AF.Ln,
                    bias=CONST[:, 0:1], scale=1.0))
                S.emit("scalar", lambda e: e.activation(
                    out=RSTD[:, :], in_=RSTD[:, :], func=AF.Exp, scale=-0.5))
                vs = S.inc("scalar", SSCA)
                S.wait("vector", SSCA, vs)
                S.emit("vector", lambda e, s=s: e.tensor_scalar(
                    out=R[:, 2048 * s:2048 * (s + 1)],
                    in0=R[:, 2048 * s:2048 * (s + 1)],
                    scalar1=MV[:, 0:1], scalar2=RSTD[:, :],
                    op0=ALU.subtract, op1=ALU.mult))
                S.inc("vector", SVEC)
            if rebuild_xt:
                S.wait("tensor", SVEC)
                return xt_rebuild()
            return SVEC.n

        def ffn_phase(l, deq_ag_done, hook=None):
            WG0 = [BFA[:, 2048 * i:2048 * (i + 1)] for i in range(2)]
            WG1 = [BFA[:, 4096 + 2048 * i:4096 + 2048 * (i + 1)] for i in range(2)]
            SST2B = SILB[:, :]
            if l == 0:
                wg_src, wu_src, wd_src = tern_ag["g"], tern_ag["u"], tern_ag["d"]
                etot = 8192
                ag_gu = max(tern_ag_done["g"], tern_ag_done["u"])
                ag_d = tern_ag_done["d"]
            else:
                wg_src, wu_src, wd_src = deq_ag["g"], deq_ag["u"], deq_ag["d"]
                etot = 16384
                ag_gu = ag_d = deq_ag_done
            n_ec = etot // 2048
            rw = 1024 if l == 0 else 2048
            gsrc3 = wg_src.rearrange("r (dc p) e -> r p dc e", p=128)
            usrc3 = wu_src.rearrange("r (dc p) e -> r p dc e", p=128)
            wdr = wd_src.rearrange("r (ec p) d -> r ec p d", p=128)
            wg_used = [STEN.n, STEN.n]
            down_done = [STEN.n]

            S.wait("tensor", SVEC)
            S.wait("tensor", SSCA)
            S.wait("vector", SOUT)
            S.wait("scalar", STSTG)
            S.wait("scalar", SDQO)
            S.wait("scalar", SWCST)
            S.wait("scalar", SGD)

            def wdnr_ap(es):
                if es < 4:
                    return BFA[:, 1024 * es:1024 * (es + 1)]
                return BFA[:, 8192 + 1024 * (es - 4):8192 + 1024 * (es - 3)]

            PSR = [PS0, PS1, PS2]
            ring_evac = [SVEC.n] * 3
            silb_evac = [SVEC.n] * 2
            rk = [0]
            for ec in range(n_ec):
                S.wait("scalar", SCC, ag_gu)
                S.wait("scalar", SCC, ag_d)
                S.wait("scalar", STEN, down_done[0])
                for es in range(4, 16):
                    e0 = 2048 * ec + 128 * es
                    r, erow = e0 // rw, (e0 % rw) // 128
                    dma("scalar", SWX[4],
                        wdnr_ap(es), wdr[r, erow, :, 0:1024])
                for es in range(16):
                    e0 = 2048 * ec + 128 * es
                    r, c0 = e0 // rw, e0 % rw
                    slot = es % 2
                    S.wait("scalar", STEN, max(wg_used[slot], down_done[0]))
                    dma("scalar", SWX[slot],
                        WG0[slot][:, :].rearrange("p (dc e) -> p dc e", e=128),
                        gsrc3[r, :, :, c0:c0 + 128])
                    dma("scalar", SWX[2 + slot],
                        WG1[slot][:, :].rearrange("p (dc e) -> p dc e", e=128),
                        usrc3[r, :, :, c0:c0 + 128])
                    S.wait("tensor", SWX[slot])
                    S.wait("tensor", SWX[2 + slot])
                    if es == 0:
                        S.wait("tensor", SVEC)
                    for th in range(2):
                        k = rk[0] % 3
                        rk[0] += 1
                        PSX = PSR[k]
                        S.wait("tensor", SVEC, ring_evac[k])
                        for dc in range(16):
                            S.emit("tensor", lambda e, slot=slot, dc=dc, th=th,
                                   PSX=PSX: e.matmul(
                                PSX[:, 0:512],
                                WG0[slot][:, 128 * dc:128 * (dc + 1)],
                                XZ[:, 1024 * dc + 512 * th:
                                   1024 * dc + 512 * (th + 1)],
                                start=(dc == 0), stop=(dc == 15)))
                        for dc in range(16):
                            S.emit("tensor", lambda e, slot=slot, dc=dc, th=th,
                                   PSX=PSX: e.matmul(
                                PSX[:, 512:1024],
                                WG1[slot][:, 128 * dc:128 * (dc + 1)],
                                XZ[:, 1024 * dc + 512 * th:
                                   1024 * dc + 512 * (th + 1)],
                                start=(dc == 0), stop=(dc == 15)))
                        v_mm = S.inc("tensor", STEN)
                        if th == 1:
                            wg_used[slot] = v_mm
                        S.wait("scalar", STEN, v_mm)
                        S.wait("scalar", SVEC, silb_evac[th])
                        if l == 0:
                            S.emit("scalar", lambda e, PSX=PSX, th=th:
                                   e.activation(
                                out=SST2B[:, 512 * th:512 * (th + 1)],
                                in_=PSX[:, 512:1024], func=AF.Silu))
                            gsrc_ps = 0
                        else:
                            S.emit("scalar", lambda e, PSX=PSX, th=th:
                                   e.activation(
                                out=SST2B[:, 512 * th:512 * (th + 1)],
                                in_=PSX[:, 0:512], func=AF.Gelu_apprx_tanh))
                            gsrc_ps = 512
                        vs = SSCA.n
                        S.wait("vector", SSCA, vs)
                        S.emit("vector", lambda e, es=es, th=th, PSX=PSX,
                               gsrc_ps=gsrc_ps: e.scalar_tensor_tensor(
                            out=QA[:, 1024 * es + 512 * th:
                                   1024 * es + 512 * (th + 1)],
                            in0=PSX[:, gsrc_ps:gsrc_ps + 512],
                            scalar=1.0, in1=SST2B[:, 512 * th:512 * (th + 1)],
                            op0=ALU.mult, op1=ALU.mult))
                        ring_evac[k] = SVEC.n
                        silb_evac[th] = SVEC.n
                    if hook is not None:
                        hook()
                v_hid = SVEC.n

                for dh in range(2):
                    S.wait("scalar", STEN)
                    S.wait("scalar", SVEC)
                    es_list = range(4) if dh == 0 else range(16)
                    for es in es_list:
                        e0 = 2048 * ec + 128 * es
                        r, erow = e0 // rw, (e0 % rw) // 128
                        dma("scalar", SWX[4],
                            wdnr_ap(es),
                            wdr[r, erow, :, 1024 * dh:1024 * (dh + 1)])
                    v_wd = SWX[4].n
                    S.wait("tensor", SWX[4], v_wd)
                    S.wait("tensor", SVEC, v_hid)
                    for s in range(NQS):
                        k = rk[0] % 3
                        rk[0] += 1
                        PSX = PSR[k]
                        S.wait("tensor", SVEC, ring_evac[k])
                        for es in range(16):
                            for th in range(2):
                                S.emit("tensor", lambda e, PSX=PSX, es=es,
                                       th=th, s=s, ap=wdnr_ap(es): e.matmul(
                                    PSX[:, 512 * th:512 * (th + 1)],
                                    QA[:, 1024 * es + 128 * s:
                                       1024 * es + 128 * (s + 1)],
                                    ap[:, 512 * th:512 * (th + 1)],
                                    start=(es == 0), stop=(es == 15)))
                        v_mm = S.inc("tensor", STEN)
                        down_done[0] = v_mm
                        S.wait("vector", STEN, v_mm)
                        S.emit("vector", lambda e, PSX=PSX, s=s, dh=dh:
                               e.scalar_tensor_tensor(
                            out=R[:, 2048 * s + 1024 * dh:
                                  2048 * s + 1024 * (dh + 1)],
                            in0=PSX[:, :], scalar=1.0,
                            in1=R[:, 2048 * s + 1024 * dh:
                                  2048 * s + 1024 * (dh + 1)],
                            op0=ALU.mult, op1=ALU.add))
                        ring_evac[k] = S.inc("vector", SVEC) or SVEC.n

        S.wait("tensor", SVEC, v_xt)
        deq_ag_done = 0
        tern_ag_done = {"u": 0, "g": 0, "d": 0}
        v_fin = SVEC.n
        if stop_after >= 1:
            kvd0 = qkv_phase(0, hook=lambda: tern_drv.step(2))
            tern_drv.flush()
        if stop_after >= 2:
            agd0 = kv_allgather(0, kvd0)
            for m in TMAT:
                S.wait("gpsimd", STSTG, tern_done[m][0])
                S.wait("gpsimd", SWCST, tern_done[m][1])
                S.emit("gpsimd", lambda e, m=m: e.collective_compute(
                    "AllGather", ALU.bypass, ins=[tern_c[m][:, :]],
                    outs=[tern_ag[m][:, :, :]], replica_groups=ALL8),
                       auto=False)
                tern_ag_done[m] = S.inc("gpsimd", SCC)
            attn_phase(0, agd0)
        if debug and stop_after >= 2:
            S.wait("sync", SVEC)
            dzr = dbg_d["d_z0"].rearrange("(c p) t -> c p t", p=128)
            dqr = dbg_d["d_q0"].rearrange("(c p) t -> c p t", p=128)
            for fc in range(16):
                dma("sync", SOUT, dzr[fc, :, :], XZ[:, 1024 * fc:1024 * (fc + 1)])
                dma("sync", SOUT, dqr[fc, :, :], QA[:, 1024 * fc:1024 * (fc + 1)])
        if stop_after >= 3:
            oproj_phase(0)
            if debug:
                S.wait("sync", SVEC)
                rr = dbg_d["d_R1"].rearrange("(s p) d -> s p d", p=128)
                for s in range(NQS):
                    dma("sync", SOUT, rr[s, :, :], R[:, 2048 * s:2048 * (s + 1)])
            ln_phase(rebuild_xt=True)
        if stop_after >= 4:
            ffn_phase(0, deq_ag_done, hook=lambda: deq_drv.step(2))
            deq_drv.flush()
            S.wait("gpsimd", SDQLD)
            S.wait("gpsimd", SVEC)
            S.emit("gpsimd", lambda e: e.iota(
                out=KIOT[:, :], pattern=[[1, 2048]], base=0,
                channel_multiplier=0))
            for m in ("g", "u", "d"):
                S.wait("gpsimd", SDQO, deq_done[m][0])
                S.wait("gpsimd", SGD, deq_done[m][1])
                S.emit("gpsimd", lambda e, m=m: e.collective_compute(
                    "AllGather", ALU.bypass, ins=[deq_c[m][:, :]],
                    outs=[deq_ag[m][:, :, :]], replica_groups=ALL8), auto=False)
                S.inc("gpsimd", SCC)
            deq_ag_done = SCC.n
            if debug:
                S.wait("sync", SVEC)
                rr = dbg_d["d_R2"].rearrange("(s p) d -> s p d", p=128)
                for s in range(NQS):
                    dma("sync", SOUT, rr[s, :, :], R[:, 2048 * s:2048 * (s + 1)])
            ln_phase(rebuild_xt=True)
        if stop_after >= 5:
            kvd1 = qkv_phase(1)
            agd1 = kv_allgather(1, kvd1)
            attn_phase(1, agd1)
        if stop_after >= 6:
            oproj_phase(1)
            if debug:
                S.wait("sync", SVEC)
                rr = dbg_d["d_R3"].rearrange("(s p) d -> s p d", p=128)
                for s in range(NQS):
                    dma("sync", SOUT, rr[s, :, :], R[:, 2048 * s:2048 * (s + 1)])
            ln_phase(rebuild_xt=True)
        if stop_after >= 7:
            ffn_phase(1, deq_ag_done)
            v_fin = ln_phase(rebuild_xt=False)

        S.wait("sync", SVEC, v_fin)
        out_r = out_d.rearrange("(s p) d -> s p d", p=128)
        for s in range(NQS):
            dma("sync", SOUT, out_r[s, :, :], R[:, 2048 * s:2048 * (s + 1)])
        if debug and stop_after >= -2:
            S.wait("sync", SCC, tern_ag_done["u"])
            dma("sync", SOUT, dbg_d["d_ternu"][:, :], tern_ag["u"][0, :, :])
        if debug and stop_after >= 2:
            S.wait("sync", SCC, deq_ag_done)
            dma("sync", SOUT, dbg_d["d_deqg"][:, :], deq_ag["g"][0, :, :])
        S.wait("sync", SOUT, SOUT.n)

        @block.sync
        def _(eng):
            S.run("sync", eng)

        @block.scalar
        def _(eng):
            S.run("scalar", eng)

        @block.vector
        def _(eng):
            S.run("vector", eng)

        @block.tensor
        def _(eng):
            S.run("tensor", eng)

        @block.gpsimd
        def _(eng):
            S.run("gpsimd", eng)

    return nc


def _prep_inputs(inputs):
    f32 = np.float32
    x = np.ascontiguousarray(np.asarray(inputs["L0_attn_x"], dtype=f32)
                             .reshape(B * T, D))
    common = {}
    for l, pre in ((0, "L0_attn_"), (1, "L1_attn_")):
        if l == 0:
            wq = np.asarray(inputs[pre + "wq"], f32)
            wk = np.asarray(inputs[pre + "wk"], f32)
            wv = np.asarray(inputs[pre + "wv"], f32)
            wo = np.asarray(inputs[pre + "wo"], f32)
        else:
            wq = np.asarray(inputs[pre + "W_Q"], f32).reshape(D, D)
            wk = np.asarray(inputs[pre + "W_K"], f32).reshape(D, D)
            wv = np.asarray(inputs[pre + "W_V"], f32).reshape(D, D)
            wo = np.asarray(inputs[pre + "W_O"], f32)
        common[f"L{l}_wqT"] = np.ascontiguousarray(wq.T)
        common[f"L{l}_wkT"] = np.ascontiguousarray(wk.T)
        common[f"L{l}_wvT"] = np.ascontiguousarray(wv.T)
        common[f"L{l}_woT"] = np.ascontiguousarray(wo.T)
    for m in ("q", "k", "v", "o"):
        common[f"L0_b{m}"] = np.ascontiguousarray(
            np.asarray(inputs[f"L0_attn_b{m}"], f32).reshape(16, 128).T)
    common["ident"] = np.eye(128, dtype=ml_dtypes.bfloat16)
    common["identf"] = np.eye(128, dtype=np.float32)

    wupT = np.asarray(inputs["L0_ffn_W_up"], f32).T
    wgT = np.asarray(inputs["L0_ffn_W_gate"], f32).T
    wdT = np.asarray(inputs["L0_ffn_W_down"], f32).T

    pg = np.asarray(inputs["L1_ffn_gate_W_packed"], np.int32)
    pu = np.asarray(inputs["L1_ffn_up_W_packed"], np.int32)
    pd = np.asarray(inputs["L1_ffn_down_W_packed"], np.int32)
    sg = np.asarray(inputs["L1_ffn_gate_scales"], f32)
    su = np.asarray(inputs["L1_ffn_up_scales"], f32)
    sd = np.asarray(inputs["L1_ffn_down_scales"], f32)

    maps = []
    for c in range(NCORE):
        mp = dict(common)
        mp["x"] = np.ascontiguousarray(x[c * TOK:(c + 1) * TOK])
        qi = ((c % 2) * 1024 + np.arange(TOK, dtype=f32))
        mp["qidx"] = np.ascontiguousarray(qi.reshape(NQS, 128).T)
        mp["tu_sl"] = np.ascontiguousarray(wupT[:, c * 1024:(c + 1) * 1024])
        mp["tg_sl"] = np.ascontiguousarray(wgT[:, c * 1024:(c + 1) * 1024])
        mp["td_sl"] = np.ascontiguousarray(wdT[c * 1024:(c + 1) * 1024, :])
        mp["pkg_sl"] = np.ascontiguousarray(pg[c * 2048:(c + 1) * 2048, :].T)
        mp["pku_sl"] = np.ascontiguousarray(pu[c * 2048:(c + 1) * 2048, :].T)
        mp["pkd_sl"] = np.ascontiguousarray(pd.T[c * 1024:(c + 1) * 1024, :])
        mp["sxg_sl"] = np.ascontiguousarray(
            np.repeat(sg[c * 2048:(c + 1) * 2048, :].T, 16, axis=0))
        mp["sxu_sl"] = np.ascontiguousarray(
            np.repeat(su[c * 2048:(c + 1) * 2048, :].T, 16, axis=0))
        mp["sxd_sl"] = np.ascontiguousarray(
            np.repeat(sd.T, 16, axis=0)[c * 1024:(c + 1) * 1024, :])
        maps.append(mp)
    return maps


_cache = {}


def kernel(**inputs):
    maps = _prep_inputs(inputs)
    if "prog" not in _cache:
        _cache["prog"] = build_program(debug=False)
    res = run_bass_kernel_spmd(_cache["prog"], maps, core_ids=list(range(NCORE)))
    out = np.concatenate([r["out"] for r in res.results], axis=0)
    return out.reshape(B, T, D).astype(np.float32)


# version 08
# speedup vs baseline: 1.0426x; 1.0004x over previous
import contextlib
import math

import numpy as np
import ml_dtypes

import concourse.bass as bass
import concourse.mybir as mybir
from concourse.bass_utils import run_bass_kernel_spmd

F32 = mybir.dt.float32
BF16 = mybir.dt.bfloat16
I32 = mybir.dt.int32
AF = mybir.ActivationFunctionType
ALU = mybir.AluOpType

B, T, D = 4, 2048, 2048
NCORE = 8
TOK = 1024
NQS = 8
H = 16
SCALE = 1.0 / math.sqrt(128.0)
NW0 = 2048 * 8192

ENGINES = ("sync", "scalar", "vector", "tensor", "gpsimd")
PAIRS = [[0, 1], [2, 3], [4, 5], [6, 7]]
ALL8 = [[0, 1, 2, 3, 4, 5, 6, 7]]


class Sem:
    def __init__(self, handle, name):
        self.h, self.name, self.n = handle, name, 0


class Sched:

    def __init__(self):
        self.q = {e: [] for e in ENGINES}
        self.last_wait = {e: {} for e in ENGINES}
        self.auto = {}

    def _raw_emit(self, eng, fn):
        self.q[eng].append(fn)

    def emit(self, eng, fn, auto=True):
        asem = self.auto.get(eng)
        if asem is not None and auto:
            self.wait(eng, asem, asem.n)
            asem.n += 1
            self.q[eng].append(
                lambda e, fn=fn, s=asem.h: fn(e).then_inc(s, 1))
        else:
            self.q[eng].append(fn)

    def inc(self, eng, sem, k=1):
        asem = self.auto.get(eng)
        if asem is not None and sem is asem:
            return sem.n
        sem.n += k
        fn = self.q[eng].pop()
        self.q[eng].append(lambda e, fn=fn, s=sem.h, k=k: fn(e).then_inc(s, k))
        return sem.n

    def wait(self, eng, sem, val=None):
        v = sem.n if val is None else val
        if v <= 0:
            return
        lw = self.last_wait[eng].get(sem.name, 0)
        if v <= lw:
            return
        self.last_wait[eng][sem.name] = v
        self.q[eng].append(lambda e, v=v, s=sem.h: e.wait_ge(s, v))

    def run(self, eng_name, eng):
        for fn in self.q[eng_name]:
            fn(eng)


def build_program(debug=False, stop_after=99):
    nc = bass.Bass("TRN2", target_bir_lowering=False)

    def din(name, shape, dtype=F32):
        return nc.dram_tensor(name, shape, dtype, kind="ExternalInput").ap()

    x_in = din("x", [TOK, D])
    qidx_in = din("qidx", [128, NQS])
    ident_in = din("ident", [128, 128], BF16)
    identf_in = din("identf", [128, 128], F32)
    wT = {(l, m): din(f"L{l}_{m}T", [D, D])
          for l in (0, 1) for m in ("wq", "wk", "wv", "wo")}
    bias_in = {m: din(f"L0_b{m}", [128, 16]) for m in ("q", "k", "v", "o")}
    tern_sl = {m: din(f"t{m}_sl", [D, 1024] if m != "d" else [1024, D])
               for m in ("u", "g", "d")}
    pk_sl = {m: din(f"pk{m}_sl", [1024, 2048], I32) for m in ("g", "u", "d")}
    sx_sl = {m: din(f"sx{m}_sl", [1024, 2048]) for m in ("g", "u", "d")}
    out_d = nc.dram_tensor("out", [TOK, D], F32, kind="ExternalOutput").ap()

    dbg_d = {}
    if debug:
        for nm, shape, dt in (
            ("d_R1", [TOK, D], F32), ("d_R2", [TOK, D], F32),
            ("d_R3", [TOK, D], F32), ("d_scl", [1, 8], F32),
            ("d_ternu", [D, 1024], BF16), ("d_deqg", [2048, 2048], BF16),
            ("d_q0", [D, TOK], BF16), ("d_z0", [D, TOK], BF16),
        ):
            dbg_d[nm] = nc.dram_tensor(nm, shape, dt, kind="ExternalOutput").ap()

    def idram(name, shape, dtype, shared=False):
        return nc.dram_tensor(name, shape, dtype, kind="Internal",
                              addr_space="Local").ap()

    kv_c = {(l, m): idram(f"kv{l}{m}_c", [D, TOK], BF16)
            for l in (0, 1) for m in ("k", "v")}
    kv_ag = {(l, m): idram(f"kv{l}{m}_ag", [2, D, TOK], BF16)
             for l in (0, 1) for m in ("k", "v")}
    tern_c = {m: idram(f"tern{m}_c", [D, 1024] if m != "d" else [1024, D], BF16)
              for m in ("u", "g", "d")}
    tern_ag = {m: idram(f"tern{m}_ag",
                        [8, D, 1024] if m != "d" else [8, 1024, D], BF16,
                        shared=True) for m in ("u", "g", "d")}
    deq_c = {m: idram(f"deq{m}_c", [2048, 2048], BF16) for m in ("g", "u", "d")}
    deq_ag = {m: idram(f"deq{m}_ag", [8, 2048, 2048], BF16, shared=True)
              for m in ("g", "u", "d")}
    ar_i = idram("ar_i", [1, 64], F32)
    ar_o = idram("ar_o", [1, 64], F32, shared=True)

    stack = contextlib.ExitStack()
    with stack:
        def sbuf(name, shape, dtype):
            return stack.enter_context(nc.sbuf_tensor(name, shape, dtype))

        def psum(name, shape, dtype=F32):
            return stack.enter_context(nc.psum_tensor(name, shape, dtype))

        R = sbuf("R", [128, 8 * 2048], F32)
        XZ = sbuf("XZ", [128, 16 * 1024], BF16)
        QA = sbuf("QA", [128, 16 * 1024], BF16)
        BFA = sbuf("BFA", [128, 18432], BF16)
        SILB = sbuf("SILB", [128, 1024], BF16)
        PTB = SILB[:, 0:512]
        DQF = sbuf("DQF", [128, 4096], BF16)
        SCR = sbuf("SCR", [128, 2048], F32)
        PRF = sbuf("PRF", [128, 2048], F32)
        DQV = sbuf("DQV", [128, 2048], I32)
        KIOT = DQV
        DQNI = sbuf("DQNI", [128, 512], I32)
        DQUI = sbuf("DQUI", [128, 512], I32)
        IDE = sbuf("IDE", [128, 128], BF16)
        IDEF = sbuf("IDEF", [128, 128], F32)
        QIDX = sbuf("QIDX", [128, NQS], F32)
        BIA = sbuf("BIA", [128, 64], F32)
        BQS = sbuf("BQS", [128, 16], F32)
        ZBI = sbuf("ZBI", [128, 1], F32)
        CONST = sbuf("CONST", [128, 8], F32)
        MV = sbuf("MV", [128, 2], F32)
        BNS = sbuf("BNS", [128, 24], F32)
        RSTD = sbuf("RSTD", [128, 1], F32)
        NMX = sbuf("NMX", [128, 1], F32)
        MXV = sbuf("MXV", [128, 1], F32)
        LS0 = sbuf("LS0", [128, 1], F32)
        LS1 = sbuf("LS1", [128, 1], F32)
        RLS = sbuf("RLS", [128, 1], F32)
        ACC = sbuf("ACC", [128, 4], F32)
        SCL = sbuf("SCL", [128, 8], F32)
        THR = sbuf("THR", [128, 8], F32)
        ONE1 = sbuf("ONE1", [128, 1], F32)
        CM = sbuf("CM", [128, 8], F32)
        LSA = sbuf("LSA", [128, 1], F32)
        LSB = sbuf("LSB", [128, 1], F32)
        ONER = sbuf("ONER", [1, 128], F32)

        PS0 = psum("PS0", [128, 1024])
        PS1 = psum("PS1", [128, 1024])
        PS2 = psum("PS2", [128, 1024])
        PS3f = psum("PS3f", [128, 512])
        PS3b = psum("PS3b", [128, 512], BF16)

        semnames = ["svec", "ssca", "sten", "scc", "sgp",
                    "sph0", "swa", "swb", "swcst", "ststg", "sdqld", "sdqo",
                    "skv0", "skv1", "sout", "sgd"] + [f"swx{i}" for i in range(8)]
        sems = {nm: Sem(stack.enter_context(nc.semaphore(nm)), nm)
                for nm in semnames}
        SVEC, SSCA, STEN = sems["svec"], sems["ssca"], sems["sten"]
        SCC, SGP = sems["scc"], sems["sgp"]
        SPH0, SWA, SWB = sems["sph0"], sems["swa"], sems["swb"]
        SWCST, STSTG = sems["swcst"], sems["ststg"]
        SDQLD, SDQO = sems["sdqld"], sems["sdqo"]
        SKV = [sems["skv0"], sems["skv1"]]
        SOUT, SGD = sems["sout"], sems["sgd"]
        SWX = [sems[f"swx{i}"] for i in range(8)]

        block = stack.enter_context(nc.Block())
        S = Sched()
        S.auto = {"vector": SVEC, "scalar": SSCA, "gpsimd": SGP}

        def dma(q, sem, out, in_):
            S._raw_emit(q, lambda e, o=out, i=in_: e.dma_start(out=o, in_=i))
            sem.n += 16
            fn = S.q[q].pop()
            S.q[q].append(lambda e, fn=fn, s=sem.h: fn(e).then_inc(s, 16))
            return sem.n

        dma("sync", SPH0, QIDX[:, :], qidx_in[:, :])
        dma("sync", SPH0, IDE[:, :], ident_in[:, :])
        dma("sync", SPH0, IDEF[:, :], identf_in[:, :])
        for j, m in enumerate(("q", "k", "v", "o")):
            dma("sync", SPH0, BIA[:, 16 * j:16 * (j + 1)], bias_in[m][:, :])
        x_r = x_in.rearrange("(s p) d -> s p d", p=128)
        for s in range(NQS):
            dma("sync", SPH0, R[:, 2048 * s:2048 * (s + 1)], x_r[s, :, :])
        v_ld0 = SPH0.n

        S.emit("gpsimd", lambda e: e.iota(out=KIOT[:, :], pattern=[[1, 2048]],
                                          base=0, channel_multiplier=0))
        S.inc("gpsimd", SGP)
        for _ms in [lambda e: e.memset(CONST[:, 0:1], 1e-5),
                    lambda e: e.memset(CONST[:, 1:2], 7.5),
                    lambda e: e.memset(CONST[:, 2:3], -4.0),
                    lambda e: e.memset(CONST[:, 3:4], -12.0),
                    lambda e: e.memset(ONE1[:, :], 1.0),
                    lambda e: e.memset(ZBI[:, :], 0.0),
                    lambda e: e.memset(ACC[:, :], 0.0)]:
            S.emit("gpsimd", _ms)
            S.inc("gpsimd", SGP)
        v_gp0 = SGP.n

        S.wait("vector", SPH0, v_ld0)
        S.emit("vector", lambda e: e.tensor_scalar(
            out=BQS[:, :], in0=BIA[:, 0:16], scalar1=SCALE, scalar2=None,
            op0=ALU.mult))
        S.inc("vector", SVEC)

        def xt_rebuild():
            for s in range(NQS):
                for dc in range(16):
                    idx = s * 16 + dc
                    slot = idx % 2
                    if idx >= 2 or idx == 0:
                        S.wait("tensor", SVEC)
                    S.emit("tensor", lambda e, s=s, dc=dc, slot=slot: e.transpose(
                        PS3f[:, 128 * slot:128 * (slot + 1)],
                        R[:, 2048 * s + 128 * dc:2048 * s + 128 * (dc + 1)],
                        IDEF[:, :]))
                    tv = S.inc("tensor", STEN)
                    S.wait("vector", STEN, tv)
                    S.emit("vector", lambda e, s=s, dc=dc, slot=slot: e.tensor_copy(
                        out=XZ[:, 1024 * dc + 128 * s:1024 * dc + 128 * (s + 1)],
                        in_=PS3f[:, 128 * slot:128 * (slot + 1)]))
                    S.inc("vector", SVEC)
            return SVEC.n

        S.wait("tensor", SPH0, v_ld0)
        S.wait("tensor", SGP, v_gp0)
        S.emit("tensor", lambda e: e.transpose(
            PS3f[0:1, 0:128], ONE1[:, 0:1], IDEF[:, :]))
        tv0 = S.inc("tensor", STEN)
        S.wait("vector", STEN, tv0)
        S.emit("vector", lambda e: e.tensor_copy(out=ONER[:, :],
                                                 in_=PS3f[0:1, 0:128]))
        v_oner = S.inc("vector", SVEC)
        v_xt = xt_rebuild()

        TMAT = ("u", "g", "d") if stop_after >= -2 else ()
        for mi, m in enumerate(TMAT):
            src = tern_sl[m]
            nrow, ncol = src.tensor.shape
            src_r = src.rearrange("(c p) e -> c p e", p=128)
            for t in range(nrow // 128):
                buf = SCR if t % 2 == 0 else PRF
                sbf = SWA if t % 2 == 0 else SWB
                S.wait("sync", SVEC)
                v = dma("sync", sbf, buf[:, 0:ncol], src_r[t, :, :])
                S.wait("vector", sbf, v)
                S.emit("vector", lambda e, buf=buf, ncol=ncol: e.tensor_reduce(
                    out=MXV[:, :], in_=buf[:, 0:ncol], axis=mybir.AxisListType.X,
                    op=ALU.add, apply_absolute_value=True))
                S.inc("vector", SVEC)
                S.wait("vector", SVEC)
                S.wait("vector", SGP, v_gp0)
                S.emit("vector", lambda e, mi=mi: e.tensor_tensor(
                    out=ACC[:, mi:mi + 1], in0=ACC[:, mi:mi + 1], in1=MXV[:, :],
                    op=ALU.add))
                S.inc("vector", SVEC)
        v_abs = SVEC.n
        S.wait("tensor", SVEC, v_abs)
        tern_done = {}
        tern_ag_done = {"u": 0, "g": 0, "d": 0}
        deq_done = {"g": 0, "u": 0, "d": 0}
        v_thr = SVEC.n
        if stop_after >= -2:
            S.emit("tensor", lambda e: e.matmul(
                PS3f[0:1, 0:4], ONE1[:, :], ACC[:, 0:4], start=True, stop=True))
            vt = S.inc("tensor", STEN)
            S.wait("vector", STEN, vt)
            S.emit("vector", lambda e: e.memset(SCR[0:1, 0:64], 0.0))
            S.inc("vector", SVEC)
            S.wait("vector", SVEC)
            S.emit("vector", lambda e: e.tensor_copy(out=SCR[0:1, 0:4],
                                                     in_=PS3f[0:1, 0:4]))
            v_sums = S.inc("vector", SVEC)
            S.wait("sync", SVEC, v_sums)
            v_ari = dma("sync", SGD, ar_i[:, 0:64], SCR[0:1, 0:64])
            S.wait("gpsimd", SGD, v_ari)
            S.emit("gpsimd", lambda e: e.collective_compute(
                "AllReduce", ALU.add, ins=[ar_i[:, :]], outs=[ar_o[:, :]],
                replica_groups=ALL8), auto=False)
            v_ar = S.inc("gpsimd", SCC)
            S.wait("sync", SCC, v_ar)
            v_aro = dma("sync", SGD, SCR[0:1, 0:64], ar_o[:, :])
            S.wait("tensor", SGD, v_aro)
            S.wait("tensor", SVEC, v_oner)
            S.emit("tensor", lambda e: e.matmul(
                PS3f[:, 128:136], ONER[:, :], SCR[0:1, 0:8],
                start=True, stop=True))
            v_bcm = S.inc("tensor", STEN)
            S.wait("vector", STEN, v_bcm)
            S.emit("vector", lambda e: e.tensor_copy(out=SCL[:, 0:8],
                                                     in_=PS3f[:, 128:136]))
            S.inc("vector", SVEC)
            S.emit("vector", lambda e: e.tensor_scalar(
                out=THR[:, :], in0=SCL[:, 0:8], scalar1=0.5 / NW0,
                scalar2=0.5e-8, op0=ALU.mult, op1=ALU.add))
            S.emit("vector", lambda e: e.tensor_scalar(
                out=SCL[:, 0:8], in0=SCL[:, 0:8], scalar1=1.0 / NW0,
                scalar2=1e-8, op0=ALU.mult, op1=ALU.add))
            v_thr = S.inc("vector", SVEC)
            if debug:
                S.wait("sync", SVEC, v_thr)
                dma("sync", SOUT, dbg_d["d_scl"][:, :], SCL[0:1, 0:8])

            TSTG2B = [BFA[:, 4096:5120], BFA[:, 9216:10240]]
            SGNB2 = [BFA[:, 5120:6144], BFA[:, 6144:7168]]
            tern_done = {}
            tern_units = []

            def _tern_unit(mi, m, t, half, is_last):
                src_t = tern_sl[m]
                nrow, ncol = src_t.tensor.shape
                src_r = src_t.rearrange("(c p) e -> c p e", p=128)
                dst_r = tern_c[m].rearrange("(c p) e -> c p e", p=128)
                c0 = 1024 * half
                stv = [0]

                def load(pu):
                    stage = PRF[:, 1024 * pu:1024 * (pu + 1)]
                    S.wait("scalar", SVEC)
                    stv[0] = dma("scalar", SWB if pu == 0 else SWA,
                                 stage, src_r[t, :, c0:c0 + 1024])

                def compute(pu):
                    stage = PRF[:, 1024 * pu:1024 * (pu + 1)]
                    sgnb = SGNB2[pu]
                    S.wait("scalar", SWB if pu == 0 else SWA, stv[0])
                    S.wait("scalar", SVEC, v_thr)
                    S.emit("scalar", lambda e, sgnb=sgnb, stage=stage: e.activation(
                        out=sgnb[:, :], in_=stage, func=AF.Sign))
                    S.emit("scalar", lambda e, stage=stage: e.activation(
                        out=stage, in_=stage, func=AF.Abs))
                    vs = SSCA.n
                    S.wait("vector", SSCA, vs)
                    S.emit("vector", lambda e, mi=mi, stage=stage: e.tensor_scalar(
                        out=stage, in0=stage,
                        scalar1=THR[:, mi:mi + 1], scalar2=None, op0=ALU.is_ge))
                    tsem = STSTG if pu == 0 else SWCST
                    S.wait("vector", tsem)
                    S.emit("vector", lambda e, mi=mi, sgnb=sgnb, stage=stage,
                           pu=pu: e.scalar_tensor_tensor(
                        out=TSTG2B[pu][:, :], in0=sgnb[:, :],
                        scalar=SCL[:, mi:mi + 1], in1=stage,
                        op0=ALU.mult, op1=ALU.mult))
                    vv = SVEC.n
                    S.wait("scalar", SVEC, vv)
                    dma("scalar", tsem, dst_r[t, :, c0:c0 + 1024],
                        TSTG2B[pu][:, :])
                    if is_last:
                        tern_done[m] = (STSTG.n, SWCST.n)
                return (load, compute)

            for mi, m in enumerate(TMAT):
                nrow, ncol = tern_sl[m].tensor.shape
                nt, nh = nrow // 128, ncol // 1024
                cnt = 0
                for t in range(nt):
                    for half in range(nh):
                        cnt += 1
                        tern_units.append(
                            _tern_unit(mi, m, t, half, cnt == nt * nh))

            DQN2 = SCR[:, 0:512]
            DQA2 = SCR[:, 512:1024]
            DQB2 = SCR[:, 1024:1536]
            DQG2 = SCR[:, 1536:2048]
            deq_done = {}
            deq_units = []

            def _deq_unit(m, jc, ch, is_last):
                pk = pk_sl[m].rearrange("(c p) n -> c p n", p=128)
                sx = sx_sl[m].rearrange("(c p) n -> c p n", p=128)
                dst_r = deq_c[m].rearrange("(c p two) n -> c two p n",
                                           p=128, two=2)
                c0 = 1024 * ch
                ldv = [0]

                def load(pu):
                    dqv = DQV[:, 1024 * pu:1024 * (pu + 1)]
                    dqs = PRF[:, 1024 * pu:1024 * (pu + 1)]
                    sem = SDQLD if pu == 0 else SWA
                    S.wait("sync", SVEC)
                    dma("sync", sem, dqv, pk[jc, :, c0:c0 + 1024])
                    ldv[0] = dma("sync", sem, dqs, sx[jc, :, c0:c0 + 1024])

                def compute(pu):
                    dqv = DQV[:, 1024 * pu:1024 * (pu + 1)]
                    dqs = PRF[:, 1024 * pu:1024 * (pu + 1)]
                    S.wait("vector", SDQLD if pu == 0 else SWA, ldv[0])
                    vb_prev = [0, 0]
                    for t4 in range(4):
                        sub, hi = t4 % 2, t4 // 2
                        s0 = 512 * sub
                        st = t4 % 2
                        b0 = 2048 * st
                        S.wait("vector", SSCA, vb_prev[0])
                        if hi == 0:
                            S.emit("vector", lambda e, s0=s0, dqv=dqv:
                                   e.tensor_scalar(
                                out=DQNI[:, :], in0=dqv[:, s0:s0 + 512],
                                scalar1=15, scalar2=None,
                                op0=ALU.bitwise_and))
                        else:
                            S.emit("vector", lambda e, s0=s0, dqv=dqv:
                                   e.tensor_scalar(
                                out=DQNI[:, :], in0=dqv[:, s0:s0 + 512],
                                scalar1=4, scalar2=None,
                                op0=ALU.logical_shift_right))
                        S.emit("vector", lambda e: e.tensor_scalar(
                            out=DQUI[:, :], in0=DQNI[:, :],
                            scalar1=7, scalar2=None, op0=ALU.bitwise_and))
                        v_a = SVEC.n
                        S.wait("scalar", SVEC, v_a)
                        S.wait("scalar", SDQO if t4 % 2 == 0 else SGD)
                        S.emit("scalar", lambda e, b0=b0: e.activation(
                            out=DQF[:, b0:b0 + 512], in_=DQNI[:, :],
                            func=AF.Sign, bias=CONST[:, 1:2], scale=-1.0))
                        S.emit("scalar", lambda e, b0=b0: e.activation(
                            out=DQF[:, b0 + 512:b0 + 1024], in_=DQUI[:, :],
                            func=AF.Identity))
                        S.emit("scalar", lambda e, b0=b0: e.activation(
                            out=DQF[:, b0 + 1024:b0 + 1536], in_=DQUI[:, :],
                            func=AF.Relu, bias=CONST[:, 2:3], scale=1.0))
                        S.emit("scalar", lambda e, b0=b0: e.activation(
                            out=DQF[:, b0 + 1536:b0 + 2048], in_=DQUI[:, :],
                            func=AF.Relu, bias=CONST[:, 3:4], scale=2.0))
                        vb_prev[0] = vb_prev[1]
                        vb_prev[1] = SSCA.n
                        vb = SSCA.n
                        S.wait("vector", SSCA, vb)
                        S.emit("vector", lambda e, b0=b0: e.tensor_tensor(
                            out=DQF[:, b0 + 1024:b0 + 1536],
                            in0=DQF[:, b0 + 512:b0 + 1024],
                            in1=DQF[:, b0 + 1024:b0 + 1536], op=ALU.add))
                        S.emit("vector", lambda e, b0=b0: e.tensor_tensor(
                            out=DQF[:, b0 + 1024:b0 + 1536],
                            in0=DQF[:, b0 + 1024:b0 + 1536],
                            in1=DQF[:, b0 + 1536:b0 + 2048], op=ALU.add))
                        S.emit("vector", lambda e, b0=b0:
                               e.scalar_tensor_tensor(
                            out=DQF[:, b0 + 1024:b0 + 1536],
                            in0=DQF[:, b0 + 1024:b0 + 1536], scalar=0.5,
                            in1=DQF[:, b0:b0 + 512],
                            op0=ALU.mult, op1=ALU.mult))
                        dsem = SDQO if t4 % 2 == 0 else SGD
                        S.emit("vector", lambda e, b0=b0,
                               dqs=dqs, s0=s0: e.tensor_tensor(
                            out=DQF[:, b0 + 1536:b0 + 2048],
                            in0=DQF[:, b0 + 1024:b0 + 1536],
                            in1=dqs[:, s0:s0 + 512], op=ALU.mult))
                        vv2 = SVEC.n
                        S.wait("sync", SVEC, vv2)
                        dma("sync", dsem,
                            dst_r[jc, hi, :, c0 + s0:c0 + s0 + 512],
                            DQF[:, b0 + 1536:b0 + 2048])
                    if is_last:
                        deq_done[m] = (SDQO.n, SGD.n)
                return (load, compute)

            for m in ("g", "u", "d"):
                for jc in range(8):
                    for ch in range(2):
                        deq_units.append(_deq_unit(
                            m, jc, ch, jc == 7 and ch == 1))

        class UnitDriver:
            def __init__(self, units):
                self.units = units
                self.li = 0
                self.ci = 0

            def step(self, k=1):
                for _ in range(k):
                    if self.li < len(self.units):
                        self.units[self.li][0](self.li % 2)
                        self.li += 1
                    if self.ci < self.li - 1 or (self.li == len(self.units)
                                                 and self.ci < self.li):
                        self.units[self.ci][1](self.ci % 2)
                        self.ci += 1

            def flush(self):
                while self.ci < len(self.units):
                    self.step()

        tern_drv = UnitDriver(tern_units if stop_after >= -2 else [])
        deq_drv = UnitDriver(deq_units if stop_after >= -2 else [])
        if stop_after < 1:
            tern_drv.flush()
        if stop_after < 4:
            deq_drv.flush()

        def qkv_phase(l, hook=None):
            WBSQ = [BFA[:, 1024 * i:1024 * (i + 1)] for i in range(2)]
            WSTQ = [SCR[:, 1024 * i:1024 * (i + 1)] for i in range(2)]
            KVST = [BFA[:, 7168 + 1024 * i:7168 + 1024 * (i + 1)] for i in range(2)]
            kv_done = {}
            S.wait("sync", STEN)
            S.wait("vector", SOUT)
            S.wait("scalar", STSTG)
            S.wait("scalar", SWCST)
            S.wait("scalar", SDQO)
            S.wait("scalar", SGD)
            S.wait("tensor", SVEC)
            S.wait("tensor", SSCA)
            wst_used = [SVEC.n] * 2
            wbs_used = [STEN.n] * 2
            PSR = [PS0, PS1, PS2]
            ring_evac = [SSCA.n] * 3
            rk = [0]
            for mat in ("wk", "wv", "wq"):
                src = wT[(l, mat)].rearrange("(dc p) o -> p dc o", p=128)
                for og in range(8):
                    k0, k1 = rk[0] % 3, (rk[0] + 1) % 3
                    rk[0] += 2
                    PSA, PSB = PSR[k0], PSR[k1]
                    for qt in range(4):
                        slot = qt % 2
                        S.wait("sync", SVEC, wst_used[slot])
                        v = dma("sync", SWX[slot], WSTQ[slot].rearrange(
                            "p (dc o) -> p dc o", o=256),
                            src[:, 4 * qt:4 * qt + 4, 256 * og:256 * (og + 1)])
                        S.wait("vector", SWX[slot], v)
                        S.wait("vector", STEN, wbs_used[slot])
                        S.emit("vector", lambda e, slot=slot: e.tensor_copy(
                            out=WBSQ[slot], in_=WSTQ[slot]))
                        wst_used[slot] = SVEC.n
                        v_cast = SVEC.n
                        S.wait("tensor", SVEC, v_cast)
                        if qt == 0:
                            S.wait("tensor", SSCA, ring_evac[k0])
                            S.wait("tensor", SSCA, ring_evac[k1])
                        for dcc in range(4):
                            for oc in range(2):
                                PSX = PSA if oc == 0 else PSB
                                for th in range(2):
                                    dc = 4 * qt + dcc
                                    S.emit("tensor", lambda e, slot=slot,
                                           dcc=dcc, oc=oc, th=th, dc=dc,
                                           PSX=PSX: e.matmul(
                                        PSX[:, 512 * th:512 * (th + 1)],
                                        WBSQ[slot][:, 256 * dcc + 128 * oc:
                                                   256 * dcc + 128 * (oc + 1)],
                                        XZ[:, 1024 * dc + 512 * th:
                                           1024 * dc + 512 * (th + 1)],
                                        start=(dc == 0), stop=(dc == 15)))
                        wbs_used[slot] = S.inc("tensor", STEN)
                    v_mm = STEN.n
                    S.wait("scalar", STEN, v_mm)
                    for oc in range(2):
                        PSX = PSA if oc == 0 else PSB
                        occ = og * 2 + oc
                        if mat == "wq":
                            bias_ap = BQS[:, occ:occ + 1] if l == 0 else ZBI[:, :]
                            S.emit("scalar", lambda e, PSX=PSX, occ=occ,
                                   bias_ap=bias_ap: e.activation(
                                out=QA[:, 1024 * occ:1024 * (occ + 1)],
                                in_=PSX[:, :], func=AF.Identity,
                                bias=bias_ap, scale=SCALE))
                        else:
                            j = 1 if mat == "wk" else 2
                            bias_ap = BIA[:, 16 * j + occ:16 * j + occ + 1] \
                                if l == 0 else ZBI[:, :]
                            kslot = occ % 2
                            S.wait("scalar", SKV[kslot])
                            S.emit("scalar", lambda e, PSX=PSX, kslot=kslot,
                                   bias_ap=bias_ap: e.activation(
                                out=KVST[kslot], in_=PSX[:, :],
                                func=AF.Identity, bias=bias_ap, scale=1.0))
                            vs = SSCA.n
                            S.wait("sync", SSCA, vs)
                            dstm = kv_c[(l, "k" if mat == "wk" else "v")]
                            S._raw_emit("sync", lambda e, dstm=dstm, occ=occ,
                                        kslot=kslot: e.dma_start(
                                out=dstm[128 * occ:128 * (occ + 1), :],
                                in_=KVST[kslot]))
                            S.inc("sync", SKV[kslot], 16)
                    ring_evac[k0] = SSCA.n
                    ring_evac[k1] = SSCA.n
                    if hook is not None:
                        hook()
                if mat in ("wk", "wv"):
                    kv_done[mat] = (SKV[0].n, SKV[1].n)
            return kv_done

        def kv_allgather(l, kv_done):
            out = {}
            for m, key in (("k", "wk"), ("v", "wv")):
                S.wait("gpsimd", SKV[0], kv_done[key][0])
                S.wait("gpsimd", SKV[1], kv_done[key][1])
                S.emit("gpsimd", lambda e, l=l, m=m: e.collective_compute(
                    "AllGather", ALU.bypass, ins=[kv_c[(l, m)][:, :]],
                    outs=[kv_ag[(l, m)][:, :, :]], replica_groups=PAIRS), auto=False)
                out[m] = S.inc("gpsimd", SCC)
            return out

        def attn_phase(l, ag_done):
            KH = [BFA[:, 2048 * i:2048 * (i + 1)] for i in range(2)]
            VST = BFA[:, 4096:6144]
            VT0 = BFA[:, 6144:8192]
            VT = [VT0, VT0]
            PRB = [BFA[:, 8192 + 2048 * i:8192 + 2048 * (i + 1)]
                   for i in range(2)]
            SCB = [SCR, PRF]
            MSKF = [BFA[:, 12288 + 256 * s:12288 + 256 * s + 128]
                    for s in range(NQS)]
            MSKL = [BFA[:, 12416 + 256 * s:12416 + 256 * s + 128]
                    for s in range(NQS)]

            S.wait("vector", SPH0, v_ld0)
            S.wait("vector", SGP)
            S.wait("vector", SKV[0])
            S.wait("vector", SKV[1])
            for s in range(NQS):
                S.emit("vector", lambda e, s=s: e.tensor_scalar(
                    out=MSKF[s][:, :], in0=KIOT[:, 128 * s:128 * (s + 1)],
                    scalar1=QIDX[:, s:s + 1], scalar2=None, op0=ALU.is_le))
                S.emit("vector", lambda e, s=s: e.tensor_scalar(
                    out=MSKL[s][:, :],
                    in0=KIOT[:, 128 * (s + 8):128 * (s + 9)],
                    scalar1=QIDX[:, s:s + 1], scalar2=None, op0=ALU.is_le))
                S.emit("vector", lambda e, s=s: e.tensor_scalar(
                    out=CM[:, s:s + 1], in0=QIDX[:, s:s + 1],
                    scalar1=float(128 * (s + 8) - 1), scalar2=None,
                    op0=ALU.is_ge))
            v_msk = S.inc("vector", SVEC)

            kag, vag = kv_ag[(l, "k")], kv_ag[(l, "v")]
            kh_ready = [0, 0]
            kh_used = [STEN.n, STEN.n]
            vst_ready = [0]
            vst_used = [STEN.n]

            def load_k(h):
                slot = h % 2
                S.wait("scalar", STEN, kh_used[slot])
                S.wait("scalar", SCC, ag_done["k"])
                for r in range(2):
                    dma("scalar", SWX[5 + slot],
                        KH[slot][:, 1024 * r:1024 * (r + 1)],
                        kag[r, 128 * h:128 * (h + 1), :])
                kh_ready[slot] = SWX[5 + slot].n

            def load_v(h):
                S.wait("scalar", STEN, vst_used[0])
                S.wait("scalar", SCC, ag_done["v"])
                for r in range(2):
                    dma("scalar", SWX[7], VST[:, 1024 * r:1024 * (r + 1)],
                        vag[r, 128 * h:128 * (h + 1), :])
                vst_ready[0] = SWX[7].n

            S.wait("scalar", STSTG)
            S.wait("scalar", SWCST)
            S.wait("scalar", SDQO)
            S.wait("scalar", SGD)
            load_k(0)
            load_v(0)
            S.wait("tensor", SSCA)
            for h in range(H):
                slot = h % 2
                S.wait("tensor", SWX[7], vst_ready[0])
                S.wait("tensor", SVEC)
                for g in range(4):
                    for j in range(4 * g, 4 * g + 4):
                        S.emit("tensor", lambda e, j=j: e.transpose(
                            PS3b[:, 128 * (j % 4):128 * (j % 4 + 1)],
                            VST[:, 128 * j:128 * (j + 1)], IDE[:, :]))
                    tv = S.inc("tensor", STEN)
                    S.wait("vector", STEN, tv)
                    S.emit("vector", lambda e, g=g, slot=slot: e.tensor_copy(
                        out=VT[slot][:, 512 * g:512 * (g + 1)],
                        in_=PS3b[:, :]))
                    if g < 3:
                        S.wait("tensor", SVEC)
                vst_used[0] = STEN.n
                if h + 1 < H:
                    load_k(h + 1)
                    load_v(h + 1)
                S.wait("tensor", SWX[5 + slot], kh_ready[slot])

                evac_done = [0] * (NQS + 1)
                nrm_done = [0] * NQS
                exp_done = [0] * NQS
                prb_free = [STEN.n, STEN.n]
                scb_free = [SSCA.n, SSCA.n]

                def emit_scores(s):
                    E = 9 + s
                    cols = 128 * E
                    qtile = QA[:, 1024 * h + 128 * s:1024 * h + 128 * (s + 1)]
                    if s >= 1:
                        S.wait("tensor", SVEC, evac_done[s - 1])
                    for half, PSX in ((0, PS0), (1, PS1)):
                        base = 1024 * half
                        width = min(cols - base, 1024)
                        off = 0
                        while off < width:
                            w = min(512, width - off)
                            S.emit("tensor", lambda e, PSX=PSX, base=base,
                                   off=off, w=w, qtile=qtile, slot=slot:
                                   e.matmul(
                                PSX[:, off:off + w], qtile,
                                KH[slot][:, base + off:base + off + w],
                                start=True, stop=True))
                            off += 512
                    return S.inc("tensor", STEN)

                def emit_softmax(s, v_sc):
                    E = 9 + s
                    cols = 128 * E
                    sb = SCB[s % 2]
                    pb = PRB[s % 2]
                    S.wait("vector", STEN, v_sc)
                    S.wait("vector", SSCA, scb_free[s % 2])
                    S.emit("vector", lambda e, sb=sb: e.tensor_scalar(
                        out=sb[:, 0:1024], in0=PS0[:, :], scalar1=1.0,
                        scalar2=None, op0=ALU.mult, op1=ALU.max,
                        accum_out=MXV[:, :]))
                    S.emit("vector", lambda e, sb=sb, cols=cols: e.tensor_scalar(
                        out=sb[:, 1024:cols], in0=PS1[:, 0:cols - 1024],
                        scalar1=1.0, scalar2=MXV[:, :], op0=ALU.mult,
                        op1=ALU.max, accum_out=MXV[:, :]))
                    evac_done[s] = SVEC.n
                    v_ev = SVEC.n
                    S.wait("scalar", SVEC, v_ev)
                    S.wait("scalar", STEN, prb_free[s % 2])
                    S.emit("scalar", lambda e: e.activation(
                        out=NMX[:, :], in_=MXV[:, :], func=AF.Identity,
                        scale=-1.0))
                    if s > 0:
                        S.emit("scalar", lambda e, sb=sb, pb=pb, s=s:
                               e.activation(
                            out=pb[:, 0:128 * s], in_=sb[:, 0:128 * s],
                            func=AF.Exp, bias=NMX[:, :], scale=1.0,
                            accum_out=LS0[:, :]))
                    S.emit("scalar", lambda e, sb=sb, s=s, cols=cols:
                           e.activation(
                        out=sb[:, 128 * s:cols], in_=sb[:, 128 * s:cols],
                        func=AF.Exp, bias=NMX[:, :], scale=1.0))
                    exp_done[s] = SSCA.n
                    scb_free[s % 2] = SSCA.n
                    S.wait("vector", SSCA, exp_done[s])
                    S.wait("vector", STEN, prb_free[s % 2])
                    S.wait("vector", SVEC, v_msk)
                    if s == 0:
                        S.emit("vector", lambda e: e.memset(LS0[:, :], 0.0))
                    S.emit("vector", lambda e, sb=sb, pb=pb, s=s:
                           e.scalar_tensor_tensor(
                        out=pb[:, 128 * s:128 * (s + 1)],
                        in0=sb[:, 128 * s:128 * (s + 1)],
                        scalar=1.0, in1=MSKF[s][:, :], op0=ALU.mult,
                        op1=ALU.mult, accum_out=LS1[:, :]))
                    S.emit("vector", lambda e, sb=sb, pb=pb, s=s: e.tensor_scalar(
                        out=pb[:, 128 * (s + 1):128 * (s + 8)],
                        in0=sb[:, 128 * (s + 1):128 * (s + 8)],
                        scalar1=CM[:, s:s + 1], scalar2=None,
                        op0=ALU.mult, op1=ALU.add, accum_out=LSA[:, :]))
                    S.emit("vector", lambda e, sb=sb, pb=pb, s=s, cols=cols:
                           e.scalar_tensor_tensor(
                        out=pb[:, 128 * (s + 8):cols],
                        in0=sb[:, 128 * (s + 8):cols],
                        scalar=1.0, in1=MSKL[s][:, :], op0=ALU.mult,
                        op1=ALU.mult, accum_out=LSB[:, :]))
                    S.emit("vector", lambda e: e.tensor_tensor(
                        out=LS0[:, :], in0=LS0[:, :], in1=LS1[:, :],
                        op=ALU.add))
                    S.emit("vector", lambda e: e.tensor_tensor(
                        out=LSA[:, :], in0=LSA[:, :], in1=LSB[:, :],
                        op=ALU.add))
                    S.emit("vector", lambda e: e.tensor_tensor(
                        out=LS0[:, :], in0=LS0[:, :], in1=LSA[:, :],
                        op=ALU.add))
                    S.emit("vector", lambda e: e.reciprocal(
                        out=RLS[:, :], in_=LS0[:, :]))
                    S.emit("vector", lambda e, pb=pb, cols=cols: e.tensor_scalar(
                        out=pb[:, 0:cols], in0=pb[:, 0:cols],
                        scalar1=RLS[:, :], scalar2=None, op0=ALU.mult))
                    nrm_done[s] = SVEC.n

                def emit_pv(s):
                    E = 9 + s
                    pb = PRB[s % 2]
                    S.wait("tensor", SVEC, nrm_done[s])
                    ngrp = (E + 3) // 4
                    for g in range(ngrp):
                        j0, j1 = 4 * g, min(4 * g + 4, E)
                        if g > 0:
                            S.wait("tensor", SVEC)
                        for j in range(j0, j1):
                            S.emit("tensor", lambda e, j=j, pb=pb: e.transpose(
                                PS3b[:, 128 * (j % 4):128 * (j % 4 + 1)],
                                pb[:, 128 * j:128 * (j + 1)], IDE[:, :]))
                        tv = S.inc("tensor", STEN)
                        S.wait("vector", STEN, tv)
                        S.emit("vector", lambda e, j0=j0, j1=j1: e.tensor_copy(
                            out=PTB[:, 0:128 * (j1 - j0)],
                            in_=PS3b[:, 0:128 * (j1 - j0)]))
                        vv = SVEC.n
                        S.wait("tensor", SVEC, vv)
                        for j in range(j0, j1):
                            S.emit("tensor", lambda e, j=j, j0=j0, E=E,
                                   slot=slot: e.matmul(
                                PS3f[:, 256:384],
                                VT[slot][:, 128 * j:128 * (j + 1)],
                                PTB[:, 128 * (j - j0):128 * (j - j0 + 1)],
                                start=(j == 0), stop=(j == E - 1)))
                        S.inc("tensor", STEN)
                    prb_free[s % 2] = STEN.n
                    v_pv = STEN.n
                    S.wait("vector", STEN, v_pv)
                    S.emit("vector", lambda e, s=s, h=h: e.tensor_copy(
                        out=XZ[:, 1024 * h + 128 * s:1024 * h + 128 * (s + 1)],
                        in_=PS3f[:, 256:384]))

                for s in range(NQS + 1):
                    if s < NQS:
                        v_sc = emit_scores(s)
                        if s == NQS - 1:
                            kh_used[slot] = v_sc
                        emit_softmax(s, v_sc)
                    if s >= 1:
                        emit_pv(s - 1)

        def oproj_phase(l):
            src = wT[(l, "wo")]
            WOST = [SCR[:, 1024 * i:1024 * (i + 1)] for i in range(2)] + \
                   [PRF[:, 1024 * i:1024 * (i + 1)] for i in range(2)]
            wost_used = [SVEC.n] * 4
            opk = [0]
            opevac = [SVEC.n] * 3
            for dh in range(2):
                S.wait("sync", STEN)
                S.wait("sync", SSCA)
                S.wait("sync", SOUT)
                for fc in range(16):
                    sl = fc % 4
                    S.wait("sync", SVEC, wost_used[sl])
                    v = dma("sync", SWX[sl], WOST[sl],
                            src[128 * fc:128 * (fc + 1),
                                1024 * dh:1024 * (dh + 1)])
                    S.wait("vector", SWX[sl], v)
                    S.wait("vector", STEN)
                    S.emit("vector", lambda e, sl=sl, fc=fc: e.tensor_copy(
                        out=QA[:, 1024 * fc:1024 * (fc + 1)], in_=WOST[sl]))
                    wost_used[sl] = SVEC.n
                v_w = SVEC.n
                S.wait("tensor", SVEC, v_w)
                S.wait("tensor", SSCA)
                for s in range(NQS):
                    k = opk[0] % 3
                    opk[0] += 1
                    PSX = [PS0, PS1, PS2][k]
                    S.wait("tensor", SVEC, opevac[k])
                    for fc in range(16):
                        for th in range(2):
                            S.emit("tensor", lambda e, PSX=PSX, fc=fc, th=th,
                                   s=s: e.matmul(
                                PSX[:, 512 * th:512 * (th + 1)],
                                XZ[:, 1024 * fc + 128 * s:
                                   1024 * fc + 128 * (s + 1)],
                                QA[:, 1024 * fc + 512 * th:
                                   1024 * fc + 512 * (th + 1)],
                                start=(fc == 0), stop=(fc == 15)))
                    v_mm = S.inc("tensor", STEN)
                    S.wait("vector", STEN, v_mm)
                    S.emit("vector", lambda e, PSX=PSX, s=s, dh=dh:
                           e.scalar_tensor_tensor(
                        out=R[:, 2048 * s + 1024 * dh:2048 * s + 1024 * (dh + 1)],
                        in0=PSX[:, :], scalar=1.0,
                        in1=R[:, 2048 * s + 1024 * dh:2048 * s + 1024 * (dh + 1)],
                        op0=ALU.mult, op1=ALU.add))
                    opevac[k] = S.inc("vector", SVEC) or SVEC.n
            if l == 0:
                S.wait("tensor", SVEC)
                for i in range(16):
                    S.wait("tensor", SVEC)
                    S.emit("tensor", lambda e, i=i: e.transpose(
                        PS3f[0:1, 0:128], BIA[:, 48 + i:49 + i], IDEF[:, :]))
                    tv = S.inc("tensor", STEN)
                    S.wait("vector", STEN, tv)
                    S.emit("vector", lambda e, i=i: e.tensor_copy(
                        out=SCR[0:1, 128 * i:128 * (i + 1)],
                        in_=PS3f[0:1, 0:128]))
                    S.inc("vector", SVEC)
                v_row = SVEC.n
                S.wait("tensor", SVEC, v_row)
                for half, PSX in ((0, PS2), (1, PS1)):
                    for th in range(2):
                        S.emit("tensor", lambda e, PSX=PSX, half=half, th=th:
                               e.matmul(
                            PSX[:, 512 * th:512 * (th + 1)],
                            ONER[:, :],
                            SCR[0:1, 1024 * half + 512 * th:
                                1024 * half + 512 * (th + 1)],
                            start=True, stop=True))
                v_bo = S.inc("tensor", STEN)
                S.wait("vector", STEN, v_bo)
                for s in range(NQS):
                    for half, PSX in ((0, PS2), (1, PS1)):
                        S.emit("vector", lambda e, s=s, half=half, PSX=PSX:
                               e.tensor_tensor(
                            out=R[:, 2048 * s + 1024 * half:
                                  2048 * s + 1024 * (half + 1)],
                            in0=R[:, 2048 * s + 1024 * half:
                                  2048 * s + 1024 * (half + 1)],
                            in1=PSX[:, :], op=ALU.add))
                        S.inc("vector", SVEC)

        def ln_phase(rebuild_xt):
            S.wait("vector", SOUT)
            for s in range(NQS):
                for c in range(4):
                    S.emit("vector", lambda e, s=s, c=c: e.bn_stats(
                        out=BNS[:, 6 * c:6 * (c + 1)],
                        in_=R[:, 2048 * s + 512 * c:2048 * s + 512 * (c + 1)]))
                    S.inc("vector", SVEC)
                S.wait("vector", SVEC)
                S.emit("vector", lambda e: e.bn_aggr(out=MV[:, :], in_=BNS[:, :]))
                v_mv = S.inc("vector", SVEC)
                S.wait("scalar", SVEC, v_mv)
                S.emit("scalar", lambda e: e.activation(
                    out=RSTD[:, :], in_=MV[:, 1:2], func=AF.Ln,
                    bias=CONST[:, 0:1], scale=1.0))
                S.emit("scalar", lambda e: e.activation(
                    out=RSTD[:, :], in_=RSTD[:, :], func=AF.Exp, scale=-0.5))
                vs = S.inc("scalar", SSCA)
                S.wait("vector", SSCA, vs)
                S.emit("vector", lambda e, s=s: e.tensor_scalar(
                    out=R[:, 2048 * s:2048 * (s + 1)],
                    in0=R[:, 2048 * s:2048 * (s + 1)],
                    scalar1=MV[:, 0:1], scalar2=RSTD[:, :],
                    op0=ALU.subtract, op1=ALU.mult))
                S.inc("vector", SVEC)
            if rebuild_xt:
                S.wait("tensor", SVEC)
                return xt_rebuild()
            return SVEC.n

        def ffn_phase(l, deq_ag_done, hook=None):
            WG0 = [BFA[:, 2048 * i:2048 * (i + 1)] for i in range(2)]
            WG1 = [BFA[:, 4096 + 2048 * i:4096 + 2048 * (i + 1)] for i in range(2)]
            SST2B = SILB[:, :]
            if l == 0:
                wg_src, wu_src, wd_src = tern_ag["g"], tern_ag["u"], tern_ag["d"]
                etot = 8192
                ag_gu = max(tern_ag_done["g"], tern_ag_done["u"])
                ag_d = tern_ag_done["d"]
            else:
                wg_src, wu_src, wd_src = deq_ag["g"], deq_ag["u"], deq_ag["d"]
                etot = 16384
                ag_gu = ag_d = deq_ag_done
            n_ec = etot // 2048
            rw = 1024 if l == 0 else 2048
            gsrc3 = wg_src.rearrange("r (dc p) e -> r p dc e", p=128)
            usrc3 = wu_src.rearrange("r (dc p) e -> r p dc e", p=128)
            wdr = wd_src.rearrange("r (ec p) d -> r ec p d", p=128)
            wg_used = [STEN.n, STEN.n]
            down_done = [STEN.n]

            S.wait("tensor", SVEC)
            S.wait("tensor", SSCA)
            S.wait("vector", SOUT)
            S.wait("scalar", STSTG)
            S.wait("scalar", SDQO)
            S.wait("scalar", SWCST)
            S.wait("scalar", SGD)

            def wdnr_ap(es):
                if es < 6:
                    return BFA[:, 1024 * es:1024 * (es + 1)]
                return BFA[:, 8192 + 1024 * (es - 6):8192 + 1024 * (es - 5)]

            PSR = [PS0, PS1, PS2]
            ring_evac = [SVEC.n] * 3
            silb_evac = [SVEC.n] * 2
            rk = [0]
            for ec in range(n_ec):
                S.wait("scalar", SCC, ag_gu)
                S.wait("scalar", SCC, ag_d)
                S.wait("scalar", STEN, down_done[0])
                for es in range(6, 16):
                    e0 = 2048 * ec + 128 * es
                    r, erow = e0 // rw, (e0 % rw) // 128
                    dma("scalar", SWX[4],
                        wdnr_ap(es), wdr[r, erow, :, 0:1024])
                for es in range(16):
                    e0 = 2048 * ec + 128 * es
                    r, c0 = e0 // rw, e0 % rw
                    slot = es % 2
                    S.wait("scalar", STEN, max(wg_used[slot], down_done[0]))
                    dma("scalar", SWX[slot],
                        WG0[slot][:, :].rearrange("p (dc e) -> p dc e", e=128),
                        gsrc3[r, :, :, c0:c0 + 128])
                    dma("scalar", SWX[2 + slot],
                        WG1[slot][:, :].rearrange("p (dc e) -> p dc e", e=128),
                        usrc3[r, :, :, c0:c0 + 128])
                    S.wait("tensor", SWX[slot])
                    S.wait("tensor", SWX[2 + slot])
                    if es == 0:
                        S.wait("tensor", SVEC)
                    for th in range(2):
                        k = rk[0] % 3
                        rk[0] += 1
                        PSX = PSR[k]
                        S.wait("tensor", SVEC, ring_evac[k])
                        for dc in range(16):
                            S.emit("tensor", lambda e, slot=slot, dc=dc, th=th,
                                   PSX=PSX: e.matmul(
                                PSX[:, 0:512],
                                WG0[slot][:, 128 * dc:128 * (dc + 1)],
                                XZ[:, 1024 * dc + 512 * th:
                                   1024 * dc + 512 * (th + 1)],
                                start=(dc == 0), stop=(dc == 15)))
                        for dc in range(16):
                            S.emit("tensor", lambda e, slot=slot, dc=dc, th=th,
                                   PSX=PSX: e.matmul(
                                PSX[:, 512:1024],
                                WG1[slot][:, 128 * dc:128 * (dc + 1)],
                                XZ[:, 1024 * dc + 512 * th:
                                   1024 * dc + 512 * (th + 1)],
                                start=(dc == 0), stop=(dc == 15)))
                        v_mm = S.inc("tensor", STEN)
                        if th == 1:
                            wg_used[slot] = v_mm
                        S.wait("scalar", STEN, v_mm)
                        S.wait("scalar", SVEC, silb_evac[th])
                        if l == 0:
                            S.emit("scalar", lambda e, PSX=PSX, th=th:
                                   e.activation(
                                out=SST2B[:, 512 * th:512 * (th + 1)],
                                in_=PSX[:, 512:1024], func=AF.Silu))
                            gsrc_ps = 0
                        else:
                            S.emit("scalar", lambda e, PSX=PSX, th=th:
                                   e.activation(
                                out=SST2B[:, 512 * th:512 * (th + 1)],
                                in_=PSX[:, 0:512], func=AF.Gelu_apprx_tanh))
                            gsrc_ps = 512
                        vs = SSCA.n
                        S.wait("vector", SSCA, vs)
                        S.emit("vector", lambda e, es=es, th=th, PSX=PSX,
                               gsrc_ps=gsrc_ps: e.scalar_tensor_tensor(
                            out=QA[:, 1024 * es + 512 * th:
                                   1024 * es + 512 * (th + 1)],
                            in0=PSX[:, gsrc_ps:gsrc_ps + 512],
                            scalar=1.0, in1=SST2B[:, 512 * th:512 * (th + 1)],
                            op0=ALU.mult, op1=ALU.mult))
                        ring_evac[k] = SVEC.n
                        silb_evac[th] = SVEC.n
                    if hook is not None:
                        hook()
                v_hid = SVEC.n

                for dh in range(2):
                    S.wait("scalar", STEN)
                    S.wait("scalar", SVEC)
                    es_list = range(6) if dh == 0 else range(16)
                    for es in es_list:
                        e0 = 2048 * ec + 128 * es
                        r, erow = e0 // rw, (e0 % rw) // 128
                        dma("scalar", SWX[4],
                            wdnr_ap(es),
                            wdr[r, erow, :, 1024 * dh:1024 * (dh + 1)])
                    v_wd = SWX[4].n
                    S.wait("tensor", SWX[4], v_wd)
                    S.wait("tensor", SVEC, v_hid)
                    for s in range(NQS):
                        k = rk[0] % 3
                        rk[0] += 1
                        PSX = PSR[k]
                        S.wait("tensor", SVEC, ring_evac[k])
                        for es in range(16):
                            for th in range(2):
                                S.emit("tensor", lambda e, PSX=PSX, es=es,
                                       th=th, s=s, ap=wdnr_ap(es): e.matmul(
                                    PSX[:, 512 * th:512 * (th + 1)],
                                    QA[:, 1024 * es + 128 * s:
                                       1024 * es + 128 * (s + 1)],
                                    ap[:, 512 * th:512 * (th + 1)],
                                    start=(es == 0), stop=(es == 15)))
                        v_mm = S.inc("tensor", STEN)
                        down_done[0] = v_mm
                        S.wait("vector", STEN, v_mm)
                        S.emit("vector", lambda e, PSX=PSX, s=s, dh=dh:
                               e.scalar_tensor_tensor(
                            out=R[:, 2048 * s + 1024 * dh:
                                  2048 * s + 1024 * (dh + 1)],
                            in0=PSX[:, :], scalar=1.0,
                            in1=R[:, 2048 * s + 1024 * dh:
                                  2048 * s + 1024 * (dh + 1)],
                            op0=ALU.mult, op1=ALU.add))
                        ring_evac[k] = S.inc("vector", SVEC) or SVEC.n

        S.wait("tensor", SVEC, v_xt)
        deq_ag_done = 0
        tern_ag_done = {"u": 0, "g": 0, "d": 0}
        v_fin = SVEC.n
        if stop_after >= 1:
            kvd0 = qkv_phase(0, hook=lambda: tern_drv.step(2))
            tern_drv.flush()
        if stop_after >= 2:
            agd0 = kv_allgather(0, kvd0)
            for m in TMAT:
                S.wait("gpsimd", STSTG, tern_done[m][0])
                S.wait("gpsimd", SWCST, tern_done[m][1])
                S.emit("gpsimd", lambda e, m=m: e.collective_compute(
                    "AllGather", ALU.bypass, ins=[tern_c[m][:, :]],
                    outs=[tern_ag[m][:, :, :]], replica_groups=ALL8),
                       auto=False)
                tern_ag_done[m] = S.inc("gpsimd", SCC)
            attn_phase(0, agd0)
        if debug and stop_after >= 2:
            S.wait("sync", SVEC)
            dzr = dbg_d["d_z0"].rearrange("(c p) t -> c p t", p=128)
            dqr = dbg_d["d_q0"].rearrange("(c p) t -> c p t", p=128)
            for fc in range(16):
                dma("sync", SOUT, dzr[fc, :, :], XZ[:, 1024 * fc:1024 * (fc + 1)])
                dma("sync", SOUT, dqr[fc, :, :], QA[:, 1024 * fc:1024 * (fc + 1)])
        if stop_after >= 3:
            oproj_phase(0)
            if debug:
                S.wait("sync", SVEC)
                rr = dbg_d["d_R1"].rearrange("(s p) d -> s p d", p=128)
                for s in range(NQS):
                    dma("sync", SOUT, rr[s, :, :], R[:, 2048 * s:2048 * (s + 1)])
            ln_phase(rebuild_xt=True)
        if stop_after >= 4:
            ffn_phase(0, deq_ag_done, hook=lambda: deq_drv.step(2))
            deq_drv.flush()
            S.wait("gpsimd", SDQLD)
            S.wait("gpsimd", SWA)
            S.wait("gpsimd", SVEC)
            S.emit("gpsimd", lambda e: e.iota(
                out=KIOT[:, :], pattern=[[1, 2048]], base=0,
                channel_multiplier=0))
            for m in ("g", "u", "d"):
                S.wait("gpsimd", SDQO, deq_done[m][0])
                S.wait("gpsimd", SGD, deq_done[m][1])
                S.emit("gpsimd", lambda e, m=m: e.collective_compute(
                    "AllGather", ALU.bypass, ins=[deq_c[m][:, :]],
                    outs=[deq_ag[m][:, :, :]], replica_groups=ALL8), auto=False)
                S.inc("gpsimd", SCC)
            deq_ag_done = SCC.n
            if debug:
                S.wait("sync", SVEC)
                rr = dbg_d["d_R2"].rearrange("(s p) d -> s p d", p=128)
                for s in range(NQS):
                    dma("sync", SOUT, rr[s, :, :], R[:, 2048 * s:2048 * (s + 1)])
            ln_phase(rebuild_xt=True)
        if stop_after >= 5:
            kvd1 = qkv_phase(1)
            agd1 = kv_allgather(1, kvd1)
            attn_phase(1, agd1)
        if stop_after >= 6:
            oproj_phase(1)
            if debug:
                S.wait("sync", SVEC)
                rr = dbg_d["d_R3"].rearrange("(s p) d -> s p d", p=128)
                for s in range(NQS):
                    dma("sync", SOUT, rr[s, :, :], R[:, 2048 * s:2048 * (s + 1)])
            ln_phase(rebuild_xt=True)
        if stop_after >= 7:
            ffn_phase(1, deq_ag_done)
            v_fin = ln_phase(rebuild_xt=False)

        S.wait("sync", SVEC, v_fin)
        out_r = out_d.rearrange("(s p) d -> s p d", p=128)
        for s in range(NQS):
            dma("sync", SOUT, out_r[s, :, :], R[:, 2048 * s:2048 * (s + 1)])
        if debug and stop_after >= -2:
            S.wait("sync", SCC, tern_ag_done["u"])
            dma("sync", SOUT, dbg_d["d_ternu"][:, :], tern_ag["u"][0, :, :])
        if debug and stop_after >= 2:
            S.wait("sync", SCC, deq_ag_done)
            dma("sync", SOUT, dbg_d["d_deqg"][:, :], deq_ag["g"][0, :, :])
        S.wait("sync", SOUT, SOUT.n)

        @block.sync
        def _(eng):
            S.run("sync", eng)

        @block.scalar
        def _(eng):
            S.run("scalar", eng)

        @block.vector
        def _(eng):
            S.run("vector", eng)

        @block.tensor
        def _(eng):
            S.run("tensor", eng)

        @block.gpsimd
        def _(eng):
            S.run("gpsimd", eng)

    return nc


def _prep_inputs(inputs):
    f32 = np.float32
    x = np.ascontiguousarray(np.asarray(inputs["L0_attn_x"], dtype=f32)
                             .reshape(B * T, D))
    common = {}
    for l, pre in ((0, "L0_attn_"), (1, "L1_attn_")):
        if l == 0:
            wq = np.asarray(inputs[pre + "wq"], f32)
            wk = np.asarray(inputs[pre + "wk"], f32)
            wv = np.asarray(inputs[pre + "wv"], f32)
            wo = np.asarray(inputs[pre + "wo"], f32)
        else:
            wq = np.asarray(inputs[pre + "W_Q"], f32).reshape(D, D)
            wk = np.asarray(inputs[pre + "W_K"], f32).reshape(D, D)
            wv = np.asarray(inputs[pre + "W_V"], f32).reshape(D, D)
            wo = np.asarray(inputs[pre + "W_O"], f32)
        common[f"L{l}_wqT"] = np.ascontiguousarray(wq.T)
        common[f"L{l}_wkT"] = np.ascontiguousarray(wk.T)
        common[f"L{l}_wvT"] = np.ascontiguousarray(wv.T)
        common[f"L{l}_woT"] = np.ascontiguousarray(wo.T)
    for m in ("q", "k", "v", "o"):
        common[f"L0_b{m}"] = np.ascontiguousarray(
            np.asarray(inputs[f"L0_attn_b{m}"], f32).reshape(16, 128).T)
    common["ident"] = np.eye(128, dtype=ml_dtypes.bfloat16)
    common["identf"] = np.eye(128, dtype=np.float32)

    wupT = np.asarray(inputs["L0_ffn_W_up"], f32).T
    wgT = np.asarray(inputs["L0_ffn_W_gate"], f32).T
    wdT = np.asarray(inputs["L0_ffn_W_down"], f32).T

    pg = np.asarray(inputs["L1_ffn_gate_W_packed"], np.int32)
    pu = np.asarray(inputs["L1_ffn_up_W_packed"], np.int32)
    pd = np.asarray(inputs["L1_ffn_down_W_packed"], np.int32)
    sg = np.asarray(inputs["L1_ffn_gate_scales"], f32)
    su = np.asarray(inputs["L1_ffn_up_scales"], f32)
    sd = np.asarray(inputs["L1_ffn_down_scales"], f32)

    maps = []
    for c in range(NCORE):
        mp = dict(common)
        mp["x"] = np.ascontiguousarray(x[c * TOK:(c + 1) * TOK])
        qi = ((c % 2) * 1024 + np.arange(TOK, dtype=f32))
        mp["qidx"] = np.ascontiguousarray(qi.reshape(NQS, 128).T)
        mp["tu_sl"] = np.ascontiguousarray(wupT[:, c * 1024:(c + 1) * 1024])
        mp["tg_sl"] = np.ascontiguousarray(wgT[:, c * 1024:(c + 1) * 1024])
        mp["td_sl"] = np.ascontiguousarray(wdT[c * 1024:(c + 1) * 1024, :])
        mp["pkg_sl"] = np.ascontiguousarray(pg[c * 2048:(c + 1) * 2048, :].T)
        mp["pku_sl"] = np.ascontiguousarray(pu[c * 2048:(c + 1) * 2048, :].T)
        mp["pkd_sl"] = np.ascontiguousarray(pd.T[c * 1024:(c + 1) * 1024, :])
        mp["sxg_sl"] = np.ascontiguousarray(
            np.repeat(sg[c * 2048:(c + 1) * 2048, :].T, 16, axis=0))
        mp["sxu_sl"] = np.ascontiguousarray(
            np.repeat(su[c * 2048:(c + 1) * 2048, :].T, 16, axis=0))
        mp["sxd_sl"] = np.ascontiguousarray(
            np.repeat(sd.T, 16, axis=0)[c * 1024:(c + 1) * 1024, :])
        maps.append(mp)
    return maps


_cache = {}


def kernel(**inputs):
    maps = _prep_inputs(inputs)
    if "prog" not in _cache:
        _cache["prog"] = build_program(debug=False)
    res = run_bass_kernel_spmd(_cache["prog"], maps, core_ids=list(range(NCORE)))
    out = np.concatenate([r["out"] for r in res.results], axis=0)
    return out.reshape(B, T, D).astype(np.float32)


# version 09
# speedup vs baseline: 1.0793x; 1.0352x over previous
import contextlib
import math

import numpy as np
import ml_dtypes

import concourse.bass as bass
import concourse.mybir as mybir
from concourse.bass_utils import run_bass_kernel_spmd

F32 = mybir.dt.float32
BF16 = mybir.dt.bfloat16
I32 = mybir.dt.int32
AF = mybir.ActivationFunctionType
ALU = mybir.AluOpType

B, T, D = 4, 2048, 2048
NCORE = 8
TOK = 1024
NQS = 8
H = 16
SCALE = 1.0 / math.sqrt(128.0)
NW0 = 2048 * 8192

ENGINES = ("sync", "scalar", "vector", "tensor", "gpsimd")
PAIRS = [[0, 1], [2, 3], [4, 5], [6, 7]]
ALL8 = [[0, 1, 2, 3, 4, 5, 6, 7]]


class Sem:
    def __init__(self, handle, name):
        self.h, self.name, self.n = handle, name, 0


class Sched:

    def __init__(self):
        self.q = {e: [] for e in ENGINES}
        self.last_wait = {e: {} for e in ENGINES}
        self.auto = {}

    def _raw_emit(self, eng, fn):
        self.q[eng].append(fn)

    def emit(self, eng, fn, auto=True):
        asem = self.auto.get(eng)
        if asem is not None and auto:
            self.wait(eng, asem, asem.n)
            asem.n += 1
            self.q[eng].append(
                lambda e, fn=fn, s=asem.h: fn(e).then_inc(s, 1))
        else:
            self.q[eng].append(fn)

    def inc(self, eng, sem, k=1):
        asem = self.auto.get(eng)
        if asem is not None and sem is asem:
            return sem.n
        sem.n += k
        fn = self.q[eng].pop()
        self.q[eng].append(lambda e, fn=fn, s=sem.h, k=k: fn(e).then_inc(s, k))
        return sem.n

    def wait(self, eng, sem, val=None):
        v = sem.n if val is None else val
        if v <= 0:
            return
        lw = self.last_wait[eng].get(sem.name, 0)
        if v <= lw:
            return
        self.last_wait[eng][sem.name] = v
        self.q[eng].append(lambda e, v=v, s=sem.h: e.wait_ge(s, v))

    def run(self, eng_name, eng):
        for fn in self.q[eng_name]:
            fn(eng)


def build_program(debug=False, stop_after=99):
    nc = bass.Bass("TRN2", target_bir_lowering=False)

    def din(name, shape, dtype=F32):
        return nc.dram_tensor(name, shape, dtype, kind="ExternalInput").ap()

    x_in = din("x", [TOK, D])
    qidx_in = din("qidx", [128, NQS])
    ident_in = din("ident", [128, 128], BF16)
    identf_in = din("identf", [128, 128], F32)
    wT = {(l, m): din(f"L{l}_{m}T", [D, D])
          for l in (0, 1) for m in ("wq", "wk", "wv", "wo")}
    bias_in = {m: din(f"L0_b{m}", [128, 16]) for m in ("q", "k", "v", "o")}
    tern_sl = {m: din(f"t{m}_sl", [D, 1024] if m != "d" else [1024, D])
               for m in ("u", "g", "d")}
    pk_sl = {m: din(f"pk{m}_sl", [1024, 2048], I32) for m in ("g", "u", "d")}
    sx_sl = {m: din(f"sx{m}_sl", [1024, 2048]) for m in ("g", "u", "d")}
    out_d = nc.dram_tensor("out", [TOK, D], F32, kind="ExternalOutput").ap()

    dbg_d = {}
    if debug:
        for nm, shape, dt in (
            ("d_R1", [TOK, D], F32), ("d_R2", [TOK, D], F32),
            ("d_R3", [TOK, D], F32), ("d_scl", [1, 8], F32),
            ("d_ternu", [D, 1024], BF16), ("d_deqg", [2048, 2048], BF16),
            ("d_q0", [D, TOK], BF16), ("d_z0", [D, TOK], BF16),
        ):
            dbg_d[nm] = nc.dram_tensor(nm, shape, dt, kind="ExternalOutput").ap()

    def idram(name, shape, dtype, shared=False):
        return nc.dram_tensor(name, shape, dtype, kind="Internal",
                              addr_space="Local").ap()

    kv_c = {(l, m): idram(f"kv{l}{m}_c", [D, TOK], BF16)
            for l in (0, 1) for m in ("k", "v")}
    kv_ag = {(l, m): idram(f"kv{l}{m}_ag", [2, D, TOK], BF16)
             for l in (0, 1) for m in ("k", "v")}
    tern_c = {m: idram(f"tern{m}_c", [D, 1024] if m != "d" else [1024, D], BF16)
              for m in ("u", "g", "d")}
    tern_ag = {m: idram(f"tern{m}_ag",
                        [8, D, 1024] if m != "d" else [8, 1024, D], BF16,
                        shared=True) for m in ("u", "g", "d")}
    deq_c = {m: idram(f"deq{m}_c", [2048, 2048], BF16) for m in ("g", "u", "d")}
    deq_ag = {m: idram(f"deq{m}_ag", [8, 2048, 2048], BF16, shared=True)
              for m in ("g", "u", "d")}
    ar_i = idram("ar_i", [1, 64], F32)
    ar_o = idram("ar_o", [1, 64], F32, shared=True)

    stack = contextlib.ExitStack()
    with stack:
        def sbuf(name, shape, dtype):
            return stack.enter_context(nc.sbuf_tensor(name, shape, dtype))

        def psum(name, shape, dtype=F32):
            return stack.enter_context(nc.psum_tensor(name, shape, dtype))

        R = sbuf("R", [128, 8 * 2048], F32)
        XZ = sbuf("XZ", [128, 16 * 1024], BF16)
        QA = sbuf("QA", [128, 16 * 1024], BF16)
        BFA = sbuf("BFA", [128, 18432], BF16)
        SILB = sbuf("SILB", [128, 1024], BF16)
        PTB = SILB[:, 0:512]
        DQF = sbuf("DQF", [128, 4096], BF16)
        SCR = sbuf("SCR", [128, 2048], F32)
        PRF = sbuf("PRF", [128, 2048], F32)
        DQV = sbuf("DQV", [128, 2048], I32)
        KIOT = DQV
        DQNI = sbuf("DQNI", [128, 512], I32)
        DQUI = sbuf("DQUI", [128, 512], I32)
        IDE = sbuf("IDE", [128, 128], BF16)
        IDEF = sbuf("IDEF", [128, 128], F32)
        QIDX = sbuf("QIDX", [128, NQS], F32)
        BIA = sbuf("BIA", [128, 64], F32)
        BQS = sbuf("BQS", [128, 16], F32)
        ZBI = sbuf("ZBI", [128, 1], F32)
        CONST = sbuf("CONST", [128, 8], F32)
        MV = sbuf("MV", [128, 2], F32)
        BNS = sbuf("BNS", [128, 24], F32)
        RSTD = sbuf("RSTD", [128, 1], F32)
        NMX = sbuf("NMX", [128, 1], F32)
        MXV = sbuf("MXV", [128, 1], F32)
        LS0 = sbuf("LS0", [128, 1], F32)
        LS1 = sbuf("LS1", [128, 1], F32)
        RLS = sbuf("RLS", [128, 1], F32)
        ACC = sbuf("ACC", [128, 4], F32)
        SCL = sbuf("SCL", [128, 8], F32)
        THR = sbuf("THR", [128, 8], F32)
        ONE1 = sbuf("ONE1", [128, 1], F32)
        CM = sbuf("CM", [128, 8], F32)
        LSA = sbuf("LSA", [128, 1], F32)
        LSB = sbuf("LSB", [128, 1], F32)
        ONER = sbuf("ONER", [1, 128], F32)

        PS0 = psum("PS0", [128, 1024])
        PS1 = psum("PS1", [128, 1024])
        PS2 = psum("PS2", [128, 1024])
        PS3f = psum("PS3f", [128, 512])
        PS3b = psum("PS3b", [128, 512], BF16)

        semnames = ["svec", "ssca", "sten", "scc", "sgp",
                    "sph0", "swa", "swb", "swcst", "ststg", "sdqld", "sdqo",
                    "skv0", "skv1", "sout", "sgd"] + [f"swx{i}" for i in range(8)]
        sems = {nm: Sem(stack.enter_context(nc.semaphore(nm)), nm)
                for nm in semnames}
        SVEC, SSCA, STEN = sems["svec"], sems["ssca"], sems["sten"]
        SCC, SGP = sems["scc"], sems["sgp"]
        SPH0, SWA, SWB = sems["sph0"], sems["swa"], sems["swb"]
        SWCST, STSTG = sems["swcst"], sems["ststg"]
        SDQLD, SDQO = sems["sdqld"], sems["sdqo"]
        SKV = [sems["skv0"], sems["skv1"]]
        SOUT, SGD = sems["sout"], sems["sgd"]
        SWX = [sems[f"swx{i}"] for i in range(8)]

        block = stack.enter_context(nc.Block())
        S = Sched()
        S.auto = {"vector": SVEC, "scalar": SSCA, "gpsimd": SGP}

        def dma(q, sem, out, in_):
            S._raw_emit(q, lambda e, o=out, i=in_: e.dma_start(out=o, in_=i))
            sem.n += 16
            fn = S.q[q].pop()
            S.q[q].append(lambda e, fn=fn, s=sem.h: fn(e).then_inc(s, 16))
            return sem.n

        dma("sync", SPH0, QIDX[:, :], qidx_in[:, :])
        dma("sync", SPH0, IDE[:, :], ident_in[:, :])
        dma("sync", SPH0, IDEF[:, :], identf_in[:, :])
        for j, m in enumerate(("q", "k", "v", "o")):
            dma("sync", SPH0, BIA[:, 16 * j:16 * (j + 1)], bias_in[m][:, :])
        x_r = x_in.rearrange("(s p) d -> s p d", p=128)
        for s in range(NQS):
            dma("sync", SPH0, R[:, 2048 * s:2048 * (s + 1)], x_r[s, :, :])
        v_ld0 = SPH0.n

        S.emit("gpsimd", lambda e: e.iota(out=KIOT[:, :], pattern=[[1, 2048]],
                                          base=0, channel_multiplier=0))
        S.inc("gpsimd", SGP)
        for _ms in [lambda e: e.memset(CONST[:, 0:1], 1e-5),
                    lambda e: e.memset(CONST[:, 1:2], 7.5),
                    lambda e: e.memset(CONST[:, 2:3], -4.0),
                    lambda e: e.memset(CONST[:, 3:4], -12.0),
                    lambda e: e.memset(ONE1[:, :], 1.0),
                    lambda e: e.memset(ZBI[:, :], 0.0),
                    lambda e: e.memset(ACC[:, :], 0.0)]:
            S.emit("gpsimd", _ms)
            S.inc("gpsimd", SGP)
        v_gp0 = SGP.n

        S.wait("vector", SPH0, v_ld0)
        S.emit("vector", lambda e: e.tensor_scalar(
            out=BQS[:, :], in0=BIA[:, 0:16], scalar1=SCALE, scalar2=None,
            op0=ALU.mult))
        S.inc("vector", SVEC)

        def xt_rebuild():
            for s in range(NQS):
                for dc in range(16):
                    idx = s * 16 + dc
                    slot = idx % 2
                    if idx >= 2 or idx == 0:
                        S.wait("tensor", SVEC)
                    S.emit("tensor", lambda e, s=s, dc=dc, slot=slot: e.transpose(
                        PS3f[:, 128 * slot:128 * (slot + 1)],
                        R[:, 2048 * s + 128 * dc:2048 * s + 128 * (dc + 1)],
                        IDEF[:, :]))
                    tv = S.inc("tensor", STEN)
                    S.wait("vector", STEN, tv)
                    S.emit("vector", lambda e, s=s, dc=dc, slot=slot: e.tensor_copy(
                        out=XZ[:, 1024 * dc + 128 * s:1024 * dc + 128 * (s + 1)],
                        in_=PS3f[:, 128 * slot:128 * (slot + 1)]))
                    S.inc("vector", SVEC)
            return SVEC.n

        S.wait("tensor", SPH0, v_ld0)
        S.wait("tensor", SGP, v_gp0)
        S.emit("tensor", lambda e: e.transpose(
            PS3f[0:1, 0:128], ONE1[:, 0:1], IDEF[:, :]))
        tv0 = S.inc("tensor", STEN)
        S.wait("vector", STEN, tv0)
        S.emit("vector", lambda e: e.tensor_copy(out=ONER[:, :],
                                                 in_=PS3f[0:1, 0:128]))
        v_oner = S.inc("vector", SVEC)
        v_xt = xt_rebuild()

        TMAT = ("u", "g", "d") if stop_after >= -2 else ()
        for mi, m in enumerate(TMAT):
            src = tern_sl[m]
            nrow, ncol = src.tensor.shape
            src_r = src.rearrange("(c p) e -> c p e", p=128)
            for t in range(nrow // 128):
                buf = SCR if t % 2 == 0 else PRF
                sbf = SWA if t % 2 == 0 else SWB
                S.wait("sync", SVEC)
                v = dma("sync", sbf, buf[:, 0:ncol], src_r[t, :, :])
                S.wait("vector", sbf, v)
                S.emit("vector", lambda e, buf=buf, ncol=ncol: e.tensor_reduce(
                    out=MXV[:, :], in_=buf[:, 0:ncol], axis=mybir.AxisListType.X,
                    op=ALU.add, apply_absolute_value=True))
                S.inc("vector", SVEC)
                S.wait("vector", SVEC)
                S.wait("vector", SGP, v_gp0)
                S.emit("vector", lambda e, mi=mi: e.tensor_tensor(
                    out=ACC[:, mi:mi + 1], in0=ACC[:, mi:mi + 1], in1=MXV[:, :],
                    op=ALU.add))
                S.inc("vector", SVEC)
        v_abs = SVEC.n
        S.wait("tensor", SVEC, v_abs)
        tern_done = {}
        tern_ag_done = {"u": 0, "g": 0, "d": 0}
        deq_done = {"g": 0, "u": 0, "d": 0}
        v_thr = SVEC.n
        if stop_after >= -2:
            S.emit("tensor", lambda e: e.matmul(
                PS3f[0:1, 0:4], ONE1[:, :], ACC[:, 0:4], start=True, stop=True))
            vt = S.inc("tensor", STEN)
            S.wait("vector", STEN, vt)
            S.emit("vector", lambda e: e.memset(SCR[0:1, 0:64], 0.0))
            S.inc("vector", SVEC)
            S.wait("vector", SVEC)
            S.emit("vector", lambda e: e.tensor_copy(out=SCR[0:1, 0:4],
                                                     in_=PS3f[0:1, 0:4]))
            v_sums = S.inc("vector", SVEC)
            S.wait("sync", SVEC, v_sums)
            v_ari = dma("sync", SGD, ar_i[:, 0:64], SCR[0:1, 0:64])
            S.wait("gpsimd", SGD, v_ari)
            S.emit("gpsimd", lambda e: e.collective_compute(
                "AllReduce", ALU.add, ins=[ar_i[:, :]], outs=[ar_o[:, :]],
                replica_groups=ALL8), auto=False)
            v_ar = S.inc("gpsimd", SCC)
            S.wait("sync", SCC, v_ar)
            v_aro = dma("sync", SGD, SCR[0:1, 0:64], ar_o[:, :])
            S.wait("tensor", SGD, v_aro)
            S.wait("tensor", SVEC, v_oner)
            S.emit("tensor", lambda e: e.matmul(
                PS3f[:, 128:136], ONER[:, :], SCR[0:1, 0:8],
                start=True, stop=True))
            v_bcm = S.inc("tensor", STEN)
            S.wait("vector", STEN, v_bcm)
            S.emit("vector", lambda e: e.tensor_copy(out=SCL[:, 0:8],
                                                     in_=PS3f[:, 128:136]))
            S.inc("vector", SVEC)
            S.emit("vector", lambda e: e.tensor_scalar(
                out=THR[:, :], in0=SCL[:, 0:8], scalar1=0.5 / NW0,
                scalar2=0.5e-8, op0=ALU.mult, op1=ALU.add))
            S.emit("vector", lambda e: e.tensor_scalar(
                out=SCL[:, 0:8], in0=SCL[:, 0:8], scalar1=1.0 / NW0,
                scalar2=1e-8, op0=ALU.mult, op1=ALU.add))
            v_thr = S.inc("vector", SVEC)
            if debug:
                S.wait("sync", SVEC, v_thr)
                dma("sync", SOUT, dbg_d["d_scl"][:, :], SCL[0:1, 0:8])

            TSTG2B = [BFA[:, 4096:5120], BFA[:, 9216:10240]]
            SGNB2 = [BFA[:, 5120:6144], BFA[:, 6144:7168]]
            tern_done = {}
            tern_units = []

            def _tern_unit(mi, m, t, half, is_last):
                src_t = tern_sl[m]
                nrow, ncol = src_t.tensor.shape
                src_r = src_t.rearrange("(c p) e -> c p e", p=128)
                dst_r = tern_c[m].rearrange("(c p) e -> c p e", p=128)
                c0 = 1024 * half
                stv = [0]

                def load(pu):
                    stage = PRF[:, 1024 * pu:1024 * (pu + 1)]
                    S.wait("scalar", SVEC)
                    stv[0] = dma("scalar", SWB if pu == 0 else SWA,
                                 stage, src_r[t, :, c0:c0 + 1024])

                def compute(pu):
                    stage = PRF[:, 1024 * pu:1024 * (pu + 1)]
                    sgnb = SGNB2[pu]
                    S.wait("scalar", SWB if pu == 0 else SWA, stv[0])
                    S.wait("scalar", SVEC, v_thr)
                    S.emit("scalar", lambda e, sgnb=sgnb, stage=stage: e.activation(
                        out=sgnb[:, :], in_=stage, func=AF.Sign))
                    S.emit("scalar", lambda e, stage=stage: e.activation(
                        out=stage, in_=stage, func=AF.Abs))
                    vs = SSCA.n
                    S.wait("vector", SSCA, vs)
                    S.emit("vector", lambda e, mi=mi, stage=stage: e.tensor_scalar(
                        out=stage, in0=stage,
                        scalar1=THR[:, mi:mi + 1], scalar2=None, op0=ALU.is_ge))
                    tsem = STSTG if pu == 0 else SWCST
                    S.wait("vector", tsem)
                    S.emit("vector", lambda e, mi=mi, sgnb=sgnb, stage=stage,
                           pu=pu: e.scalar_tensor_tensor(
                        out=TSTG2B[pu][:, :], in0=sgnb[:, :],
                        scalar=SCL[:, mi:mi + 1], in1=stage,
                        op0=ALU.mult, op1=ALU.mult))
                    vv = SVEC.n
                    S.wait("scalar", SVEC, vv)
                    dma("scalar", tsem, dst_r[t, :, c0:c0 + 1024],
                        TSTG2B[pu][:, :])
                    if is_last:
                        tern_done[m] = (STSTG.n, SWCST.n)
                return (load, compute)

            for mi, m in enumerate(TMAT):
                nrow, ncol = tern_sl[m].tensor.shape
                nt, nh = nrow // 128, ncol // 1024
                cnt = 0
                for t in range(nt):
                    for half in range(nh):
                        cnt += 1
                        tern_units.append(
                            _tern_unit(mi, m, t, half, cnt == nt * nh))

            DQN2 = SCR[:, 0:512]
            DQA2 = SCR[:, 512:1024]
            DQB2 = SCR[:, 1024:1536]
            DQG2 = SCR[:, 1536:2048]
            deq_done = {}
            deq_units = []

            def _deq_unit(m, jc, ch, is_last):
                pk = pk_sl[m].rearrange("(c p) n -> c p n", p=128)
                sx = sx_sl[m].rearrange("(c p) n -> c p n", p=128)
                dst_r = deq_c[m].rearrange("(c p two) n -> c two p n",
                                           p=128, two=2)
                c0 = 1024 * ch
                ldv = [0]

                def load(pu):
                    dqv = DQV[:, 1024 * pu:1024 * (pu + 1)]
                    dqs = PRF[:, 1024 * pu:1024 * (pu + 1)]
                    sem = SDQLD if pu == 0 else SWA
                    S.wait("sync", SVEC)
                    dma("sync", sem, dqv, pk[jc, :, c0:c0 + 1024])
                    ldv[0] = dma("sync", sem, dqs, sx[jc, :, c0:c0 + 1024])

                def compute(pu):
                    dqv = DQV[:, 1024 * pu:1024 * (pu + 1)]
                    dqs = PRF[:, 1024 * pu:1024 * (pu + 1)]
                    S.wait("vector", SDQLD if pu == 0 else SWA, ldv[0])
                    vb_prev = [0, 0]
                    for t4 in range(4):
                        sub, hi = t4 % 2, t4 // 2
                        s0 = 512 * sub
                        st = t4 % 2
                        b0 = 2048 * st
                        S.wait("vector", SSCA, vb_prev[0])
                        if hi == 0:
                            S.emit("vector", lambda e, s0=s0, dqv=dqv:
                                   e.tensor_scalar(
                                out=DQNI[:, :], in0=dqv[:, s0:s0 + 512],
                                scalar1=15, scalar2=None,
                                op0=ALU.bitwise_and))
                        else:
                            S.emit("vector", lambda e, s0=s0, dqv=dqv:
                                   e.tensor_scalar(
                                out=DQNI[:, :], in0=dqv[:, s0:s0 + 512],
                                scalar1=4, scalar2=None,
                                op0=ALU.logical_shift_right))
                        S.emit("vector", lambda e: e.tensor_scalar(
                            out=DQUI[:, :], in0=DQNI[:, :],
                            scalar1=7, scalar2=None, op0=ALU.bitwise_and))
                        v_a = SVEC.n
                        S.wait("scalar", SVEC, v_a)
                        S.wait("scalar", SDQO if t4 % 2 == 0 else SGD)
                        S.emit("scalar", lambda e, b0=b0: e.activation(
                            out=DQF[:, b0:b0 + 512], in_=DQNI[:, :],
                            func=AF.Sign, bias=CONST[:, 1:2], scale=-1.0))
                        S.emit("scalar", lambda e, b0=b0: e.activation(
                            out=DQF[:, b0 + 512:b0 + 1024], in_=DQUI[:, :],
                            func=AF.Identity))
                        S.emit("scalar", lambda e, b0=b0: e.activation(
                            out=DQF[:, b0 + 1024:b0 + 1536], in_=DQUI[:, :],
                            func=AF.Relu, bias=CONST[:, 2:3], scale=1.0))
                        S.emit("scalar", lambda e, b0=b0: e.activation(
                            out=DQF[:, b0 + 1536:b0 + 2048], in_=DQUI[:, :],
                            func=AF.Relu, bias=CONST[:, 3:4], scale=2.0))
                        vb_prev[0] = vb_prev[1]
                        vb_prev[1] = SSCA.n
                        vb = SSCA.n
                        S.wait("vector", SSCA, vb)
                        S.emit("vector", lambda e, b0=b0: e.tensor_tensor(
                            out=DQF[:, b0 + 1024:b0 + 1536],
                            in0=DQF[:, b0 + 512:b0 + 1024],
                            in1=DQF[:, b0 + 1024:b0 + 1536], op=ALU.add))
                        S.emit("vector", lambda e, b0=b0: e.tensor_tensor(
                            out=DQF[:, b0 + 1024:b0 + 1536],
                            in0=DQF[:, b0 + 1024:b0 + 1536],
                            in1=DQF[:, b0 + 1536:b0 + 2048], op=ALU.add))
                        S.emit("vector", lambda e, b0=b0:
                               e.scalar_tensor_tensor(
                            out=DQF[:, b0 + 1024:b0 + 1536],
                            in0=DQF[:, b0 + 1024:b0 + 1536], scalar=0.5,
                            in1=DQF[:, b0:b0 + 512],
                            op0=ALU.mult, op1=ALU.mult))
                        dsem = SDQO if t4 % 2 == 0 else SGD
                        S.emit("vector", lambda e, b0=b0,
                               dqs=dqs, s0=s0: e.tensor_tensor(
                            out=DQF[:, b0 + 1536:b0 + 2048],
                            in0=DQF[:, b0 + 1024:b0 + 1536],
                            in1=dqs[:, s0:s0 + 512], op=ALU.mult))
                        vv2 = SVEC.n
                        S.wait("sync", SVEC, vv2)
                        dma("sync", dsem,
                            dst_r[jc, hi, :, c0 + s0:c0 + s0 + 512],
                            DQF[:, b0 + 1536:b0 + 2048])
                    if is_last:
                        deq_done[m] = (SDQO.n, SGD.n)
                return (load, compute)

            for m in ("g", "u", "d"):
                for jc in range(8):
                    for ch in range(2):
                        deq_units.append(_deq_unit(
                            m, jc, ch, jc == 7 and ch == 1))

        class UnitDriver:
            def __init__(self, units):
                self.units = units
                self.li = 0
                self.ci = 0

            def step(self, k=1):
                for _ in range(k):
                    if self.li < len(self.units):
                        self.units[self.li][0](self.li % 2)
                        self.li += 1
                    if self.ci < self.li - 1 or (self.li == len(self.units)
                                                 and self.ci < self.li):
                        self.units[self.ci][1](self.ci % 2)
                        self.ci += 1

            def flush(self):
                while self.ci < len(self.units):
                    self.step()

        tern_drv = UnitDriver(tern_units if stop_after >= -2 else [])
        deq_drv = UnitDriver(deq_units if stop_after >= -2 else [])
        if stop_after < 1:
            tern_drv.flush()
        if stop_after < 4:
            deq_drv.flush()

        def qkv_phase(l, hook=None):
            WBSQ = [BFA[:, 1024 * i:1024 * (i + 1)] for i in range(2)]
            WSTQ = [SCR[:, 1024 * i:1024 * (i + 1)] for i in range(2)]
            KVST = [BFA[:, 7168 + 1024 * i:7168 + 1024 * (i + 1)] for i in range(2)]
            kv_done = {}
            S.wait("sync", STEN)
            S.wait("vector", SOUT)
            S.wait("scalar", STSTG)
            S.wait("scalar", SWCST)
            S.wait("scalar", SDQO)
            S.wait("scalar", SGD)
            S.wait("tensor", SVEC)
            S.wait("tensor", SSCA)
            wst_used = [SVEC.n] * 2
            wbs_used = [STEN.n] * 2
            PSR = [PS0, PS1, PS2]
            ring_evac = [SSCA.n] * 3
            rk = [0]
            for mat in ("wk", "wv", "wq"):
                src = wT[(l, mat)].rearrange("(dc p) o -> p dc o", p=128)
                for og in range(8):
                    k0, k1 = rk[0] % 3, (rk[0] + 1) % 3
                    rk[0] += 2
                    PSA, PSB = PSR[k0], PSR[k1]
                    for qt in range(4):
                        slot = qt % 2
                        S.wait("sync", SVEC, wst_used[slot])
                        v = dma("sync", SWX[slot], WSTQ[slot].rearrange(
                            "p (dc o) -> p dc o", o=256),
                            src[:, 4 * qt:4 * qt + 4, 256 * og:256 * (og + 1)])
                        S.wait("vector", SWX[slot], v)
                        S.wait("vector", STEN, wbs_used[slot])
                        S.emit("vector", lambda e, slot=slot: e.tensor_copy(
                            out=WBSQ[slot], in_=WSTQ[slot]))
                        wst_used[slot] = SVEC.n
                        v_cast = SVEC.n
                        S.wait("tensor", SVEC, v_cast)
                        if qt == 0:
                            S.wait("tensor", SSCA, ring_evac[k0])
                            S.wait("tensor", SSCA, ring_evac[k1])
                        for dcc in range(4):
                            for oc in range(2):
                                PSX = PSA if oc == 0 else PSB
                                for th in range(2):
                                    dc = 4 * qt + dcc
                                    S.emit("tensor", lambda e, slot=slot,
                                           dcc=dcc, oc=oc, th=th, dc=dc,
                                           PSX=PSX: e.matmul(
                                        PSX[:, 512 * th:512 * (th + 1)],
                                        WBSQ[slot][:, 256 * dcc + 128 * oc:
                                                   256 * dcc + 128 * (oc + 1)],
                                        XZ[:, 1024 * dc + 512 * th:
                                           1024 * dc + 512 * (th + 1)],
                                        start=(dc == 0), stop=(dc == 15)))
                        wbs_used[slot] = S.inc("tensor", STEN)
                    v_mm = STEN.n
                    S.wait("scalar", STEN, v_mm)
                    for oc in range(2):
                        PSX = PSA if oc == 0 else PSB
                        occ = og * 2 + oc
                        if mat == "wq":
                            bias_ap = BQS[:, occ:occ + 1] if l == 0 else ZBI[:, :]
                            S.emit("scalar", lambda e, PSX=PSX, occ=occ,
                                   bias_ap=bias_ap: e.activation(
                                out=QA[:, 1024 * occ:1024 * (occ + 1)],
                                in_=PSX[:, :], func=AF.Identity,
                                bias=bias_ap, scale=SCALE))
                        else:
                            j = 1 if mat == "wk" else 2
                            bias_ap = BIA[:, 16 * j + occ:16 * j + occ + 1] \
                                if l == 0 else ZBI[:, :]
                            kslot = occ % 2
                            S.wait("scalar", SKV[kslot])
                            S.emit("scalar", lambda e, PSX=PSX, kslot=kslot,
                                   bias_ap=bias_ap: e.activation(
                                out=KVST[kslot], in_=PSX[:, :],
                                func=AF.Identity, bias=bias_ap, scale=1.0))
                            vs = SSCA.n
                            S.wait("sync", SSCA, vs)
                            dstm = kv_c[(l, "k" if mat == "wk" else "v")]
                            S._raw_emit("sync", lambda e, dstm=dstm, occ=occ,
                                        kslot=kslot: e.dma_start(
                                out=dstm[128 * occ:128 * (occ + 1), :],
                                in_=KVST[kslot]))
                            S.inc("sync", SKV[kslot], 16)
                    ring_evac[k0] = SSCA.n
                    ring_evac[k1] = SSCA.n
                    if hook is not None:
                        hook()
                if mat in ("wk", "wv"):
                    kv_done[mat] = (SKV[0].n, SKV[1].n)
            return kv_done

        def kv_allgather(l, kv_done):
            out = {}
            for m, key in (("k", "wk"), ("v", "wv")):
                S.wait("gpsimd", SKV[0], kv_done[key][0])
                S.wait("gpsimd", SKV[1], kv_done[key][1])
                S.emit("gpsimd", lambda e, l=l, m=m: e.collective_compute(
                    "AllGather", ALU.bypass, ins=[kv_c[(l, m)][:, :]],
                    outs=[kv_ag[(l, m)][:, :, :]], replica_groups=PAIRS), auto=False)
                out[m] = S.inc("gpsimd", SCC)
            return out

        def attn_phase(l, ag_done):
            KH = [BFA[:, 2048 * i:2048 * (i + 1)] for i in range(2)]
            VST = BFA[:, 4096:6144]
            VT0 = BFA[:, 6144:8192]
            VT = [VT0, VT0]
            PRB = [BFA[:, 8192 + 2048 * i:8192 + 2048 * (i + 1)]
                   for i in range(2)]
            SCB = [SCR, PRF]
            MSKF = [BFA[:, 12288 + 256 * s:12288 + 256 * s + 128]
                    for s in range(NQS)]
            MSKL = [BFA[:, 12416 + 256 * s:12416 + 256 * s + 128]
                    for s in range(NQS)]

            S.wait("vector", SPH0, v_ld0)
            S.wait("vector", SGP)
            S.wait("vector", SKV[0])
            S.wait("vector", SKV[1])
            for s in range(NQS):
                S.emit("vector", lambda e, s=s: e.tensor_scalar(
                    out=MSKF[s][:, :], in0=KIOT[:, 128 * s:128 * (s + 1)],
                    scalar1=QIDX[:, s:s + 1], scalar2=None, op0=ALU.is_le))
                S.emit("vector", lambda e, s=s: e.tensor_scalar(
                    out=MSKL[s][:, :],
                    in0=KIOT[:, 128 * (s + 8):128 * (s + 9)],
                    scalar1=QIDX[:, s:s + 1], scalar2=None, op0=ALU.is_le))
                S.emit("vector", lambda e, s=s: e.tensor_scalar(
                    out=CM[:, s:s + 1], in0=QIDX[:, s:s + 1],
                    scalar1=float(128 * (s + 8) - 1), scalar2=None,
                    op0=ALU.is_ge))
            v_msk = S.inc("vector", SVEC)

            kag, vag = kv_ag[(l, "k")], kv_ag[(l, "v")]
            kh_ready = [0, 0]
            kh_used = [STEN.n, STEN.n]
            vst_ready = [0]
            vst_used = [STEN.n]

            def load_k(h):
                slot = h % 2
                S.wait("scalar", STEN, kh_used[slot])
                S.wait("scalar", SCC, ag_done["k"])
                for r in range(2):
                    dma("scalar", SWX[5 + slot],
                        KH[slot][:, 1024 * r:1024 * (r + 1)],
                        kag[r, 128 * h:128 * (h + 1), :])
                kh_ready[slot] = SWX[5 + slot].n

            def load_v(h):
                S.wait("scalar", STEN, vst_used[0])
                S.wait("scalar", SCC, ag_done["v"])
                for r in range(2):
                    dma("scalar", SWX[7], VST[:, 1024 * r:1024 * (r + 1)],
                        vag[r, 128 * h:128 * (h + 1), :])
                vst_ready[0] = SWX[7].n

            S.wait("scalar", STSTG)
            S.wait("scalar", SWCST)
            S.wait("scalar", SDQO)
            S.wait("scalar", SGD)
            load_k(0)
            load_v(0)
            S.wait("tensor", SSCA)
            for h in range(H):
                slot = h % 2
                S.wait("tensor", SWX[7], vst_ready[0])
                S.wait("tensor", SVEC)
                for g in range(4):
                    for j in range(4 * g, 4 * g + 4):
                        S.emit("tensor", lambda e, j=j: e.transpose(
                            PS3b[:, 128 * (j % 4):128 * (j % 4 + 1)],
                            VST[:, 128 * j:128 * (j + 1)], IDE[:, :]))
                    tv = S.inc("tensor", STEN)
                    S.wait("vector", STEN, tv)
                    S.emit("vector", lambda e, g=g, slot=slot: e.tensor_copy(
                        out=VT[slot][:, 512 * g:512 * (g + 1)],
                        in_=PS3b[:, :]))
                    if g < 3:
                        S.wait("tensor", SVEC)
                vst_used[0] = STEN.n
                if h + 1 < H:
                    load_k(h + 1)
                    load_v(h + 1)
                S.wait("tensor", SWX[5 + slot], kh_ready[slot])

                evac_done = [0] * (NQS + 1)
                nrm_done = [0] * NQS
                exp_done = [0] * NQS
                prb_free = [STEN.n, STEN.n]
                scb_free = [SSCA.n, SSCA.n]

                def emit_scores(s):
                    E = 9 + s
                    cols = 128 * E
                    qtile = QA[:, 1024 * h + 128 * s:1024 * h + 128 * (s + 1)]
                    if s >= 1:
                        S.wait("tensor", SVEC, evac_done[s - 1])
                    for half, PSX in ((0, PS0), (1, PS1)):
                        base = 1024 * half
                        width = min(cols - base, 1024)
                        off = 0
                        while off < width:
                            w = min(512, width - off)
                            S.emit("tensor", lambda e, PSX=PSX, base=base,
                                   off=off, w=w, qtile=qtile, slot=slot:
                                   e.matmul(
                                PSX[:, off:off + w], qtile,
                                KH[slot][:, base + off:base + off + w],
                                start=True, stop=True))
                            off += 512
                    return S.inc("tensor", STEN)

                def emit_softmax(s, v_sc):
                    E = 9 + s
                    cols = 128 * E
                    sb = SCB[s % 2]
                    pb = PRB[s % 2]
                    S.wait("vector", STEN, v_sc)
                    S.wait("vector", SSCA, scb_free[s % 2])
                    S.emit("vector", lambda e, sb=sb: e.tensor_scalar(
                        out=sb[:, 0:1024], in0=PS0[:, :], scalar1=1.0,
                        scalar2=None, op0=ALU.mult, op1=ALU.max,
                        accum_out=MXV[:, :]))
                    S.emit("vector", lambda e, sb=sb, cols=cols: e.tensor_scalar(
                        out=sb[:, 1024:cols], in0=PS1[:, 0:cols - 1024],
                        scalar1=1.0, scalar2=MXV[:, :], op0=ALU.mult,
                        op1=ALU.max, accum_out=MXV[:, :]))
                    evac_done[s] = SVEC.n
                    v_ev = SVEC.n
                    S.wait("scalar", SVEC, v_ev)
                    S.wait("scalar", STEN, prb_free[s % 2])
                    S.emit("scalar", lambda e: e.activation(
                        out=NMX[:, :], in_=MXV[:, :], func=AF.Identity,
                        scale=-1.0))
                    if s > 0:
                        S.emit("scalar", lambda e, sb=sb, pb=pb, s=s:
                               e.activation(
                            out=pb[:, 0:128 * s], in_=sb[:, 0:128 * s],
                            func=AF.Exp, bias=NMX[:, :], scale=1.0,
                            accum_out=LS0[:, :]))
                    S.emit("scalar", lambda e, sb=sb, s=s, cols=cols:
                           e.activation(
                        out=sb[:, 128 * s:cols], in_=sb[:, 128 * s:cols],
                        func=AF.Exp, bias=NMX[:, :], scale=1.0))
                    exp_done[s] = SSCA.n
                    scb_free[s % 2] = SSCA.n
                    S.wait("vector", SSCA, exp_done[s])
                    S.wait("vector", STEN, prb_free[s % 2])
                    S.wait("vector", SVEC, v_msk)
                    if s == 0:
                        S.emit("vector", lambda e: e.memset(LS0[:, :], 0.0))
                    S.emit("vector", lambda e, sb=sb, pb=pb, s=s:
                           e.scalar_tensor_tensor(
                        out=pb[:, 128 * s:128 * (s + 1)],
                        in0=sb[:, 128 * s:128 * (s + 1)],
                        scalar=1.0, in1=MSKF[s][:, :], op0=ALU.mult,
                        op1=ALU.mult, accum_out=LS1[:, :]))
                    S.emit("vector", lambda e, sb=sb, pb=pb, s=s: e.tensor_scalar(
                        out=pb[:, 128 * (s + 1):128 * (s + 8)],
                        in0=sb[:, 128 * (s + 1):128 * (s + 8)],
                        scalar1=CM[:, s:s + 1], scalar2=None,
                        op0=ALU.mult, op1=ALU.add, accum_out=LSA[:, :]))
                    S.emit("vector", lambda e, sb=sb, pb=pb, s=s, cols=cols:
                           e.scalar_tensor_tensor(
                        out=pb[:, 128 * (s + 8):cols],
                        in0=sb[:, 128 * (s + 8):cols],
                        scalar=1.0, in1=MSKL[s][:, :], op0=ALU.mult,
                        op1=ALU.mult, accum_out=LSB[:, :]))
                    S.emit("vector", lambda e: e.tensor_tensor(
                        out=LS0[:, :], in0=LS0[:, :], in1=LS1[:, :],
                        op=ALU.add))
                    S.emit("vector", lambda e: e.tensor_tensor(
                        out=LSA[:, :], in0=LSA[:, :], in1=LSB[:, :],
                        op=ALU.add))
                    S.emit("vector", lambda e: e.tensor_tensor(
                        out=LS0[:, :], in0=LS0[:, :], in1=LSA[:, :],
                        op=ALU.add))
                    S.emit("vector", lambda e: e.reciprocal(
                        out=RLS[:, :], in_=LS0[:, :]))
                    S.emit("vector", lambda e, pb=pb, cols=cols: e.tensor_scalar(
                        out=pb[:, 0:cols], in0=pb[:, 0:cols],
                        scalar1=RLS[:, :], scalar2=None, op0=ALU.mult))
                    nrm_done[s] = SVEC.n

                def emit_pv(s):
                    E = 9 + s
                    pb = PRB[s % 2]
                    S.wait("tensor", SVEC, nrm_done[s])
                    ngrp = (E + 3) // 4
                    for g in range(ngrp):
                        j0, j1 = 4 * g, min(4 * g + 4, E)
                        if g > 0:
                            S.wait("tensor", SVEC)
                        for j in range(j0, j1):
                            S.emit("tensor", lambda e, j=j, pb=pb: e.transpose(
                                PS3b[:, 128 * (j % 4):128 * (j % 4 + 1)],
                                pb[:, 128 * j:128 * (j + 1)], IDE[:, :]))
                        tv = S.inc("tensor", STEN)
                        S.wait("vector", STEN, tv)
                        S.emit("vector", lambda e, j0=j0, j1=j1: e.tensor_copy(
                            out=PTB[:, 0:128 * (j1 - j0)],
                            in_=PS3b[:, 0:128 * (j1 - j0)]))
                        vv = SVEC.n
                        S.wait("tensor", SVEC, vv)
                        for j in range(j0, j1):
                            S.emit("tensor", lambda e, j=j, j0=j0, E=E,
                                   slot=slot: e.matmul(
                                PS3f[:, 256:384],
                                VT[slot][:, 128 * j:128 * (j + 1)],
                                PTB[:, 128 * (j - j0):128 * (j - j0 + 1)],
                                start=(j == 0), stop=(j == E - 1)))
                        S.inc("tensor", STEN)
                    prb_free[s % 2] = STEN.n
                    v_pv = STEN.n
                    S.wait("vector", STEN, v_pv)
                    S.emit("vector", lambda e, s=s, h=h: e.tensor_copy(
                        out=XZ[:, 1024 * h + 128 * s:1024 * h + 128 * (s + 1)],
                        in_=PS3f[:, 256:384]))

                for s in range(NQS + 1):
                    if s < NQS:
                        v_sc = emit_scores(s)
                        if s == NQS - 1:
                            kh_used[slot] = v_sc
                        emit_softmax(s, v_sc)
                    if s >= 1:
                        emit_pv(s - 1)

        def oproj_phase(l):
            src = wT[(l, "wo")]
            WOST = [SCR[:, 1024 * i:1024 * (i + 1)] for i in range(2)] + \
                   [PRF[:, 1024 * i:1024 * (i + 1)] for i in range(2)]
            wost_used = [SVEC.n] * 4
            opk = [0]
            opevac = [SVEC.n] * 3
            for dh in range(2):
                S.wait("sync", STEN)
                S.wait("sync", SSCA)
                S.wait("sync", SOUT)
                for fc in range(16):
                    sl = fc % 4
                    S.wait("sync", SVEC, wost_used[sl])
                    v = dma("sync", SWX[sl], WOST[sl],
                            src[128 * fc:128 * (fc + 1),
                                1024 * dh:1024 * (dh + 1)])
                    S.wait("vector", SWX[sl], v)
                    S.wait("vector", STEN)
                    S.emit("vector", lambda e, sl=sl, fc=fc: e.tensor_copy(
                        out=QA[:, 1024 * fc:1024 * (fc + 1)], in_=WOST[sl]))
                    wost_used[sl] = SVEC.n
                v_w = SVEC.n
                S.wait("tensor", SVEC, v_w)
                S.wait("tensor", SSCA)
                for s in range(NQS):
                    k = opk[0] % 3
                    opk[0] += 1
                    PSX = [PS0, PS1, PS2][k]
                    S.wait("tensor", SVEC, opevac[k])
                    for fc in range(16):
                        for th in range(2):
                            S.emit("tensor", lambda e, PSX=PSX, fc=fc, th=th,
                                   s=s: e.matmul(
                                PSX[:, 512 * th:512 * (th + 1)],
                                XZ[:, 1024 * fc + 128 * s:
                                   1024 * fc + 128 * (s + 1)],
                                QA[:, 1024 * fc + 512 * th:
                                   1024 * fc + 512 * (th + 1)],
                                start=(fc == 0), stop=(fc == 15)))
                    v_mm = S.inc("tensor", STEN)
                    S.wait("vector", STEN, v_mm)
                    S.emit("vector", lambda e, PSX=PSX, s=s, dh=dh:
                           e.scalar_tensor_tensor(
                        out=R[:, 2048 * s + 1024 * dh:2048 * s + 1024 * (dh + 1)],
                        in0=PSX[:, :], scalar=1.0,
                        in1=R[:, 2048 * s + 1024 * dh:2048 * s + 1024 * (dh + 1)],
                        op0=ALU.mult, op1=ALU.add))
                    opevac[k] = S.inc("vector", SVEC) or SVEC.n
            if l == 0:
                S.wait("tensor", SVEC)
                for i in range(16):
                    S.wait("tensor", SVEC)
                    S.emit("tensor", lambda e, i=i: e.transpose(
                        PS3f[0:1, 0:128], BIA[:, 48 + i:49 + i], IDEF[:, :]))
                    tv = S.inc("tensor", STEN)
                    S.wait("vector", STEN, tv)
                    S.emit("vector", lambda e, i=i: e.tensor_copy(
                        out=SCR[0:1, 128 * i:128 * (i + 1)],
                        in_=PS3f[0:1, 0:128]))
                    S.inc("vector", SVEC)
                v_row = SVEC.n
                S.wait("tensor", SVEC, v_row)
                for half, PSX in ((0, PS2), (1, PS1)):
                    for th in range(2):
                        S.emit("tensor", lambda e, PSX=PSX, half=half, th=th:
                               e.matmul(
                            PSX[:, 512 * th:512 * (th + 1)],
                            ONER[:, :],
                            SCR[0:1, 1024 * half + 512 * th:
                                1024 * half + 512 * (th + 1)],
                            start=True, stop=True))
                v_bo = S.inc("tensor", STEN)
                S.wait("vector", STEN, v_bo)
                for s in range(NQS):
                    for half, PSX in ((0, PS2), (1, PS1)):
                        S.emit("vector", lambda e, s=s, half=half, PSX=PSX:
                               e.tensor_tensor(
                            out=R[:, 2048 * s + 1024 * half:
                                  2048 * s + 1024 * (half + 1)],
                            in0=R[:, 2048 * s + 1024 * half:
                                  2048 * s + 1024 * (half + 1)],
                            in1=PSX[:, :], op=ALU.add))
                        S.inc("vector", SVEC)

        def ln_phase(rebuild_xt):
            S.wait("vector", SOUT)
            for s in range(NQS):
                for c in range(4):
                    S.emit("vector", lambda e, s=s, c=c: e.bn_stats(
                        out=BNS[:, 6 * c:6 * (c + 1)],
                        in_=R[:, 2048 * s + 512 * c:2048 * s + 512 * (c + 1)]))
                    S.inc("vector", SVEC)
                S.wait("vector", SVEC)
                S.emit("vector", lambda e: e.bn_aggr(out=MV[:, :], in_=BNS[:, :]))
                v_mv = S.inc("vector", SVEC)
                S.wait("scalar", SVEC, v_mv)
                S.emit("scalar", lambda e: e.activation(
                    out=RSTD[:, :], in_=MV[:, 1:2], func=AF.Ln,
                    bias=CONST[:, 0:1], scale=1.0))
                S.emit("scalar", lambda e: e.activation(
                    out=RSTD[:, :], in_=RSTD[:, :], func=AF.Exp, scale=-0.5))
                vs = S.inc("scalar", SSCA)
                S.wait("vector", SSCA, vs)
                S.emit("vector", lambda e, s=s: e.tensor_scalar(
                    out=R[:, 2048 * s:2048 * (s + 1)],
                    in0=R[:, 2048 * s:2048 * (s + 1)],
                    scalar1=MV[:, 0:1], scalar2=RSTD[:, :],
                    op0=ALU.subtract, op1=ALU.mult))
                S.inc("vector", SVEC)
            if rebuild_xt:
                S.wait("tensor", SVEC)
                return xt_rebuild()
            return SVEC.n

        def ffn_phase(l, deq_ag_done, hook=None):
            WG0 = [BFA[:, 2048 * i:2048 * (i + 1)] for i in range(2)]
            WG1 = [BFA[:, 4096 + 2048 * i:4096 + 2048 * (i + 1)] for i in range(2)]
            SST2B = SILB[:, :]
            if l == 0:
                wg_src, wu_src, wd_src = tern_ag["g"], tern_ag["u"], tern_ag["d"]
                etot = 8192
                ag_gu = max(tern_ag_done["g"], tern_ag_done["u"])
                ag_d = tern_ag_done["d"]
            else:
                wg_src, wu_src, wd_src = deq_ag["g"], deq_ag["u"], deq_ag["d"]
                etot = 16384
                ag_gu = ag_d = deq_ag_done
            n_ec = etot // 2048
            rw = 1024 if l == 0 else 2048
            gsrc3 = wg_src.rearrange("r (dc p) e -> r p dc e", p=128)
            usrc3 = wu_src.rearrange("r (dc p) e -> r p dc e", p=128)
            wdr = wd_src.rearrange("r (ec p) d -> r ec p d", p=128)
            wg_used = [STEN.n, STEN.n]
            down_done = [STEN.n]

            S.wait("tensor", SVEC)
            S.wait("tensor", SSCA)
            S.wait("vector", SOUT)
            S.wait("scalar", STSTG)
            S.wait("scalar", SDQO)
            S.wait("scalar", SWCST)
            S.wait("scalar", SGD)

            def wdnr_ap(es):
                if es < 6:
                    return BFA[:, 1024 * es:1024 * (es + 1)]
                return BFA[:, 8192 + 1024 * (es - 6):8192 + 1024 * (es - 5)]

            PSR = [PS0, PS1, PS2]
            ring_evac = [SVEC.n] * 3
            silb_evac = [SVEC.n] * 2
            rk = [0]
            for ec in range(n_ec):
                S.wait("scalar", SCC, ag_gu)
                S.wait("scalar", SCC, ag_d)
                S.wait("scalar", STEN, down_done[0])
                for es in range(6, 16):
                    e0 = 2048 * ec + 128 * es
                    r, erow = e0 // rw, (e0 % rw) // 128
                    dma("scalar", SWX[4],
                        wdnr_ap(es), wdr[r, erow, :, 0:1024])
                for es in range(16):
                    e0 = 2048 * ec + 128 * es
                    r, c0 = e0 // rw, e0 % rw
                    slot = es % 2
                    S.wait("scalar", STEN, max(wg_used[slot], down_done[0]))
                    dma("scalar", SWX[slot],
                        WG0[slot][:, :].rearrange("p (dc e) -> p dc e", e=128),
                        gsrc3[r, :, :, c0:c0 + 128])
                    dma("scalar", SWX[2 + slot],
                        WG1[slot][:, :].rearrange("p (dc e) -> p dc e", e=128),
                        usrc3[r, :, :, c0:c0 + 128])
                    S.wait("tensor", SWX[slot])
                    S.wait("tensor", SWX[2 + slot])
                    if es == 0:
                        S.wait("tensor", SVEC)
                    for th in range(2):
                        k = rk[0] % 3
                        rk[0] += 1
                        PSX = PSR[k]
                        S.wait("tensor", SVEC, ring_evac[k])
                        for dc in range(16):
                            S.emit("tensor", lambda e, slot=slot, dc=dc, th=th,
                                   PSX=PSX: e.matmul(
                                PSX[:, 0:512],
                                WG0[slot][:, 128 * dc:128 * (dc + 1)],
                                XZ[:, 1024 * dc + 512 * th:
                                   1024 * dc + 512 * (th + 1)],
                                start=(dc == 0), stop=(dc == 15)))
                        for dc in range(16):
                            S.emit("tensor", lambda e, slot=slot, dc=dc, th=th,
                                   PSX=PSX: e.matmul(
                                PSX[:, 512:1024],
                                WG1[slot][:, 128 * dc:128 * (dc + 1)],
                                XZ[:, 1024 * dc + 512 * th:
                                   1024 * dc + 512 * (th + 1)],
                                start=(dc == 0), stop=(dc == 15)))
                        v_mm = S.inc("tensor", STEN)
                        if th == 1:
                            wg_used[slot] = v_mm
                        S.wait("scalar", STEN, v_mm)
                        S.wait("scalar", SVEC, silb_evac[th])
                        if l == 0:
                            S.emit("scalar", lambda e, PSX=PSX, th=th:
                                   e.activation(
                                out=SST2B[:, 512 * th:512 * (th + 1)],
                                in_=PSX[:, 512:1024], func=AF.Silu))
                            gsrc_ps = 0
                        else:
                            S.emit("scalar", lambda e, PSX=PSX, th=th:
                                   e.activation(
                                out=SST2B[:, 512 * th:512 * (th + 1)],
                                in_=PSX[:, 0:512], func=AF.Gelu_apprx_tanh))
                            gsrc_ps = 512
                        vs = SSCA.n
                        S.wait("vector", SSCA, vs)
                        S.emit("vector", lambda e, es=es, th=th, PSX=PSX,
                               gsrc_ps=gsrc_ps: e.scalar_tensor_tensor(
                            out=QA[:, 1024 * es + 512 * th:
                                   1024 * es + 512 * (th + 1)],
                            in0=PSX[:, gsrc_ps:gsrc_ps + 512],
                            scalar=1.0, in1=SST2B[:, 512 * th:512 * (th + 1)],
                            op0=ALU.mult, op1=ALU.mult))
                        ring_evac[k] = SVEC.n
                        silb_evac[th] = SVEC.n
                    if hook is not None:
                        hook()
                v_hid = SVEC.n

                for dh in range(2):
                    S.wait("scalar", STEN)
                    S.wait("scalar", SVEC)
                    es_list = range(6) if dh == 0 else range(16)
                    for es in es_list:
                        e0 = 2048 * ec + 128 * es
                        r, erow = e0 // rw, (e0 % rw) // 128
                        dma("scalar", SWX[4],
                            wdnr_ap(es),
                            wdr[r, erow, :, 1024 * dh:1024 * (dh + 1)])
                    v_wd = SWX[4].n
                    S.wait("tensor", SWX[4], v_wd)
                    S.wait("tensor", SVEC, v_hid)
                    for s in range(NQS):
                        k = rk[0] % 3
                        rk[0] += 1
                        PSX = PSR[k]
                        S.wait("tensor", SVEC, ring_evac[k])
                        for es in range(16):
                            for th in range(2):
                                S.emit("tensor", lambda e, PSX=PSX, es=es,
                                       th=th, s=s, ap=wdnr_ap(es): e.matmul(
                                    PSX[:, 512 * th:512 * (th + 1)],
                                    QA[:, 1024 * es + 128 * s:
                                       1024 * es + 128 * (s + 1)],
                                    ap[:, 512 * th:512 * (th + 1)],
                                    start=(es == 0), stop=(es == 15)))
                        v_mm = S.inc("tensor", STEN)
                        down_done[0] = v_mm
                        S.wait("vector", STEN, v_mm)
                        S.emit("vector", lambda e, PSX=PSX, s=s, dh=dh:
                               e.scalar_tensor_tensor(
                            out=R[:, 2048 * s + 1024 * dh:
                                  2048 * s + 1024 * (dh + 1)],
                            in0=PSX[:, :], scalar=1.0,
                            in1=R[:, 2048 * s + 1024 * dh:
                                  2048 * s + 1024 * (dh + 1)],
                            op0=ALU.mult, op1=ALU.add))
                        ring_evac[k] = S.inc("vector", SVEC) or SVEC.n

        S.wait("tensor", SVEC, v_xt)
        deq_ag_done = 0
        tern_ag_done = {"u": 0, "g": 0, "d": 0}
        v_fin = SVEC.n
        if stop_after >= 1:
            kvd0 = qkv_phase(0, hook=lambda: tern_drv.step(2))
            tern_drv.flush()
        if stop_after >= 2:
            agd0 = kv_allgather(0, kvd0)
            for m in TMAT:
                S.wait("gpsimd", STSTG, tern_done[m][0])
                S.wait("gpsimd", SWCST, tern_done[m][1])
                S.emit("gpsimd", lambda e, m=m: e.collective_compute(
                    "AllGather", ALU.bypass, ins=[tern_c[m][:, :]],
                    outs=[tern_ag[m][:, :, :]], replica_groups=ALL8),
                       auto=False)
                tern_ag_done[m] = S.inc("gpsimd", SCC)
            attn_phase(0, agd0)
        if debug and stop_after >= 2:
            S.wait("sync", SVEC)
            dzr = dbg_d["d_z0"].rearrange("(c p) t -> c p t", p=128)
            dqr = dbg_d["d_q0"].rearrange("(c p) t -> c p t", p=128)
            for fc in range(16):
                dma("sync", SOUT, dzr[fc, :, :], XZ[:, 1024 * fc:1024 * (fc + 1)])
                dma("sync", SOUT, dqr[fc, :, :], QA[:, 1024 * fc:1024 * (fc + 1)])
        if stop_after >= 3:
            oproj_phase(0)
            if debug:
                S.wait("sync", SVEC)
                rr = dbg_d["d_R1"].rearrange("(s p) d -> s p d", p=128)
                for s in range(NQS):
                    dma("sync", SOUT, rr[s, :, :], R[:, 2048 * s:2048 * (s + 1)])
            ln_phase(rebuild_xt=True)
        if stop_after >= 4:
            ffn_phase(0, deq_ag_done, hook=lambda: deq_drv.step(2))
            deq_drv.flush()
            S.wait("gpsimd", SDQLD)
            S.wait("gpsimd", SWA)
            S.wait("gpsimd", SVEC)
            S.emit("gpsimd", lambda e: e.iota(
                out=KIOT[:, :], pattern=[[1, 2048]], base=0,
                channel_multiplier=0))
            for m in ("g",):
                S.wait("gpsimd", SDQO, deq_done[m][0])
                S.wait("gpsimd", SGD, deq_done[m][1])
                S.emit("gpsimd", lambda e, m=m: e.collective_compute(
                    "AllGather", ALU.bypass, ins=[deq_c[m][:, :]],
                    outs=[deq_ag[m][:, :, :]], replica_groups=ALL8), auto=False)
                S.inc("gpsimd", SCC)
            deq_ag_done = SCC.n
            if debug:
                S.wait("sync", SVEC)
                rr = dbg_d["d_R2"].rearrange("(s p) d -> s p d", p=128)
                for s in range(NQS):
                    dma("sync", SOUT, rr[s, :, :], R[:, 2048 * s:2048 * (s + 1)])
            ln_phase(rebuild_xt=True)
        if stop_after >= 5:
            kvd1 = qkv_phase(1)
            agd1 = kv_allgather(1, kvd1)
            for m in ("u", "d"):
                S.wait("gpsimd", SDQO, deq_done[m][0])
                S.wait("gpsimd", SGD, deq_done[m][1])
                S.emit("gpsimd", lambda e, m=m: e.collective_compute(
                    "AllGather", ALU.bypass, ins=[deq_c[m][:, :]],
                    outs=[deq_ag[m][:, :, :]], replica_groups=ALL8), auto=False)
                S.inc("gpsimd", SCC)
            deq_ag_done = SCC.n
            attn_phase(1, agd1)
        if stop_after >= 6:
            oproj_phase(1)
            if debug:
                S.wait("sync", SVEC)
                rr = dbg_d["d_R3"].rearrange("(s p) d -> s p d", p=128)
                for s in range(NQS):
                    dma("sync", SOUT, rr[s, :, :], R[:, 2048 * s:2048 * (s + 1)])
            ln_phase(rebuild_xt=True)
        if stop_after >= 7:
            ffn_phase(1, deq_ag_done)
            v_fin = ln_phase(rebuild_xt=False)

        S.wait("sync", SVEC, v_fin)
        out_r = out_d.rearrange("(s p) d -> s p d", p=128)
        for s in range(NQS):
            dma("sync", SOUT, out_r[s, :, :], R[:, 2048 * s:2048 * (s + 1)])
        if debug and stop_after >= -2:
            S.wait("sync", SCC, tern_ag_done["u"])
            dma("sync", SOUT, dbg_d["d_ternu"][:, :], tern_ag["u"][0, :, :])
        if debug and stop_after >= 2:
            S.wait("sync", SCC, deq_ag_done)
            dma("sync", SOUT, dbg_d["d_deqg"][:, :], deq_ag["g"][0, :, :])
        S.wait("sync", SOUT, SOUT.n)

        @block.sync
        def _(eng):
            S.run("sync", eng)

        @block.scalar
        def _(eng):
            S.run("scalar", eng)

        @block.vector
        def _(eng):
            S.run("vector", eng)

        @block.tensor
        def _(eng):
            S.run("tensor", eng)

        @block.gpsimd
        def _(eng):
            S.run("gpsimd", eng)

    return nc


def _prep_inputs(inputs):
    f32 = np.float32
    x = np.ascontiguousarray(np.asarray(inputs["L0_attn_x"], dtype=f32)
                             .reshape(B * T, D))
    common = {}
    for l, pre in ((0, "L0_attn_"), (1, "L1_attn_")):
        if l == 0:
            wq = np.asarray(inputs[pre + "wq"], f32)
            wk = np.asarray(inputs[pre + "wk"], f32)
            wv = np.asarray(inputs[pre + "wv"], f32)
            wo = np.asarray(inputs[pre + "wo"], f32)
        else:
            wq = np.asarray(inputs[pre + "W_Q"], f32).reshape(D, D)
            wk = np.asarray(inputs[pre + "W_K"], f32).reshape(D, D)
            wv = np.asarray(inputs[pre + "W_V"], f32).reshape(D, D)
            wo = np.asarray(inputs[pre + "W_O"], f32)
        common[f"L{l}_wqT"] = np.ascontiguousarray(wq.T)
        common[f"L{l}_wkT"] = np.ascontiguousarray(wk.T)
        common[f"L{l}_wvT"] = np.ascontiguousarray(wv.T)
        common[f"L{l}_woT"] = np.ascontiguousarray(wo.T)
    for m in ("q", "k", "v", "o"):
        common[f"L0_b{m}"] = np.ascontiguousarray(
            np.asarray(inputs[f"L0_attn_b{m}"], f32).reshape(16, 128).T)
    common["ident"] = np.eye(128, dtype=ml_dtypes.bfloat16)
    common["identf"] = np.eye(128, dtype=np.float32)

    wupT = np.asarray(inputs["L0_ffn_W_up"], f32).T
    wgT = np.asarray(inputs["L0_ffn_W_gate"], f32).T
    wdT = np.asarray(inputs["L0_ffn_W_down"], f32).T

    pg = np.asarray(inputs["L1_ffn_gate_W_packed"], np.int32)
    pu = np.asarray(inputs["L1_ffn_up_W_packed"], np.int32)
    pd = np.asarray(inputs["L1_ffn_down_W_packed"], np.int32)
    sg = np.asarray(inputs["L1_ffn_gate_scales"], f32)
    su = np.asarray(inputs["L1_ffn_up_scales"], f32)
    sd = np.asarray(inputs["L1_ffn_down_scales"], f32)

    maps = []
    for c in range(NCORE):
        mp = dict(common)
        mp["x"] = np.ascontiguousarray(x[c * TOK:(c + 1) * TOK])
        qi = ((c % 2) * 1024 + np.arange(TOK, dtype=f32))
        mp["qidx"] = np.ascontiguousarray(qi.reshape(NQS, 128).T)
        mp["tu_sl"] = np.ascontiguousarray(wupT[:, c * 1024:(c + 1) * 1024])
        mp["tg_sl"] = np.ascontiguousarray(wgT[:, c * 1024:(c + 1) * 1024])
        mp["td_sl"] = np.ascontiguousarray(wdT[c * 1024:(c + 1) * 1024, :])
        mp["pkg_sl"] = np.ascontiguousarray(pg[c * 2048:(c + 1) * 2048, :].T)
        mp["pku_sl"] = np.ascontiguousarray(pu[c * 2048:(c + 1) * 2048, :].T)
        mp["pkd_sl"] = np.ascontiguousarray(pd.T[c * 1024:(c + 1) * 1024, :])
        mp["sxg_sl"] = np.ascontiguousarray(
            np.repeat(sg[c * 2048:(c + 1) * 2048, :].T, 16, axis=0))
        mp["sxu_sl"] = np.ascontiguousarray(
            np.repeat(su[c * 2048:(c + 1) * 2048, :].T, 16, axis=0))
        mp["sxd_sl"] = np.ascontiguousarray(
            np.repeat(sd.T, 16, axis=0)[c * 1024:(c + 1) * 1024, :])
        maps.append(mp)
    return maps


_cache = {}


def kernel(**inputs):
    maps = _prep_inputs(inputs)
    if "prog" not in _cache:
        _cache["prog"] = build_program(debug=False)
    res = run_bass_kernel_spmd(_cache["prog"], maps, core_ids=list(range(NCORE)))
    out = np.concatenate([r["out"] for r in res.results], axis=0)
    return out.reshape(B, T, D).astype(np.float32)


# version 11 (final)
# speedup vs baseline: 1.1348x; 1.0514x over previous
import contextlib
import math

import numpy as np
import ml_dtypes

import concourse.bass as bass
import concourse.mybir as mybir
from concourse.bass_utils import run_bass_kernel_spmd

F32 = mybir.dt.float32
BF16 = mybir.dt.bfloat16
I32 = mybir.dt.int32
AF = mybir.ActivationFunctionType
ALU = mybir.AluOpType

B, T, D = 4, 2048, 2048
NCORE = 8
TOK = 1024
NQS = 8
H = 16
SCALE = 1.0 / math.sqrt(128.0)
NW0 = 2048 * 8192

ENGINES = ("sync", "scalar", "vector", "tensor", "gpsimd")
PAIRS = [[0, 1], [2, 3], [4, 5], [6, 7]]
ALL8 = [[0, 1, 2, 3, 4, 5, 6, 7]]


class Sem:
    def __init__(self, handle, name):
        self.h, self.name, self.n = handle, name, 0


class Sched:

    def __init__(self):
        self.q = {e: [] for e in ENGINES}
        self.last_wait = {e: {} for e in ENGINES}
        self.auto = {}

    def _raw_emit(self, eng, fn):
        self.q[eng].append(fn)

    def emit(self, eng, fn, auto=True):
        asem = self.auto.get(eng)
        if asem is not None and auto:
            self.wait(eng, asem, asem.n)
            asem.n += 1
            self.q[eng].append(
                lambda e, fn=fn, s=asem.h: fn(e).then_inc(s, 1))
        else:
            self.q[eng].append(fn)

    def inc(self, eng, sem, k=1):
        asem = self.auto.get(eng)
        if asem is not None and sem is asem:
            return sem.n
        sem.n += k
        fn = self.q[eng].pop()
        self.q[eng].append(lambda e, fn=fn, s=sem.h, k=k: fn(e).then_inc(s, k))
        return sem.n

    def wait(self, eng, sem, val=None):
        v = sem.n if val is None else val
        if v <= 0:
            return
        lw = self.last_wait[eng].get(sem.name, 0)
        if v <= lw:
            return
        self.last_wait[eng][sem.name] = v
        self.q[eng].append(lambda e, v=v, s=sem.h: e.wait_ge(s, v))

    def run(self, eng_name, eng):
        for fn in self.q[eng_name]:
            fn(eng)


def build_program(debug=False, stop_after=99):
    nc = bass.Bass("TRN2", target_bir_lowering=False)

    def din(name, shape, dtype=F32):
        return nc.dram_tensor(name, shape, dtype, kind="ExternalInput").ap()

    x_in = din("x", [TOK, D])
    qidx_in = din("qidx", [128, NQS])
    ident_in = din("ident", [128, 128], BF16)
    identf_in = din("identf", [128, 128], F32)
    wT = {(l, m): din(f"L{l}_{m}T", [D, D])
          for l in (0, 1) for m in ("wq", "wk", "wv", "wo")}
    bias_in = {m: din(f"L0_b{m}", [128, 16]) for m in ("q", "k", "v", "o")}
    tern_sl = {m: din(f"t{m}_sl", [D, 1024] if m != "d" else [1024, D])
               for m in ("u", "g", "d")}
    pk_sl = {m: din(f"pk{m}_sl", [1024, 2048], I32) for m in ("g", "u", "d")}
    sx_sl = {m: din(f"sx{m}_sl", [1024, 2048]) for m in ("g", "u", "d")}
    out_d = nc.dram_tensor("out", [TOK, D], F32, kind="ExternalOutput").ap()

    dbg_d = {}
    if debug:
        for nm, shape, dt in (
            ("d_R1", [TOK, D], F32), ("d_R2", [TOK, D], F32),
            ("d_R3", [TOK, D], F32), ("d_scl", [1, 8], F32),
            ("d_ternu", [D, 1024], BF16), ("d_deqg", [2048, 2048], BF16),
            ("d_q0", [D, TOK], BF16), ("d_z0", [D, TOK], BF16),
        ):
            dbg_d[nm] = nc.dram_tensor(nm, shape, dt, kind="ExternalOutput").ap()

    def idram(name, shape, dtype, shared=False):
        return nc.dram_tensor(name, shape, dtype, kind="Internal",
                              addr_space="Local").ap()

    kv_c = {(l, m): idram(f"kv{l}{m}_c", [D, TOK], BF16)
            for l in (0, 1) for m in ("k", "v")}
    kv_ag = {(l, m): idram(f"kv{l}{m}_ag", [2, D, TOK], BF16)
             for l in (0, 1) for m in ("k", "v")}
    tern_c = {m: idram(f"tern{m}_c", [D, 1024] if m != "d" else [1024, D], BF16)
              for m in ("u", "g", "d")}
    tern_ag = {m: idram(f"tern{m}_ag",
                        [8, D, 1024] if m != "d" else [8, 1024, D], BF16,
                        shared=True) for m in ("u", "g", "d")}
    deq_c = {m: idram(f"deq{m}_c", [2048, 2048], BF16) for m in ("g", "u", "d")}
    deq_ag = {m: idram(f"deq{m}_ag", [8, 2048, 2048], BF16, shared=True)
              for m in ("g", "u", "d")}
    ar_i = idram("ar_i", [1, 64], F32)
    ar_o = idram("ar_o", [1, 64], F32, shared=True)

    stack = contextlib.ExitStack()
    with stack:
        def sbuf(name, shape, dtype):
            return stack.enter_context(nc.sbuf_tensor(name, shape, dtype))

        def psum(name, shape, dtype=F32):
            return stack.enter_context(nc.psum_tensor(name, shape, dtype))

        R = sbuf("R", [128, 8 * 2048], F32)
        XZ = sbuf("XZ", [128, 16 * 1024], BF16)
        QA = sbuf("QA", [128, 16 * 1024], BF16)
        BFA = sbuf("BFA", [128, 18432], BF16)
        SILB = sbuf("SILB", [128, 1024], BF16)
        PTB = SILB[:, 0:512]
        DQF = sbuf("DQF", [128, 4096], BF16)
        SCR = sbuf("SCR", [128, 2048], F32)
        PRF = sbuf("PRF", [128, 2048], F32)
        DQV = sbuf("DQV", [128, 2048], I32)
        KIOT = DQV
        DQNI = sbuf("DQNI", [128, 512], I32)
        DQUI = sbuf("DQUI", [128, 512], I32)
        IDE = sbuf("IDE", [128, 128], BF16)
        IDEF = sbuf("IDEF", [128, 128], F32)
        QIDX = sbuf("QIDX", [128, NQS], F32)
        BIA = sbuf("BIA", [128, 64], F32)
        BQS = sbuf("BQS", [128, 16], F32)
        ZBI = sbuf("ZBI", [128, 1], F32)
        CONST = sbuf("CONST", [128, 8], F32)
        MV = sbuf("MV", [128, 2], F32)
        BNS = sbuf("BNS", [128, 24], F32)
        RSTD = sbuf("RSTD", [128, 1], F32)
        NMX = sbuf("NMX", [128, 1], F32)
        MXV = sbuf("MXV", [128, 1], F32)
        LS0 = sbuf("LS0", [128, 1], F32)
        LS1 = sbuf("LS1", [128, 1], F32)
        RLS = sbuf("RLS", [128, 1], F32)
        ACC = sbuf("ACC", [128, 4], F32)
        SCL = sbuf("SCL", [128, 8], F32)
        THR = sbuf("THR", [128, 8], F32)
        ONE1 = sbuf("ONE1", [128, 1], F32)
        CM = sbuf("CM", [128, 8], F32)
        LSA = sbuf("LSA", [128, 1], F32)
        LSB = sbuf("LSB", [128, 1], F32)
        ONER = sbuf("ONER", [1, 128], F32)

        PS0 = psum("PS0", [128, 1024])
        PS1 = psum("PS1", [128, 1024])
        PS2 = psum("PS2", [128, 1024])
        PS3f = psum("PS3f", [128, 512])
        PS3b = psum("PS3b", [128, 512], BF16)

        semnames = ["svec", "ssca", "sten", "scc", "sgp",
                    "sph0", "swa", "swb", "swcst", "ststg", "sdqld", "sdqo",
                    "skv0", "skv1", "sout", "sgd"] + [f"swx{i}" for i in range(8)]
        sems = {nm: Sem(stack.enter_context(nc.semaphore(nm)), nm)
                for nm in semnames}
        SVEC, SSCA, STEN = sems["svec"], sems["ssca"], sems["sten"]
        SCC, SGP = sems["scc"], sems["sgp"]
        SPH0, SWA, SWB = sems["sph0"], sems["swa"], sems["swb"]
        SWCST, STSTG = sems["swcst"], sems["ststg"]
        SDQLD, SDQO = sems["sdqld"], sems["sdqo"]
        SKV = [sems["skv0"], sems["skv1"]]
        SOUT, SGD = sems["sout"], sems["sgd"]
        SWX = [sems[f"swx{i}"] for i in range(8)]

        block = stack.enter_context(nc.Block())
        S = Sched()
        S.auto = {"vector": SVEC, "scalar": SSCA, "gpsimd": SGP}

        def dma(q, sem, out, in_):
            S._raw_emit(q, lambda e, o=out, i=in_: e.dma_start(out=o, in_=i))
            sem.n += 16
            fn = S.q[q].pop()
            S.q[q].append(lambda e, fn=fn, s=sem.h: fn(e).then_inc(s, 16))
            return sem.n

        dma("sync", SPH0, QIDX[:, :], qidx_in[:, :])
        dma("sync", SPH0, IDE[:, :], ident_in[:, :])
        dma("sync", SPH0, IDEF[:, :], identf_in[:, :])
        for j, m in enumerate(("q", "k", "v", "o")):
            dma("sync", SPH0, BIA[:, 16 * j:16 * (j + 1)], bias_in[m][:, :])
        x_r = x_in.rearrange("(s p) d -> s p d", p=128)
        for s in range(NQS):
            dma("sync", SPH0, R[:, 2048 * s:2048 * (s + 1)], x_r[s, :, :])
        v_ld0 = SPH0.n

        S.emit("gpsimd", lambda e: e.iota(out=KIOT[:, :], pattern=[[1, 2048]],
                                          base=0, channel_multiplier=0))
        S.inc("gpsimd", SGP)
        for _ms in [lambda e: e.memset(CONST[:, 0:1], 1e-5),
                    lambda e: e.memset(CONST[:, 1:2], 7.5),
                    lambda e: e.memset(CONST[:, 2:3], -4.0),
                    lambda e: e.memset(CONST[:, 3:4], -12.0),
                    lambda e: e.memset(ONE1[:, :], 1.0),
                    lambda e: e.memset(ZBI[:, :], 0.0),
                    lambda e: e.memset(ACC[:, :], 0.0)]:
            S.emit("gpsimd", _ms)
            S.inc("gpsimd", SGP)
        v_gp0 = SGP.n

        S.wait("vector", SPH0, v_ld0)
        S.emit("vector", lambda e: e.tensor_scalar(
            out=BQS[:, :], in0=BIA[:, 0:16], scalar1=SCALE, scalar2=None,
            op0=ALU.mult))
        S.inc("vector", SVEC)

        def xt_rebuild():
            for s in range(NQS):
                for dc in range(16):
                    idx = s * 16 + dc
                    slot = idx % 2
                    if idx >= 2 or idx == 0:
                        S.wait("tensor", SVEC)
                    S.emit("tensor", lambda e, s=s, dc=dc, slot=slot: e.transpose(
                        PS3f[:, 128 * slot:128 * (slot + 1)],
                        R[:, 2048 * s + 128 * dc:2048 * s + 128 * (dc + 1)],
                        IDEF[:, :]))
                    tv = S.inc("tensor", STEN)
                    S.wait("vector", STEN, tv)
                    S.emit("vector", lambda e, s=s, dc=dc, slot=slot: e.tensor_copy(
                        out=XZ[:, 1024 * dc + 128 * s:1024 * dc + 128 * (s + 1)],
                        in_=PS3f[:, 128 * slot:128 * (slot + 1)]))
                    S.inc("vector", SVEC)
            return SVEC.n

        S.wait("tensor", SPH0, v_ld0)
        S.wait("tensor", SGP, v_gp0)
        S.emit("tensor", lambda e: e.transpose(
            PS3f[0:1, 0:128], ONE1[:, 0:1], IDEF[:, :]))
        tv0 = S.inc("tensor", STEN)
        S.wait("vector", STEN, tv0)
        S.emit("vector", lambda e: e.tensor_copy(out=ONER[:, :],
                                                 in_=PS3f[0:1, 0:128]))
        v_oner = S.inc("vector", SVEC)
        v_xt = xt_rebuild()

        TMAT = ("u", "g", "d") if stop_after >= -2 else ()
        for mi, m in enumerate(TMAT):
            src = tern_sl[m]
            nrow, ncol = src.tensor.shape
            src_r = src.rearrange("(c p) e -> c p e", p=128)
            for t in range(nrow // 128):
                buf = SCR if t % 2 == 0 else PRF
                sbf = SWA if t % 2 == 0 else SWB
                S.wait("sync", SVEC)
                v = dma("sync", sbf, buf[:, 0:ncol], src_r[t, :, :])
                S.wait("vector", sbf, v)
                S.emit("vector", lambda e, buf=buf, ncol=ncol: e.tensor_reduce(
                    out=MXV[:, :], in_=buf[:, 0:ncol], axis=mybir.AxisListType.X,
                    op=ALU.add, apply_absolute_value=True))
                S.inc("vector", SVEC)
                S.wait("vector", SVEC)
                S.wait("vector", SGP, v_gp0)
                S.emit("vector", lambda e, mi=mi: e.tensor_tensor(
                    out=ACC[:, mi:mi + 1], in0=ACC[:, mi:mi + 1], in1=MXV[:, :],
                    op=ALU.add))
                S.inc("vector", SVEC)
        v_abs = SVEC.n
        S.wait("tensor", SVEC, v_abs)
        tern_done = {}
        tern_ag_done = {"u": 0, "g": 0, "d": 0}
        deq_done = {"g": 0, "u": 0, "d": 0}
        v_thr = SVEC.n
        if stop_after >= -2:
            S.emit("tensor", lambda e: e.matmul(
                PS3f[0:1, 0:4], ONE1[:, :], ACC[:, 0:4], start=True, stop=True))
            vt = S.inc("tensor", STEN)
            S.wait("vector", STEN, vt)
            S.emit("vector", lambda e: e.memset(SCR[0:1, 0:64], 0.0))
            S.inc("vector", SVEC)
            S.wait("vector", SVEC)
            S.emit("vector", lambda e: e.tensor_copy(out=SCR[0:1, 0:4],
                                                     in_=PS3f[0:1, 0:4]))
            v_sums = S.inc("vector", SVEC)
            S.wait("sync", SVEC, v_sums)
            v_ari = dma("sync", SGD, ar_i[:, 0:64], SCR[0:1, 0:64])
            S.wait("gpsimd", SGD, v_ari)
            S.emit("gpsimd", lambda e: e.collective_compute(
                "AllReduce", ALU.add, ins=[ar_i[:, :]], outs=[ar_o[:, :]],
                replica_groups=ALL8), auto=False)
            v_ar = S.inc("gpsimd", SCC)
            S.wait("sync", SCC, v_ar)
            v_aro = dma("sync", SGD, SCR[0:1, 0:64], ar_o[:, :])
            S.wait("tensor", SGD, v_aro)
            S.wait("tensor", SVEC, v_oner)
            S.emit("tensor", lambda e: e.matmul(
                PS3f[:, 128:136], ONER[:, :], SCR[0:1, 0:8],
                start=True, stop=True))
            v_bcm = S.inc("tensor", STEN)
            S.wait("vector", STEN, v_bcm)
            S.emit("vector", lambda e: e.tensor_copy(out=SCL[:, 0:8],
                                                     in_=PS3f[:, 128:136]))
            S.inc("vector", SVEC)
            S.emit("vector", lambda e: e.tensor_scalar(
                out=THR[:, :], in0=SCL[:, 0:8], scalar1=0.5 / NW0,
                scalar2=0.5e-8, op0=ALU.mult, op1=ALU.add))
            S.emit("vector", lambda e: e.tensor_scalar(
                out=SCL[:, 0:8], in0=SCL[:, 0:8], scalar1=1.0 / NW0,
                scalar2=1e-8, op0=ALU.mult, op1=ALU.add))
            v_thr = S.inc("vector", SVEC)
            if debug:
                S.wait("sync", SVEC, v_thr)
                dma("sync", SOUT, dbg_d["d_scl"][:, :], SCL[0:1, 0:8])

            TSTG2B = [BFA[:, 4096:5120], BFA[:, 9216:10240]]
            SGNB2 = [BFA[:, 5120:6144], BFA[:, 6144:7168]]
            tern_done = {}
            tern_units = []

            def _tern_unit(mi, m, t, half, is_last):
                src_t = tern_sl[m]
                nrow, ncol = src_t.tensor.shape
                src_r = src_t.rearrange("(c p) e -> c p e", p=128)
                dst_r = tern_c[m].rearrange("(c p) e -> c p e", p=128)
                c0 = 1024 * half
                stv = [0]

                def load(pu):
                    stage = PRF[:, 1024 * pu:1024 * (pu + 1)]
                    S.wait("scalar", SVEC)
                    stv[0] = dma("scalar", SWB if pu == 0 else SWA,
                                 stage, src_r[t, :, c0:c0 + 1024])

                def compute(pu):
                    stage = PRF[:, 1024 * pu:1024 * (pu + 1)]
                    sgnb = SGNB2[pu]
                    S.wait("scalar", SWB if pu == 0 else SWA, stv[0])
                    S.wait("scalar", SVEC, v_thr)
                    S.emit("scalar", lambda e, sgnb=sgnb, stage=stage: e.activation(
                        out=sgnb[:, :], in_=stage, func=AF.Sign))
                    S.emit("scalar", lambda e, stage=stage: e.activation(
                        out=stage, in_=stage, func=AF.Abs))
                    vs = SSCA.n
                    S.wait("vector", SSCA, vs)
                    S.emit("vector", lambda e, mi=mi, stage=stage: e.tensor_scalar(
                        out=stage, in0=stage,
                        scalar1=THR[:, mi:mi + 1], scalar2=None, op0=ALU.is_ge))
                    tsem = STSTG if pu == 0 else SWCST
                    S.wait("vector", tsem)
                    S.emit("vector", lambda e, mi=mi, sgnb=sgnb, stage=stage,
                           pu=pu: e.scalar_tensor_tensor(
                        out=TSTG2B[pu][:, :], in0=sgnb[:, :],
                        scalar=SCL[:, mi:mi + 1], in1=stage,
                        op0=ALU.mult, op1=ALU.mult))
                    vv = SVEC.n
                    S.wait("scalar", SVEC, vv)
                    dma("scalar", tsem, dst_r[t, :, c0:c0 + 1024],
                        TSTG2B[pu][:, :])
                    if is_last:
                        tern_done[m] = (STSTG.n, SWCST.n)
                return (load, compute)

            for mi, m in enumerate(TMAT):
                nrow, ncol = tern_sl[m].tensor.shape
                nt, nh = nrow // 128, ncol // 1024
                cnt = 0
                for t in range(nt):
                    for half in range(nh):
                        cnt += 1
                        tern_units.append(
                            _tern_unit(mi, m, t, half, cnt == nt * nh))

            DQN2 = SCR[:, 0:512]
            DQA2 = SCR[:, 512:1024]
            DQB2 = SCR[:, 1024:1536]
            DQG2 = SCR[:, 1536:2048]
            deq_done = {}
            deq_units = []

            def _deq_unit(m, jc, ch, is_last):
                pk = pk_sl[m].rearrange("(c p) n -> c p n", p=128)
                sx = sx_sl[m].rearrange("(c p) n -> c p n", p=128)
                dst_r = deq_c[m].rearrange("(c p two) n -> c two p n",
                                           p=128, two=2)
                c0 = 1024 * ch
                ldv = [0]

                def load(pu):
                    dqv = DQV[:, 1024 * pu:1024 * (pu + 1)]
                    dqs = PRF[:, 1024 * pu:1024 * (pu + 1)]
                    sem = SDQLD if pu == 0 else SWA
                    S.wait("sync", SVEC)
                    dma("sync", sem, dqv, pk[jc, :, c0:c0 + 1024])
                    ldv[0] = dma("sync", sem, dqs, sx[jc, :, c0:c0 + 1024])

                def compute(pu):
                    dqv = DQV[:, 1024 * pu:1024 * (pu + 1)]
                    dqs = PRF[:, 1024 * pu:1024 * (pu + 1)]
                    S.wait("vector", SDQLD if pu == 0 else SWA, ldv[0])
                    vb_prev = [0, 0]
                    for t4 in range(4):
                        sub, hi = t4 % 2, t4 // 2
                        s0 = 512 * sub
                        st = t4 % 2
                        b0 = 2048 * st
                        S.wait("vector", SSCA, vb_prev[0])
                        if hi == 0:
                            S.emit("vector", lambda e, s0=s0, dqv=dqv:
                                   e.tensor_scalar(
                                out=DQNI[:, :], in0=dqv[:, s0:s0 + 512],
                                scalar1=15, scalar2=None,
                                op0=ALU.bitwise_and))
                        else:
                            S.emit("vector", lambda e, s0=s0, dqv=dqv:
                                   e.tensor_scalar(
                                out=DQNI[:, :], in0=dqv[:, s0:s0 + 512],
                                scalar1=4, scalar2=None,
                                op0=ALU.logical_shift_right))
                        S.emit("vector", lambda e: e.tensor_scalar(
                            out=DQUI[:, :], in0=DQNI[:, :],
                            scalar1=7, scalar2=None, op0=ALU.bitwise_and))
                        v_a = SVEC.n
                        S.wait("scalar", SVEC, v_a)
                        S.wait("scalar", SDQO if t4 % 2 == 0 else SGD)
                        S.emit("scalar", lambda e, b0=b0: e.activation(
                            out=DQF[:, b0:b0 + 512], in_=DQNI[:, :],
                            func=AF.Sign, bias=CONST[:, 1:2], scale=-1.0))
                        S.emit("scalar", lambda e, b0=b0: e.activation(
                            out=DQF[:, b0 + 512:b0 + 1024], in_=DQUI[:, :],
                            func=AF.Identity))
                        S.emit("scalar", lambda e, b0=b0: e.activation(
                            out=DQF[:, b0 + 1024:b0 + 1536], in_=DQUI[:, :],
                            func=AF.Relu, bias=CONST[:, 2:3], scale=1.0))
                        S.emit("scalar", lambda e, b0=b0: e.activation(
                            out=DQF[:, b0 + 1536:b0 + 2048], in_=DQUI[:, :],
                            func=AF.Relu, bias=CONST[:, 3:4], scale=2.0))
                        vb_prev[0] = vb_prev[1]
                        vb_prev[1] = SSCA.n
                        vb = SSCA.n
                        S.wait("vector", SSCA, vb)
                        S.emit("vector", lambda e, b0=b0: e.tensor_tensor(
                            out=DQF[:, b0 + 1024:b0 + 1536],
                            in0=DQF[:, b0 + 512:b0 + 1024],
                            in1=DQF[:, b0 + 1024:b0 + 1536], op=ALU.add))
                        S.emit("vector", lambda e, b0=b0: e.tensor_tensor(
                            out=DQF[:, b0 + 1024:b0 + 1536],
                            in0=DQF[:, b0 + 1024:b0 + 1536],
                            in1=DQF[:, b0 + 1536:b0 + 2048], op=ALU.add))
                        S.emit("vector", lambda e, b0=b0:
                               e.scalar_tensor_tensor(
                            out=DQF[:, b0 + 1024:b0 + 1536],
                            in0=DQF[:, b0 + 1024:b0 + 1536], scalar=0.5,
                            in1=DQF[:, b0:b0 + 512],
                            op0=ALU.mult, op1=ALU.mult))
                        dsem = SDQO if t4 % 2 == 0 else SGD
                        S.emit("vector", lambda e, b0=b0,
                               dqs=dqs, s0=s0: e.tensor_tensor(
                            out=DQF[:, b0 + 1536:b0 + 2048],
                            in0=DQF[:, b0 + 1024:b0 + 1536],
                            in1=dqs[:, s0:s0 + 512], op=ALU.mult))
                        vv2 = SVEC.n
                        S.wait("sync", SVEC, vv2)
                        dma("sync", dsem,
                            dst_r[jc, hi, :, c0 + s0:c0 + s0 + 512],
                            DQF[:, b0 + 1536:b0 + 2048])
                    if is_last:
                        deq_done[m] = (SDQO.n, SGD.n)
                return (load, compute)

            for m in ("g", "u", "d"):
                for jc in range(8):
                    for ch in range(2):
                        deq_units.append(_deq_unit(
                            m, jc, ch, jc == 7 and ch == 1))

        class UnitDriver:
            def __init__(self, units):
                self.units = units
                self.li = 0
                self.ci = 0

            def step(self, k=1):
                for _ in range(k):
                    if self.li < len(self.units):
                        self.units[self.li][0](self.li % 2)
                        self.li += 1
                    if self.ci < self.li - 1 or (self.li == len(self.units)
                                                 and self.ci < self.li):
                        self.units[self.ci][1](self.ci % 2)
                        self.ci += 1

            def flush(self):
                while self.ci < len(self.units):
                    self.step()

        tern_drv = UnitDriver(tern_units if stop_after >= -2 else [])
        deq_drv = UnitDriver(deq_units if stop_after >= -2 else [])
        if stop_after < 1:
            tern_drv.flush()
        if stop_after < 4:
            deq_drv.flush()

        def qkv_phase(l, hook=None):
            WBSQ = [BFA[:, 1024 * i:1024 * (i + 1)] for i in range(2)]
            WSTQ = [SCR[:, 1024 * i:1024 * (i + 1)] for i in range(2)]
            KVST = [BFA[:, 7168 + 1024 * i:7168 + 1024 * (i + 1)] for i in range(2)]
            kv_done = {}
            S.wait("sync", STEN)
            S.wait("vector", SOUT)
            S.wait("scalar", STSTG)
            S.wait("scalar", SWCST)
            S.wait("scalar", SDQO)
            S.wait("scalar", SGD)
            S.wait("tensor", SVEC)
            S.wait("tensor", SSCA)
            wst_used = [SVEC.n] * 2
            wbs_used = [STEN.n] * 2
            PSR = [PS0, PS1, PS2]
            ring_evac = [SSCA.n] * 3
            rk = [0]
            for mat in ("wk", "wv", "wq"):
                src = wT[(l, mat)].rearrange("(dc p) o -> p dc o", p=128)
                for og in range(8):
                    k0, k1 = rk[0] % 3, (rk[0] + 1) % 3
                    rk[0] += 2
                    PSA, PSB = PSR[k0], PSR[k1]
                    for qt in range(4):
                        slot = qt % 2
                        S.wait("sync", SVEC, wst_used[slot])
                        v = dma("sync", SWX[slot], WSTQ[slot].rearrange(
                            "p (dc o) -> p dc o", o=256),
                            src[:, 4 * qt:4 * qt + 4, 256 * og:256 * (og + 1)])
                        S.wait("vector", SWX[slot], v)
                        S.wait("vector", STEN, wbs_used[slot])
                        S.emit("vector", lambda e, slot=slot: e.tensor_copy(
                            out=WBSQ[slot], in_=WSTQ[slot]))
                        wst_used[slot] = SVEC.n
                        v_cast = SVEC.n
                        S.wait("tensor", SVEC, v_cast)
                        if qt == 0:
                            S.wait("tensor", SSCA, ring_evac[k0])
                            S.wait("tensor", SSCA, ring_evac[k1])
                        for dcc in range(4):
                            for oc in range(2):
                                PSX = PSA if oc == 0 else PSB
                                for th in range(2):
                                    dc = 4 * qt + dcc
                                    S.emit("tensor", lambda e, slot=slot,
                                           dcc=dcc, oc=oc, th=th, dc=dc,
                                           PSX=PSX: e.matmul(
                                        PSX[:, 512 * th:512 * (th + 1)],
                                        WBSQ[slot][:, 256 * dcc + 128 * oc:
                                                   256 * dcc + 128 * (oc + 1)],
                                        XZ[:, 1024 * dc + 512 * th:
                                           1024 * dc + 512 * (th + 1)],
                                        start=(dc == 0), stop=(dc == 15)))
                        wbs_used[slot] = S.inc("tensor", STEN)
                    v_mm = STEN.n
                    S.wait("scalar", STEN, v_mm)
                    for oc in range(2):
                        PSX = PSA if oc == 0 else PSB
                        occ = og * 2 + oc
                        if mat == "wq":
                            bias_ap = BQS[:, occ:occ + 1] if l == 0 else ZBI[:, :]
                            S.emit("scalar", lambda e, PSX=PSX, occ=occ,
                                   bias_ap=bias_ap: e.activation(
                                out=QA[:, 1024 * occ:1024 * (occ + 1)],
                                in_=PSX[:, :], func=AF.Identity,
                                bias=bias_ap, scale=SCALE))
                        else:
                            j = 1 if mat == "wk" else 2
                            bias_ap = BIA[:, 16 * j + occ:16 * j + occ + 1] \
                                if l == 0 else ZBI[:, :]
                            kslot = occ % 2
                            S.wait("scalar", SKV[kslot])
                            S.emit("scalar", lambda e, PSX=PSX, kslot=kslot,
                                   bias_ap=bias_ap: e.activation(
                                out=KVST[kslot], in_=PSX[:, :],
                                func=AF.Identity, bias=bias_ap, scale=1.0))
                            vs = SSCA.n
                            S.wait("sync", SSCA, vs)
                            dstm = kv_c[(l, "k" if mat == "wk" else "v")]
                            S._raw_emit("sync", lambda e, dstm=dstm, occ=occ,
                                        kslot=kslot: e.dma_start(
                                out=dstm[128 * occ:128 * (occ + 1), :],
                                in_=KVST[kslot]))
                            S.inc("sync", SKV[kslot], 16)
                    ring_evac[k0] = SSCA.n
                    ring_evac[k1] = SSCA.n
                    if hook is not None:
                        hook()
                if mat in ("wk", "wv"):
                    kv_done[mat] = (SKV[0].n, SKV[1].n)
            return kv_done

        def kv_allgather(l, kv_done):
            out = {}
            for m, key in (("k", "wk"), ("v", "wv")):
                S.wait("gpsimd", SKV[0], kv_done[key][0])
                S.wait("gpsimd", SKV[1], kv_done[key][1])
                S.emit("gpsimd", lambda e, l=l, m=m: e.collective_compute(
                    "AllGather", ALU.bypass, ins=[kv_c[(l, m)][:, :]],
                    outs=[kv_ag[(l, m)][:, :, :]], replica_groups=PAIRS), auto=False)
                out[m] = S.inc("gpsimd", SCC)
            return out

        def attn_phase(l, ag_done):
            KH = [BFA[:, 2048 * i:2048 * (i + 1)] for i in range(2)]
            VST = BFA[:, 4096:6144]
            VT0 = BFA[:, 6144:8192]
            VT = [VT0, VT0]
            PRB = [BFA[:, 8192 + 2048 * i:8192 + 2048 * (i + 1)]
                   for i in range(2)]
            SCB = [SCR, PRF]
            MSKF = [BFA[:, 12288 + 256 * s:12288 + 256 * s + 128]
                    for s in range(NQS)]
            MSKL = [BFA[:, 12416 + 256 * s:12416 + 256 * s + 128]
                    for s in range(NQS)]

            S.wait("vector", SPH0, v_ld0)
            S.wait("vector", SGP)
            S.wait("vector", SKV[0])
            S.wait("vector", SKV[1])
            for s in range(NQS):
                S.emit("vector", lambda e, s=s: e.tensor_scalar(
                    out=MSKF[s][:, :], in0=KIOT[:, 128 * s:128 * (s + 1)],
                    scalar1=QIDX[:, s:s + 1], scalar2=None, op0=ALU.is_le))
                S.emit("vector", lambda e, s=s: e.tensor_scalar(
                    out=MSKL[s][:, :],
                    in0=KIOT[:, 128 * (s + 8):128 * (s + 9)],
                    scalar1=QIDX[:, s:s + 1], scalar2=None, op0=ALU.is_le))
                S.emit("vector", lambda e, s=s: e.tensor_scalar(
                    out=CM[:, s:s + 1], in0=QIDX[:, s:s + 1],
                    scalar1=float(128 * (s + 8) - 1), scalar2=None,
                    op0=ALU.is_ge))
            v_msk = S.inc("vector", SVEC)

            kag, vag = kv_ag[(l, "k")], kv_ag[(l, "v")]
            kh_ready = [0, 0]
            kh_used = [STEN.n, STEN.n]
            vst_ready = [0]
            vst_used = [STEN.n]

            def load_k(h):
                slot = h % 2
                S.wait("scalar", STEN, kh_used[slot])
                S.wait("scalar", SCC, ag_done["k"])
                for r in range(2):
                    dma("scalar", SWX[5 + slot],
                        KH[slot][:, 1024 * r:1024 * (r + 1)],
                        kag[r, 128 * h:128 * (h + 1), :])
                kh_ready[slot] = SWX[5 + slot].n

            def load_v(h):
                S.wait("scalar", STEN, vst_used[0])
                S.wait("scalar", SCC, ag_done["v"])
                for r in range(2):
                    dma("scalar", SWX[7], VST[:, 1024 * r:1024 * (r + 1)],
                        vag[r, 128 * h:128 * (h + 1), :])
                vst_ready[0] = SWX[7].n

            S.wait("scalar", STSTG)
            S.wait("scalar", SWCST)
            S.wait("scalar", SDQO)
            S.wait("scalar", SGD)
            load_k(0)
            load_v(0)
            S.wait("tensor", SSCA)
            for h in range(H):
                slot = h % 2
                S.wait("tensor", SWX[7], vst_ready[0])
                S.wait("tensor", SVEC)
                for g in range(4):
                    for j in range(4 * g, 4 * g + 4):
                        S.emit("tensor", lambda e, j=j: e.transpose(
                            PS3b[:, 128 * (j % 4):128 * (j % 4 + 1)],
                            VST[:, 128 * j:128 * (j + 1)], IDE[:, :]))
                    tv = S.inc("tensor", STEN)
                    S.wait("vector", STEN, tv)
                    S.emit("vector", lambda e, g=g, slot=slot: e.tensor_copy(
                        out=VT[slot][:, 512 * g:512 * (g + 1)],
                        in_=PS3b[:, :]))
                    if g < 3:
                        S.wait("tensor", SVEC)
                vst_used[0] = STEN.n
                if h + 1 < H:
                    load_k(h + 1)
                    load_v(h + 1)
                S.wait("tensor", SWX[5 + slot], kh_ready[slot])

                evac_done = [0] * (NQS + 1)
                nrm_done = [0] * NQS
                exp_done = [0] * NQS
                prb_free = [STEN.n, STEN.n]
                scb_free = [SSCA.n, SSCA.n]

                def emit_scores(s):
                    E = 9 + s
                    cols = 128 * E
                    qtile = QA[:, 1024 * h + 128 * s:1024 * h + 128 * (s + 1)]
                    if s >= 1:
                        S.wait("tensor", SVEC, evac_done[s - 1])
                    for half, PSX in ((0, PS0), (1, PS1)):
                        base = 1024 * half
                        width = min(cols - base, 1024)
                        off = 0
                        while off < width:
                            w = min(512, width - off)
                            S.emit("tensor", lambda e, PSX=PSX, base=base,
                                   off=off, w=w, qtile=qtile, slot=slot:
                                   e.matmul(
                                PSX[:, off:off + w], qtile,
                                KH[slot][:, base + off:base + off + w],
                                start=True, stop=True))
                            off += 512
                    return S.inc("tensor", STEN)

                def emit_softmax(s, v_sc):
                    E = 9 + s
                    cols = 128 * E
                    sb = SCB[s % 2]
                    pb = PRB[s % 2]
                    S.wait("vector", STEN, v_sc)
                    S.wait("vector", SSCA, scb_free[s % 2])
                    S.emit("vector", lambda e, sb=sb: e.tensor_scalar(
                        out=sb[:, 0:1024], in0=PS0[:, :], scalar1=1.0,
                        scalar2=None, op0=ALU.mult, op1=ALU.max,
                        accum_out=MXV[:, :]))
                    S.emit("vector", lambda e, sb=sb, cols=cols: e.tensor_scalar(
                        out=sb[:, 1024:cols], in0=PS1[:, 0:cols - 1024],
                        scalar1=1.0, scalar2=MXV[:, :], op0=ALU.mult,
                        op1=ALU.max, accum_out=MXV[:, :]))
                    evac_done[s] = SVEC.n
                    v_ev = SVEC.n
                    S.wait("scalar", SVEC, v_ev)
                    S.wait("scalar", STEN, prb_free[s % 2])
                    S.emit("scalar", lambda e: e.activation(
                        out=NMX[:, :], in_=MXV[:, :], func=AF.Identity,
                        scale=-1.0))
                    if s > 0:
                        S.emit("scalar", lambda e, sb=sb, pb=pb, s=s:
                               e.activation(
                            out=pb[:, 0:128 * s], in_=sb[:, 0:128 * s],
                            func=AF.Exp, bias=NMX[:, :], scale=1.0,
                            accum_out=LS0[:, :]))
                    S.emit("scalar", lambda e, sb=sb, s=s, cols=cols:
                           e.activation(
                        out=sb[:, 128 * s:cols], in_=sb[:, 128 * s:cols],
                        func=AF.Exp, bias=NMX[:, :], scale=1.0))
                    exp_done[s] = SSCA.n
                    scb_free[s % 2] = SSCA.n
                    S.wait("vector", SSCA, exp_done[s])
                    S.wait("vector", STEN, prb_free[s % 2])
                    S.wait("vector", SVEC, v_msk)
                    if s == 0:
                        S.emit("vector", lambda e: e.memset(LS0[:, :], 0.0))
                    S.emit("vector", lambda e, sb=sb, pb=pb, s=s:
                           e.scalar_tensor_tensor(
                        out=pb[:, 128 * s:128 * (s + 1)],
                        in0=sb[:, 128 * s:128 * (s + 1)],
                        scalar=1.0, in1=MSKF[s][:, :], op0=ALU.mult,
                        op1=ALU.mult, accum_out=LS1[:, :]))
                    S.emit("vector", lambda e, sb=sb, pb=pb, s=s: e.tensor_scalar(
                        out=pb[:, 128 * (s + 1):128 * (s + 8)],
                        in0=sb[:, 128 * (s + 1):128 * (s + 8)],
                        scalar1=CM[:, s:s + 1], scalar2=None,
                        op0=ALU.mult, op1=ALU.add, accum_out=LSA[:, :]))
                    S.emit("vector", lambda e, sb=sb, pb=pb, s=s, cols=cols:
                           e.scalar_tensor_tensor(
                        out=pb[:, 128 * (s + 8):cols],
                        in0=sb[:, 128 * (s + 8):cols],
                        scalar=1.0, in1=MSKL[s][:, :], op0=ALU.mult,
                        op1=ALU.mult, accum_out=LSB[:, :]))
                    S.emit("vector", lambda e: e.tensor_tensor(
                        out=LS0[:, :], in0=LS0[:, :], in1=LS1[:, :],
                        op=ALU.add))
                    S.emit("vector", lambda e: e.tensor_tensor(
                        out=LSA[:, :], in0=LSA[:, :], in1=LSB[:, :],
                        op=ALU.add))
                    S.emit("vector", lambda e: e.tensor_tensor(
                        out=LS0[:, :], in0=LS0[:, :], in1=LSA[:, :],
                        op=ALU.add))
                    S.emit("vector", lambda e: e.reciprocal(
                        out=RLS[:, :], in_=LS0[:, :]))
                    S.emit("vector", lambda e, pb=pb, cols=cols: e.tensor_scalar(
                        out=pb[:, 0:cols], in0=pb[:, 0:cols],
                        scalar1=RLS[:, :], scalar2=None, op0=ALU.mult))
                    nrm_done[s] = SVEC.n

                def emit_pv(s):
                    E = 9 + s
                    pb = PRB[s % 2]
                    S.wait("tensor", SVEC, nrm_done[s])
                    ngrp = (E + 3) // 4
                    for g in range(ngrp):
                        j0, j1 = 4 * g, min(4 * g + 4, E)
                        if g > 0:
                            S.wait("tensor", SVEC)
                        for j in range(j0, j1):
                            S.emit("tensor", lambda e, j=j, pb=pb: e.transpose(
                                PS3b[:, 128 * (j % 4):128 * (j % 4 + 1)],
                                pb[:, 128 * j:128 * (j + 1)], IDE[:, :]))
                        tv = S.inc("tensor", STEN)
                        S.wait("vector", STEN, tv)
                        S.emit("vector", lambda e, j0=j0, j1=j1: e.tensor_copy(
                            out=PTB[:, 0:128 * (j1 - j0)],
                            in_=PS3b[:, 0:128 * (j1 - j0)]))
                        vv = SVEC.n
                        S.wait("tensor", SVEC, vv)
                        for j in range(j0, j1):
                            S.emit("tensor", lambda e, j=j, j0=j0, E=E,
                                   slot=slot: e.matmul(
                                PS3f[:, 256:384],
                                VT[slot][:, 128 * j:128 * (j + 1)],
                                PTB[:, 128 * (j - j0):128 * (j - j0 + 1)],
                                start=(j == 0), stop=(j == E - 1)))
                        S.inc("tensor", STEN)
                    prb_free[s % 2] = STEN.n
                    v_pv = STEN.n
                    S.wait("vector", STEN, v_pv)
                    S.emit("vector", lambda e, s=s, h=h: e.tensor_copy(
                        out=XZ[:, 1024 * h + 128 * s:1024 * h + 128 * (s + 1)],
                        in_=PS3f[:, 256:384]))

                for s in range(NQS + 1):
                    if s < NQS:
                        v_sc = emit_scores(s)
                        if s == NQS - 1:
                            kh_used[slot] = v_sc
                        emit_softmax(s, v_sc)
                    if s >= 1:
                        emit_pv(s - 1)

        def oproj_phase(l):
            src = wT[(l, "wo")]
            WOST = [SCR[:, 1024 * i:1024 * (i + 1)] for i in range(2)] + \
                   [PRF[:, 1024 * i:1024 * (i + 1)] for i in range(2)]
            wost_used = [SVEC.n] * 4
            opk = [0]
            opevac = [SVEC.n] * 3
            for dh in range(2):
                S.wait("sync", STEN)
                S.wait("sync", SSCA)
                S.wait("sync", SOUT)
                for fc in range(16):
                    sl = fc % 4
                    S.wait("sync", SVEC, wost_used[sl])
                    v = dma("sync", SWX[sl], WOST[sl],
                            src[128 * fc:128 * (fc + 1),
                                1024 * dh:1024 * (dh + 1)])
                    S.wait("vector", SWX[sl], v)
                    S.wait("vector", STEN)
                    S.emit("vector", lambda e, sl=sl, fc=fc: e.tensor_copy(
                        out=QA[:, 1024 * fc:1024 * (fc + 1)], in_=WOST[sl]))
                    wost_used[sl] = SVEC.n
                v_w = SVEC.n
                S.wait("tensor", SVEC, v_w)
                S.wait("tensor", SSCA)
                for s in range(NQS):
                    k = opk[0] % 3
                    opk[0] += 1
                    PSX = [PS0, PS1, PS2][k]
                    S.wait("tensor", SVEC, opevac[k])
                    for fc in range(16):
                        for th in range(2):
                            S.emit("tensor", lambda e, PSX=PSX, fc=fc, th=th,
                                   s=s: e.matmul(
                                PSX[:, 512 * th:512 * (th + 1)],
                                XZ[:, 1024 * fc + 128 * s:
                                   1024 * fc + 128 * (s + 1)],
                                QA[:, 1024 * fc + 512 * th:
                                   1024 * fc + 512 * (th + 1)],
                                start=(fc == 0), stop=(fc == 15)))
                    v_mm = S.inc("tensor", STEN)
                    S.wait("vector", STEN, v_mm)
                    S.emit("vector", lambda e, PSX=PSX, s=s, dh=dh:
                           e.scalar_tensor_tensor(
                        out=R[:, 2048 * s + 1024 * dh:2048 * s + 1024 * (dh + 1)],
                        in0=PSX[:, :], scalar=1.0,
                        in1=R[:, 2048 * s + 1024 * dh:2048 * s + 1024 * (dh + 1)],
                        op0=ALU.mult, op1=ALU.add))
                    opevac[k] = S.inc("vector", SVEC) or SVEC.n
            if l == 0:
                S.wait("tensor", SVEC)
                for i in range(16):
                    S.wait("tensor", SVEC)
                    S.emit("tensor", lambda e, i=i: e.transpose(
                        PS3f[0:1, 0:128], BIA[:, 48 + i:49 + i], IDEF[:, :]))
                    tv = S.inc("tensor", STEN)
                    S.wait("vector", STEN, tv)
                    S.emit("vector", lambda e, i=i: e.tensor_copy(
                        out=SCR[0:1, 128 * i:128 * (i + 1)],
                        in_=PS3f[0:1, 0:128]))
                    S.inc("vector", SVEC)
                v_row = SVEC.n
                S.wait("tensor", SVEC, v_row)
                for half, PSX in ((0, PS2), (1, PS1)):
                    for th in range(2):
                        S.emit("tensor", lambda e, PSX=PSX, half=half, th=th:
                               e.matmul(
                            PSX[:, 512 * th:512 * (th + 1)],
                            ONER[:, :],
                            SCR[0:1, 1024 * half + 512 * th:
                                1024 * half + 512 * (th + 1)],
                            start=True, stop=True))
                v_bo = S.inc("tensor", STEN)
                S.wait("vector", STEN, v_bo)
                for s in range(NQS):
                    for half, PSX in ((0, PS2), (1, PS1)):
                        S.emit("vector", lambda e, s=s, half=half, PSX=PSX:
                               e.tensor_tensor(
                            out=R[:, 2048 * s + 1024 * half:
                                  2048 * s + 1024 * (half + 1)],
                            in0=R[:, 2048 * s + 1024 * half:
                                  2048 * s + 1024 * (half + 1)],
                            in1=PSX[:, :], op=ALU.add))
                        S.inc("vector", SVEC)

        def ln_phase(rebuild_xt):
            S.wait("vector", SOUT)
            for s in range(NQS):
                for c in range(4):
                    S.emit("vector", lambda e, s=s, c=c: e.bn_stats(
                        out=BNS[:, 6 * c:6 * (c + 1)],
                        in_=R[:, 2048 * s + 512 * c:2048 * s + 512 * (c + 1)]))
                    S.inc("vector", SVEC)
                S.wait("vector", SVEC)
                S.emit("vector", lambda e: e.bn_aggr(out=MV[:, :], in_=BNS[:, :]))
                v_mv = S.inc("vector", SVEC)
                S.wait("scalar", SVEC, v_mv)
                S.emit("scalar", lambda e: e.activation(
                    out=RSTD[:, :], in_=MV[:, 1:2], func=AF.Ln,
                    bias=CONST[:, 0:1], scale=1.0))
                S.emit("scalar", lambda e: e.activation(
                    out=RSTD[:, :], in_=RSTD[:, :], func=AF.Exp, scale=-0.5))
                vs = S.inc("scalar", SSCA)
                S.wait("vector", SSCA, vs)
                S.emit("vector", lambda e, s=s: e.tensor_scalar(
                    out=R[:, 2048 * s:2048 * (s + 1)],
                    in0=R[:, 2048 * s:2048 * (s + 1)],
                    scalar1=MV[:, 0:1], scalar2=RSTD[:, :],
                    op0=ALU.subtract, op1=ALU.mult))
                S.inc("vector", SVEC)
            if rebuild_xt:
                S.wait("tensor", SVEC)
                return xt_rebuild()
            return SVEC.n

        def ffn_phase(l, deq_ag_done, hook=None):
            WG0 = [BFA[:, 2048 * i:2048 * (i + 1)] for i in range(2)]
            WG1 = [BFA[:, 4096 + 2048 * i:4096 + 2048 * (i + 1)] for i in range(2)]
            SST2B = SILB[:, :]
            if l == 0:
                wg_src, wu_src, wd_src = tern_ag["g"], tern_ag["u"], tern_ag["d"]
                etot = 8192
                ag_gu = max(tern_ag_done["g"], tern_ag_done["u"])
                ag_d = tern_ag_done["d"]
            else:
                wg_src, wu_src, wd_src = deq_ag["g"], deq_ag["u"], deq_ag["d"]
                etot = 16384
                ag_gu = ag_d = deq_ag_done
            n_ec = etot // 2048
            rw = 1024 if l == 0 else 2048
            gsrc3 = wg_src.rearrange("r (dc p) e -> r p dc e", p=128)
            usrc3 = wu_src.rearrange("r (dc p) e -> r p dc e", p=128)
            wdr = wd_src.rearrange("r (ec p) d -> r ec p d", p=128)
            wg_used = [STEN.n, STEN.n]
            down_done = [STEN.n]

            S.wait("tensor", SVEC)
            S.wait("tensor", SSCA)
            S.wait("vector", SOUT)
            S.wait("scalar", STSTG)
            S.wait("scalar", SDQO)
            S.wait("scalar", SWCST)
            S.wait("scalar", SGD)

            def wdnr_ap(es):
                if es < 6:
                    return BFA[:, 1024 * es:1024 * (es + 1)]
                return BFA[:, 8192 + 1024 * (es - 6):8192 + 1024 * (es - 5)]

            PSR = [PS0, PS1, PS2]
            ring_evac = [SVEC.n] * 3
            silb_evac = [SVEC.n] * 2
            rk = [0]
            for ec in range(n_ec):
                S.wait("scalar", SCC, ag_gu)
                S.wait("scalar", SCC, ag_d)
                S.wait("scalar", STEN, down_done[0])
                for es in range(6, 16):
                    e0 = 2048 * ec + 128 * es
                    r, erow = e0 // rw, (e0 % rw) // 128
                    dma("scalar", SWX[4],
                        wdnr_ap(es), wdr[r, erow, :, 0:1024])
                for es in range(16):
                    e0 = 2048 * ec + 128 * es
                    r, c0 = e0 // rw, e0 % rw
                    slot = es % 2
                    S.wait("scalar", STEN, max(wg_used[slot], down_done[0]))
                    dma("scalar", SWX[slot],
                        WG0[slot][:, :].rearrange("p (dc e) -> p dc e", e=128),
                        gsrc3[r, :, :, c0:c0 + 128])
                    dma("scalar", SWX[2 + slot],
                        WG1[slot][:, :].rearrange("p (dc e) -> p dc e", e=128),
                        usrc3[r, :, :, c0:c0 + 128])
                    S.wait("tensor", SWX[slot])
                    S.wait("tensor", SWX[2 + slot])
                    if es == 0:
                        S.wait("tensor", SVEC)
                    for th in range(2):
                        k = rk[0] % 3
                        rk[0] += 1
                        PSX = PSR[k]
                        S.wait("tensor", SVEC, ring_evac[k])
                        for dc in range(16):
                            S.emit("tensor", lambda e, slot=slot, dc=dc, th=th,
                                   PSX=PSX: e.matmul(
                                PSX[:, 0:512],
                                WG0[slot][:, 128 * dc:128 * (dc + 1)],
                                XZ[:, 1024 * dc + 512 * th:
                                   1024 * dc + 512 * (th + 1)],
                                start=(dc == 0), stop=(dc == 15)))
                        for dc in range(16):
                            S.emit("tensor", lambda e, slot=slot, dc=dc, th=th,
                                   PSX=PSX: e.matmul(
                                PSX[:, 512:1024],
                                WG1[slot][:, 128 * dc:128 * (dc + 1)],
                                XZ[:, 1024 * dc + 512 * th:
                                   1024 * dc + 512 * (th + 1)],
                                start=(dc == 0), stop=(dc == 15)))
                        v_mm = S.inc("tensor", STEN)
                        if th == 1:
                            wg_used[slot] = v_mm
                        S.wait("scalar", STEN, v_mm)
                        S.wait("scalar", SVEC, silb_evac[th])
                        if l == 0:
                            S.emit("scalar", lambda e, PSX=PSX, th=th:
                                   e.activation(
                                out=SST2B[:, 512 * th:512 * (th + 1)],
                                in_=PSX[:, 512:1024], func=AF.Silu))
                            gsrc_ps = 0
                        else:
                            S.emit("scalar", lambda e, PSX=PSX, th=th:
                                   e.activation(
                                out=SST2B[:, 512 * th:512 * (th + 1)],
                                in_=PSX[:, 0:512], func=AF.Gelu_apprx_tanh))
                            gsrc_ps = 512
                        vs = SSCA.n
                        S.wait("vector", SSCA, vs)
                        S.emit("vector", lambda e, es=es, th=th, PSX=PSX,
                               gsrc_ps=gsrc_ps: e.scalar_tensor_tensor(
                            out=QA[:, 1024 * es + 512 * th:
                                   1024 * es + 512 * (th + 1)],
                            in0=PSX[:, gsrc_ps:gsrc_ps + 512],
                            scalar=1.0, in1=SST2B[:, 512 * th:512 * (th + 1)],
                            op0=ALU.mult, op1=ALU.mult))
                        ring_evac[k] = SVEC.n
                        silb_evac[th] = SVEC.n
                    if hook is not None:
                        hook()
                v_hid = SVEC.n

                for dh in range(2):
                    S.wait("scalar", STEN)
                    S.wait("scalar", SVEC)
                    es_list = range(6) if dh == 0 else range(16)
                    for es in es_list:
                        e0 = 2048 * ec + 128 * es
                        r, erow = e0 // rw, (e0 % rw) // 128
                        dma("scalar", SWX[4],
                            wdnr_ap(es),
                            wdr[r, erow, :, 1024 * dh:1024 * (dh + 1)])
                    v_wd = SWX[4].n
                    S.wait("tensor", SWX[4], v_wd)
                    S.wait("tensor", SVEC, v_hid)
                    for s in range(NQS):
                        k = rk[0] % 3
                        rk[0] += 1
                        PSX = PSR[k]
                        S.wait("tensor", SVEC, ring_evac[k])
                        for es in range(16):
                            for th in range(2):
                                S.emit("tensor", lambda e, PSX=PSX, es=es,
                                       th=th, s=s, ap=wdnr_ap(es): e.matmul(
                                    PSX[:, 512 * th:512 * (th + 1)],
                                    QA[:, 1024 * es + 128 * s:
                                       1024 * es + 128 * (s + 1)],
                                    ap[:, 512 * th:512 * (th + 1)],
                                    start=(es == 0), stop=(es == 15)))
                        v_mm = S.inc("tensor", STEN)
                        down_done[0] = v_mm
                        S.wait("vector", STEN, v_mm)
                        S.emit("vector", lambda e, PSX=PSX, s=s, dh=dh:
                               e.scalar_tensor_tensor(
                            out=R[:, 2048 * s + 1024 * dh:
                                  2048 * s + 1024 * (dh + 1)],
                            in0=PSX[:, :], scalar=1.0,
                            in1=R[:, 2048 * s + 1024 * dh:
                                  2048 * s + 1024 * (dh + 1)],
                            op0=ALU.mult, op1=ALU.add))
                        ring_evac[k] = S.inc("vector", SVEC) or SVEC.n

        S.wait("tensor", SVEC, v_xt)
        deq_ag_done = 0
        tern_ag_done = {"u": 0, "g": 0, "d": 0}
        v_fin = SVEC.n
        if stop_after >= 1:
            kvd0 = qkv_phase(0, hook=lambda: tern_drv.step(2))
            tern_drv.flush()
        if stop_after >= 2:
            agd0 = kv_allgather(0, kvd0)
            for m in TMAT:
                S.wait("gpsimd", STSTG, tern_done[m][0])
                S.wait("gpsimd", SWCST, tern_done[m][1])
                S.emit("gpsimd", lambda e, m=m: e.collective_compute(
                    "AllGather", ALU.bypass, ins=[tern_c[m][:, :]],
                    outs=[tern_ag[m][:, :, :]], replica_groups=ALL8),
                       auto=False)
                tern_ag_done[m] = S.inc("gpsimd", SCC)
            attn_phase(0, agd0)
        if debug and stop_after >= 2:
            S.wait("sync", SVEC)
            dzr = dbg_d["d_z0"].rearrange("(c p) t -> c p t", p=128)
            dqr = dbg_d["d_q0"].rearrange("(c p) t -> c p t", p=128)
            for fc in range(16):
                dma("sync", SOUT, dzr[fc, :, :], XZ[:, 1024 * fc:1024 * (fc + 1)])
                dma("sync", SOUT, dqr[fc, :, :], QA[:, 1024 * fc:1024 * (fc + 1)])
        if stop_after >= 3:
            oproj_phase(0)
            if debug:
                S.wait("sync", SVEC)
                rr = dbg_d["d_R1"].rearrange("(s p) d -> s p d", p=128)
                for s in range(NQS):
                    dma("sync", SOUT, rr[s, :, :], R[:, 2048 * s:2048 * (s + 1)])
            ln_phase(rebuild_xt=True)
        if stop_after >= 4:
            ffn_phase(0, deq_ag_done, hook=lambda: deq_drv.step(2))
            deq_drv.flush()
            S.wait("gpsimd", SDQLD)
            S.wait("gpsimd", SWA)
            S.wait("gpsimd", SVEC)
            S.emit("gpsimd", lambda e: e.iota(
                out=KIOT[:, :], pattern=[[1, 2048]], base=0,
                channel_multiplier=0))
            if debug:
                S.wait("sync", SVEC)
                rr = dbg_d["d_R2"].rearrange("(s p) d -> s p d", p=128)
                for s in range(NQS):
                    dma("sync", SOUT, rr[s, :, :], R[:, 2048 * s:2048 * (s + 1)])
            ln_phase(rebuild_xt=True)
        if stop_after >= 5:
            kvd1 = qkv_phase(1)
            agd1 = kv_allgather(1, kvd1)
            for m in ("g", "u", "d"):
                S.wait("gpsimd", SDQO, deq_done[m][0])
                S.wait("gpsimd", SGD, deq_done[m][1])
                S.emit("gpsimd", lambda e, m=m: e.collective_compute(
                    "AllGather", ALU.bypass, ins=[deq_c[m][:, :]],
                    outs=[deq_ag[m][:, :, :]], replica_groups=ALL8), auto=False)
                S.inc("gpsimd", SCC)
            deq_ag_done = SCC.n
            attn_phase(1, agd1)
        if stop_after >= 6:
            oproj_phase(1)
            if debug:
                S.wait("sync", SVEC)
                rr = dbg_d["d_R3"].rearrange("(s p) d -> s p d", p=128)
                for s in range(NQS):
                    dma("sync", SOUT, rr[s, :, :], R[:, 2048 * s:2048 * (s + 1)])
            ln_phase(rebuild_xt=True)
        if stop_after >= 7:
            ffn_phase(1, deq_ag_done)
            v_fin = ln_phase(rebuild_xt=False)

        S.wait("sync", SVEC, v_fin)
        out_r = out_d.rearrange("(s p) d -> s p d", p=128)
        for s in range(NQS):
            dma("sync", SOUT, out_r[s, :, :], R[:, 2048 * s:2048 * (s + 1)])
        if debug and stop_after >= -2:
            S.wait("sync", SCC, tern_ag_done["u"])
            dma("sync", SOUT, dbg_d["d_ternu"][:, :], tern_ag["u"][0, :, :])
        if debug and stop_after >= 2:
            S.wait("sync", SCC, deq_ag_done)
            dma("sync", SOUT, dbg_d["d_deqg"][:, :], deq_ag["g"][0, :, :])
        S.wait("sync", SOUT, SOUT.n)

        @block.sync
        def _(eng):
            S.run("sync", eng)

        @block.scalar
        def _(eng):
            S.run("scalar", eng)

        @block.vector
        def _(eng):
            S.run("vector", eng)

        @block.tensor
        def _(eng):
            S.run("tensor", eng)

        @block.gpsimd
        def _(eng):
            S.run("gpsimd", eng)

    return nc


def _prep_inputs(inputs):
    f32 = np.float32
    x = np.ascontiguousarray(np.asarray(inputs["L0_attn_x"], dtype=f32)
                             .reshape(B * T, D))
    common = {}
    for l, pre in ((0, "L0_attn_"), (1, "L1_attn_")):
        if l == 0:
            wq = np.asarray(inputs[pre + "wq"], f32)
            wk = np.asarray(inputs[pre + "wk"], f32)
            wv = np.asarray(inputs[pre + "wv"], f32)
            wo = np.asarray(inputs[pre + "wo"], f32)
        else:
            wq = np.asarray(inputs[pre + "W_Q"], f32).reshape(D, D)
            wk = np.asarray(inputs[pre + "W_K"], f32).reshape(D, D)
            wv = np.asarray(inputs[pre + "W_V"], f32).reshape(D, D)
            wo = np.asarray(inputs[pre + "W_O"], f32)
        common[f"L{l}_wqT"] = np.ascontiguousarray(wq.T)
        common[f"L{l}_wkT"] = np.ascontiguousarray(wk.T)
        common[f"L{l}_wvT"] = np.ascontiguousarray(wv.T)
        common[f"L{l}_woT"] = np.ascontiguousarray(wo.T)
    for m in ("q", "k", "v", "o"):
        common[f"L0_b{m}"] = np.ascontiguousarray(
            np.asarray(inputs[f"L0_attn_b{m}"], f32).reshape(16, 128).T)
    common["ident"] = np.eye(128, dtype=ml_dtypes.bfloat16)
    common["identf"] = np.eye(128, dtype=np.float32)

    wupT = np.asarray(inputs["L0_ffn_W_up"], f32).T
    wgT = np.asarray(inputs["L0_ffn_W_gate"], f32).T
    wdT = np.asarray(inputs["L0_ffn_W_down"], f32).T

    pg = np.asarray(inputs["L1_ffn_gate_W_packed"], np.int32)
    pu = np.asarray(inputs["L1_ffn_up_W_packed"], np.int32)
    pd = np.asarray(inputs["L1_ffn_down_W_packed"], np.int32)
    sg = np.asarray(inputs["L1_ffn_gate_scales"], f32)
    su = np.asarray(inputs["L1_ffn_up_scales"], f32)
    sd = np.asarray(inputs["L1_ffn_down_scales"], f32)

    maps = []
    for c in range(NCORE):
        mp = dict(common)
        mp["x"] = np.ascontiguousarray(x[c * TOK:(c + 1) * TOK])
        qi = ((c % 2) * 1024 + np.arange(TOK, dtype=f32))
        mp["qidx"] = np.ascontiguousarray(qi.reshape(NQS, 128).T)
        mp["tu_sl"] = np.ascontiguousarray(wupT[:, c * 1024:(c + 1) * 1024])
        mp["tg_sl"] = np.ascontiguousarray(wgT[:, c * 1024:(c + 1) * 1024])
        mp["td_sl"] = np.ascontiguousarray(wdT[c * 1024:(c + 1) * 1024, :])
        mp["pkg_sl"] = np.ascontiguousarray(pg[c * 2048:(c + 1) * 2048, :].T)
        mp["pku_sl"] = np.ascontiguousarray(pu[c * 2048:(c + 1) * 2048, :].T)
        mp["pkd_sl"] = np.ascontiguousarray(pd.T[c * 1024:(c + 1) * 1024, :])
        mp["sxg_sl"] = np.ascontiguousarray(
            np.repeat(sg[c * 2048:(c + 1) * 2048, :].T, 16, axis=0))
        mp["sxu_sl"] = np.ascontiguousarray(
            np.repeat(su[c * 2048:(c + 1) * 2048, :].T, 16, axis=0))
        mp["sxd_sl"] = np.ascontiguousarray(
            np.repeat(sd.T, 16, axis=0)[c * 1024:(c + 1) * 1024, :])
        maps.append(mp)
    return maps


_cache = {}


def kernel(**inputs):
    maps = _prep_inputs(inputs)
    if "prog" not in _cache:
        _cache["prog"] = build_program(debug=False)
    res = run_bass_kernel_spmd(_cache["prog"], maps, core_ids=list(range(NCORE)))
    out = np.concatenate([r["out"] for r in res.results], axis=0)
    return out.reshape(B, T, D).astype(np.float32)
